# version 01
# baseline (speedup 1.0000x reference)
import os
import numpy as np
import ml_dtypes
from contextlib import ExitStack

import concourse.bass as bass
import concourse.mybir as mybir
from concourse.bass_utils import run_bass_kernel_spmd

F32 = mybir.dt.float32
BF16 = mybir.dt.bfloat16
AF = mybir.ActivationFunctionType
ALU = mybir.AluOpType
AXX = mybir.AxisListType.X

D = 2048
NCH = 16
NH = 16
DH = 128
NF = 8
NE = 16
NEX = 18
TOWN = 2048
HALF = 1024
SEQ = 4096
NSLOT = 4
EXT = [2, 4, 6, 8]
CHUNKS_EVEN = [0, 3, 4, 7]
CHUNKS_ODD = [1, 2, 5, 6]
CHUNK_OWNER = {g: (0, i) for i, g in enumerate(CHUNKS_EVEN)}
CHUNK_OWNER.update({g: (1, i) for i, g in enumerate(CHUNKS_ODD)})
GROUPS = [(2 * i, 2 * i + 1) for i in range(9)]

DEV_UPTO = "full"
SKIP = set(os.environ.get("KSKIP", "").split(",")) - {""}


def _host_layouts(inp):
    f32 = np.float32
    out = {}

    def fmw(W):
        return np.ascontiguousarray(
            np.asarray(W).reshape(NCH, 128, NCH, 128).transpose(2, 1, 0, 3).reshape(D, D)
        ).astype(f32, copy=False)

    out["wq"] = fmw(inp["L0_attn_Wq"])
    out["wk"] = fmw(inp["L0_attn_Wk"])
    out["wv"] = fmw(inp["L0_attn_Wv"])
    out["wo"] = fmw(inp["L0_attn_Wo"])

    g_all = np.concatenate([np.asarray(inp["L0_ffn_rg"]), np.asarray(inp["L0_ffn_sg"])], 0)
    w1_all = np.concatenate([np.asarray(inp["L0_ffn_rw1"]), np.asarray(inp["L0_ffn_sw1"])], 0)
    w2_all = np.concatenate([np.asarray(inp["L0_ffn_rw2"]), np.asarray(inp["L0_ffn_sw2"])], 0)

    def gw(g):
        return np.ascontiguousarray(
            g.reshape(NEX, NCH, 128, NF, 128).transpose(0, 3, 2, 1, 4).reshape(NEX * NF * 128, D)
        ).astype(f32, copy=False)

    out["g"] = gw(g_all)
    out["w1"] = gw(w1_all)
    out["w2"] = np.ascontiguousarray(
        w2_all.reshape(NEX, NF, 128, NCH, 128).transpose(3, 2, 0, 1, 4).reshape(D, NEX * NF * 128)
    ).astype(f32, copy=False)
    out["wa"] = np.ascontiguousarray(np.asarray(inp["L0_ffn_wa"])).astype(f32, copy=False)

    gb_all = np.concatenate([np.asarray(inp["L0_ffn_rgb"]), np.asarray(inp["L0_ffn_sgb"])], 0)
    b1_all = np.concatenate([np.asarray(inp["L0_ffn_rb1"]), np.asarray(inp["L0_ffn_sb1"])], 0)
    out["gb"] = np.ascontiguousarray(
        gb_all.reshape(NEX, NF, 128).transpose(2, 0, 1).reshape(128, NEX * NF)).astype(f32, copy=False)
    out["b1"] = np.ascontiguousarray(
        b1_all.reshape(NEX, NF, 128).transpose(2, 0, 1).reshape(128, NEX * NF)).astype(f32, copy=False)
    out["b2"] = np.ascontiguousarray(
        np.concatenate([np.asarray(inp["L0_ffn_rb2"]), np.asarray(inp["L0_ffn_sb2"])], 0)
    ).astype(f32, copy=False)
    for nm, key in [("bq", "L0_attn_bq"), ("bk", "L0_attn_bk"),
                    ("bv", "L0_attn_bv"), ("bo", "L0_attn_bo")]:
        out[nm] = np.ascontiguousarray(np.asarray(inp[key]).reshape(NCH, 128).T).astype(f32, copy=False)
    return out


def _percore(inp, core):
    f32 = np.float32
    b, par = core // 2, core % 2
    chunks = CHUNKS_EVEN if par == 0 else CHUNKS_ODD
    x = np.asarray(inp["L0_attn_x"])
    rows = np.concatenate([np.arange(g * 512, (g + 1) * 512) for g in chunks])
    xs = np.ascontiguousarray(x[b][rows]).astype(f32, copy=False)

    inv = (1.0 / 10000.0 ** (np.arange(0, DH, 2, dtype=f32) / DH)).astype(f32)
    th = np.einsum("t,i->it", rows.astype(f32), inv).astype(f32)
    cos = np.cos(np.concatenate([th, th], 0)).astype(f32)
    sin = np.sin(np.concatenate([th, th], 0)).astype(f32)
    sinf = np.concatenate([-sin[:64], sin[64:]], 0)
    s = np.float32(DH ** -0.5)

    def pattern(kind):
        m = np.zeros((128, 8 * 512), f32)
        for blk in range(8):
            ch, r = blk // 4, blk % 4
            col = blk * 512
            kk = np.arange(128)[:, None] + r * 128
            qq = np.arange(512)[None, :]
            if kind == 0:
                if ch == 0:
                    m[:, col:col + 512] = (qq >= kk)
            else:
                m[:, col:col + 512] = 1.0 if ch == 0 else (qq >= kk)
        return m

    sel = np.zeros((128, 2 * 8 * 512), f32)
    for jj in range(2):
        gq = chunks[jj]
        diag_at = gq - EXT[jj] + 2
        sel[:, jj * 4096:(jj + 1) * 4096] = pattern(1 if diag_at == 1 else 0)
    return {"x": xs, "cosq": np.ascontiguousarray(cos * s), "sinq": np.ascontiguousarray(sinf * s),
            "cosk": np.ascontiguousarray(cos), "sink": np.ascontiguousarray(sinf),
            "cmask": np.ascontiguousarray(sel.astype(ml_dtypes.bfloat16))}


class Sem:
    def __init__(self, nc, name, stack, is_dma=False):
        self.h = stack.enter_context(nc.semaphore(name))
        self.n = 0
        self.name = name
        self.is_dma = is_dma


class Buf:
    __slots__ = ("name", "w", "readers")

    def __init__(self, name):
        self.name = name
        self.w = None
        self.readers = []


class Ctx:

    def __init__(self, nc, stack):
        self.nc = nc
        self.eng_sems = {}
        self.waited = {}
        self.te_id = id(nc.tensor)
        for eng, nm in [(nc.tensor, "te"), (nc.scalar, "sc"), (nc.vector, "ve"),
                        (nc.gpsimd, "gp")]:
            self.eng_sems[id(eng)] = Sem(nc, "e_" + nm, stack)
        self.dma_sems = {nm: Sem(nc, "d_" + nm, stack, is_dma=True) for nm in
                         ["a", "b", "c", "d", "e", "f", "g", "h"]}
        self.cc_sem = Sem(nc, "cc", stack, is_dma=True)

    def _wait(self, eng, semobj, cnt):
        if cnt <= 0:
            return
        if id(eng) == self.te_id and self.eng_sems.get(self.te_id) is semobj:
            return
        if semobj.is_dma:
            cnt = semobj.n
        key = (id(eng), id(semobj))
        if self.waited.get(key, 0) >= cnt:
            return
        eng.wait_ge(semobj.h, cnt)
        self.waited[key] = cnt

    def _pre(self, eng, reads, writes, is_dma):
        need = {}
        for b in reads:
            if b.w:
                s, c = b.w
                need[id(s)] = (s, max(c, need.get(id(s), (s, 0))[1]))
        for b in writes:
            if b.w:
                s, c = b.w
                need[id(s)] = (s, max(c, need.get(id(s), (s, 0))[1]))
            for s, c in b.readers:
                need[id(s)] = (s, max(c, need.get(id(s), (s, 0))[1]))
        for s, c in need.values():
            self._wait(eng, s, c)

    def _post(self, inst, sem, reads, writes, k):
        inst.then_inc(sem.h, k)
        sem.n += k
        for b in writes:
            b.w = (sem, sem.n)
            b.readers = []
        for b in reads:
            b.readers.append((sem, sem.n))
        return sem.n

    def op(self, eng, fn, reads=(), writes=()):
        self._pre(eng, reads, writes, False)
        return self._post(fn(), self.eng_sems[id(eng)], reads, writes, 1)

    def group(self, eng, fns, reads=(), writes=()):
        self._pre(eng, reads, writes, False)
        inst = None
        for f in fns:
            inst = f()
        return self._post(inst, self.eng_sems[id(eng)], reads, writes, 1)

    def dma(self, q, fn, reads=(), writes=(), ch="a"):
        sem = self.dma_sems[ch]
        self._wait(q, sem, sem.n)
        self._pre(q, reads, writes, True)
        return self._post(fn(), sem, reads, writes, 16)

    def cc(self, q, fn, reads=(), writes=()):
        self._pre(q, reads, writes, True)
        return self._post(fn(), self.cc_sem, reads, writes, 1)

    def join(self, engines):
        for e in engines:
            for s in list(self.eng_sems.values()) + list(self.dma_sems.values()) + [self.cc_sem]:
                self._wait(e, s, s.n)


class B:
    def __init__(self, dev="full"):
        self.dev = dev
        nc = bass.Bass("TRN2", target_bir_lowering=False, num_devices=8)
        self.nc = nc
        self.stk = ExitStack()
        self.sy, self.sc, self.ve, self.te, self.gp = nc.sync, nc.scalar, nc.vector, nc.tensor, nc.gpsimd
        self.cx = Ctx(nc, self.stk)

        dt = nc.dram_tensor
        A = lambda name, shape, d=F32, kind="ExternalInput": dt(name, shape, d, kind=kind).ap()
        self.i = {
            "x": A("x", [TOWN, D]),
            "wq": A("wq", [D, D]), "wk": A("wk", [D, D]), "wv": A("wv", [D, D]),
            "wo": A("wo", [D, D]),
            "g": A("g", [NEX * NF * 128, D]), "w1": A("w1", [NEX * NF * 128, D]),
            "w2": A("w2", [D, NEX * NF * 128]),
            "wa": A("wa", [D, NE]),
            "gb": A("gb", [128, NEX * NF]), "b1": A("b1", [128, NEX * NF]),
            "b2": A("b2", [NEX, D]),
            "bq": A("bq", [128, NCH]), "bk": A("bk", [128, NCH]),
            "bv": A("bv", [128, NCH]), "bo": A("bo", [128, NCH]),
            "cosq": A("cosq", [128, TOWN]), "sinq": A("sinq", [128, TOWN]),
            "cosk": A("cosk", [128, TOWN]), "sink": A("sink", [128, TOWN]),
            "cmask": A("cmask", [128, 2 * 8 * 512], BF16),
        }
        self.o_out = A("out", [TOWN, D], F32, "ExternalOutput")
        self.d_xsp = [dt("xsp0", [TOWN, D], F32).ap(), dt("xsp1", [TOWN, D], F32).ap()]
        self.d_ln = dt("lnsp", [TOWN, D], BF16).ap()
        self.d_kvin = dt("kvin", [2 * D, TOWN], BF16)
        self.d_kvout = dt("kvout", [4 * D, TOWN], BF16)

        self.dbg = {}
        if self.dev != "full":
            for nm, shape, d in [("ln1", [TOWN, D], BF16), ("gates", [NEX, TOWN], BF16),
                                 ("x1", [TOWN, D], F32), ("q", [D, TOWN], BF16),
                                 ("aout", [D, TOWN], BF16), ("x2", [TOWN, D], F32),
                                 ("ln2", [TOWN, D], BF16)]:
                self.dbg[nm] = A("dbg_" + nm, shape, d, "ExternalOutput")

        sb = self.stk.enter_context
        U8 = mybir.dt.uint8
        self.A1 = sb(nc.sbuf_tensor("A1", [128, 65536], U8))
        self.A2 = sb(nc.sbuf_tensor("A2", [128, 65536], U8))
        self.A3 = sb(nc.sbuf_tensor("A3", [128, 65536], U8))
        self.SM = sb(nc.sbuf_tensor("SM", [128, 8704], U8))
        self.bufs = {}

        self.bias = self.v(self.SM, 0, 128, 352, F32)
        self.b2e = self.v(self.SM, 1408, NEX, 2048, BF16)
        self.gate_b = self.v(self.SM, 5504, NEX, 1024, BF16)
        self.ones_bf = self.v(self.SM, 7552, 128, 128, BF16)
        self.ident_bf = self.v(self.SM, 7808, 128, 128, BF16)
        self.col = [self.v(self.SM, 8064 + 4 * i, 128, 1, F32) for i in range(4)]
        self.wa_bf = self.v(self.SM, 8192, 128, 256, BF16)

    def bf(self, name):
        if name not in self.bufs:
            self.bufs[name] = Buf(name)
        return self.bufs[name]

    def v(self, arena, off, parts, cols, dtype):
        isz = {F32: 4, BF16: 2}[dtype]
        return arena[0:parts, off:off + cols * isz].bitcast(dtype)

    def bias_col(self, which, idx):
        offs = {"bq": 0, "bk": NCH, "bv": 2 * NCH, "bo": 3 * NCH,
                "gb": 4 * NCH, "b1": 4 * NCH + NEX * NF}
        o = offs[which] + idx
        return self.bias[:, o:o + 1]

    RG = {
        "gwf0": 0, "gwf1": 8192,
        "w2f0": 16384, "w2f1": 20480,
        "gwb0": 24576, "gwb1": 28672,
        "w2b0": 32768, "w2b1": 34816,
        "gbc0": 36864, "gbc1": 38912,
        "ffa0": 40960, "ffa1": 41984,
        "rt": 43008,
        "ct_x0": 45056, "ct_x1": 53248,
        "ct_xb": 61440,
    }

    def reg(self, name, parts, cols, dtype):
        return self.v(self.A2, self.RG[name], parts, cols, dtype), self.bf("A2_" + name)

    def setup(self):
        cx, sy, gp, ve = self.cx, self.sy, self.gp, self.ve
        bS = self.bf("smalls")
        cx.op(gp, lambda: gp.memset(self.ident_bf[:], 1.0), writes=[bS])
        cx.op(gp, lambda: gp.affine_select(self.ident_bf[:], self.ident_bf[:],
                                           pattern=[[1, 128]], compare_op=ALU.is_equal,
                                           fill=0.0, base=0, channel_multiplier=-1),
              reads=[bS], writes=[bS])
        cx.op(ve, lambda: ve.memset(self.ones_bf[:], 1.0), writes=[bS])
        o = 0
        for nm in ["bq", "bk", "bv", "bo", "gb", "b1"]:
            src = self.i[nm]
            w = NCH if nm in ("bq", "bk", "bv", "bo") else NEX * NF
            cx.dma(sy, lambda src=src, o=o, w=w: sy.dma_start(out=self.bias[:, o:o + w], in_=src),
                   writes=[bS], ch="a")
            o += w
        b2f = self.v(self.A3, 0, NEX, 2048, F32)
        bT = self.bf("A3_setup")
        cx.dma(sy, lambda: sy.dma_start(out=b2f, in_=self.i["b2"]), writes=[bT], ch="a")
        cx.op(ve, lambda: ve.tensor_scalar(self.b2e[:, :], b2f, 0.5, None, op0=ALU.mult),
              reads=[bT], writes=[bS])
        rt = self.RG["rt"]
        self.r_abf = self.A2[0:1, rt + 512: rt + 1536].bitcast(BF16)
        wa_f = self.v(self.A3, 16384, 128, 256, F32)
        cx.dma(sy, lambda: sy.dma_start(
            out=wa_f.rearrange("p (c e) -> p c e", c=NCH),
            in_=self.i["wa"].rearrange("(c p) e -> p c e", p=128)), writes=[bT], ch="a")
        cx.op(ve, lambda: ve.tensor_copy(self.wa_bf, wa_f), reads=[bT], writes=[self.bf("wa_bf")])

    def tile_unit(self, tt, x_src, bxsrc, x_dst, bxdst, inc_bf, binc, psums,
                  do_ln, dbg_x=None, dbg_ln=None):
        return self.tile_unit_span(tt, x_src, bxsrc, x_dst, bxdst, inc_bf, binc,
                                   HALF, psums, do_ln, dbg_x, dbg_ln)

    def tile_unit_span(self, tt, x_src, bxsrc, x_dst, bxdst, inc_bf, binc, span,
                       psums, do_ln, dbg_x=None, dbg_ln=None):
        cx = self.cx
        sy, sc, ve, te = self.sy, self.sc, self.ve, self.te
        xt, bxt = self.reg("ct_x%d" % (tt % 2), 128, 2048, F32)
        xb, bxb = self.reg("ct_xb", 128, 2048, BF16)
        sq0, bsq0 = self.reg("ffa0", 128, 512, BF16)
        sq1, bsq1 = self.reg("ffa1", 128, 512, BF16)
        sqs = [(sq0, bsq0), (sq1, bsq1)]
        tloc = tt % (span // 128)
        cx.dma(sy, lambda: sy.dma_start(out=xt, in_=x_src[tt * 128:(tt + 1) * 128, :]),
               reads=[bxsrc] if bxsrc else [], writes=[bxt], ch="g")
        if inc_bf is not None:
            for b in range(4):
                fns = []
                for j in range(4):
                    ds = 4 * b + j
                    fns.append(lambda ds=ds, b=b, j=j: te.matmul(
                        psums[b][0][:, j * 128:(j + 1) * 128],
                        inc_bf[:, ds * span + tloc * 128: ds * span + tloc * 128 + 128],
                        self.ident_bf[:, :], start=True, stop=True))
                cx.group(te, fns, reads=[binc, self.bf("smalls")], writes=[psums[b][1]])
            for b in range(4):
                cx.op(ve, lambda b=b: ve.tensor_tensor(
                    xt[:, b * 512:(b + 1) * 512], xt[:, b * 512:(b + 1) * 512],
                    psums[b][0][:, 0:512], op=ALU.add),
                    reads=[psums[b][1], bxt], writes=[bxt])
        if x_dst is not None:
            cx.dma(sy, lambda: sy.dma_start(out=x_dst[tt * 128:(tt + 1) * 128, :], in_=xt),
                   reads=[bxt], writes=[bxdst] if bxdst else [], ch="c")
        if dbg_x is not None:
            cx.dma(sy, lambda: sy.dma_start(out=dbg_x[tt * 128:(tt + 1) * 128, :], in_=xt),
                   reads=[bxt], ch="d")
        if not do_ln:
            return
        cx.op(ve, lambda: ve.tensor_copy(xb, xt), reads=[bxt], writes=[bxb])
        c0, c1, c2, c3 = self.col
        bR = self.bf("cols")
        for ch in range(4):
            sq, bsq = sqs[ch % 2]
            cx.op(sc, lambda ch=ch, sq=sq: sc.activation(
                sq, xb[:, ch * 512:(ch + 1) * 512], AF.Square), reads=[bxb], writes=[bsq])
            if ch == 0:
                cx.op(ve, lambda: ve.reduce_sum(c0, xb[:, 0:512], axis=AXX),
                      reads=[bxb], writes=[bR])
                cx.op(ve, lambda sq=sq: ve.reduce_sum(c1, sq, axis=AXX),
                      reads=[bsq], writes=[bR])
            else:
                cx.op(ve, lambda ch=ch: ve.reduce_sum(c2, xb[:, ch * 512:(ch + 1) * 512],
                                                      axis=AXX), reads=[bxb], writes=[bR])
                cx.op(ve, lambda: ve.tensor_add(c0, c0, c2), reads=[bR], writes=[bR])
                cx.op(ve, lambda sq=sq: ve.reduce_sum(c3, sq, axis=AXX),
                      reads=[bsq], writes=[bR])
                cx.op(ve, lambda: ve.tensor_add(c1, c1, c3), reads=[bR], writes=[bR])
        cx.op(ve, lambda: ve.tensor_scalar(c0, c0, 1.0 / D, None, op0=ALU.mult),
              reads=[bR], writes=[bR])
        cx.op(ve, lambda: ve.tensor_scalar(c1, c1, 1.0 / D, None, op0=ALU.mult),
              reads=[bR], writes=[bR])
        cx.op(ve, lambda: ve.tensor_mul(c2, c0, c0), reads=[bR], writes=[bR])
        cx.op(ve, lambda: ve.tensor_sub(c1, c1, c2), reads=[bR], writes=[bR])
        cx.op(ve, lambda: ve.tensor_scalar(c1, c1, 1e-5, None, op0=ALU.add),
              reads=[bR], writes=[bR])
        cx.op(sc, lambda: sc.activation(c1, c1, AF.Sqrt), reads=[bR], writes=[bR])
        cx.op(ve, lambda: ve.reciprocal(c1, c1), reads=[bR], writes=[bR])
        cx.op(ve, lambda: ve.tensor_mul(c0, c0, c1), reads=[bR], writes=[bR])
        cx.op(ve, lambda: ve.tensor_scalar(c0, c0, -1.0, None, op0=ALU.mult),
              reads=[bR], writes=[bR])
        cx.op(ve, lambda: ve.tensor_scalar(xb, xb, c1, c0, op0=ALU.mult, op1=ALU.add),
              reads=[bR, bxb], writes=[bxb])
        cx.dma(sy, lambda: sy.dma_start(out=self.d_ln[tt * 128:(tt + 1) * 128, :], in_=xb),
               reads=[bxb], writes=[self.bf("dram_ln")], ch="f")
        if dbg_ln is not None:
            cx.dma(sy, lambda: sy.dma_start(out=dbg_ln[tt * 128:(tt + 1) * 128, :], in_=xb),
                   reads=[bxb], ch="d")

    def pre_phase(self, psums):
        for tt in range(16):
            self.tile_unit(tt, self.i["x"], None, None, None, None, None, psums,
                           do_ln=True, dbg_ln=self.dbg.get("ln1"))

    def lnA_ap(self):
        return self.v(self.A3, 0, 128, NCH * HALF, BF16), self.bf("A3_lnA")

    def load_lnA(self, half, psums):
        cx, sy, te, ve = self.cx, self.sy, self.te, self.ve
        lnA, blnA = self.lnA_ap()
        bln = self.bf("dram_ln")
        for i in range(8):
            tt = half * 8 + i
            st, bst = self.reg("ct_x%d" % (i % 2), 128, 4096, BF16)
            cx.dma(sy, lambda tt=tt, st=st: sy.dma_start(
                out=st[:, 0:2048], in_=self.d_ln[tt * 128:(tt + 1) * 128, :]),
                reads=[bln], writes=[bst], ch="g")
            for b in range(4):
                fns = []
                for j in range(4):
                    ds = 4 * b + j
                    fns.append(lambda ds=ds, b=b, j=j, st=st: te.matmul(
                        psums[b][0][:, j * 128:(j + 1) * 128],
                        st[:, ds * 128:(ds + 1) * 128],
                        self.ident_bf[:, :], start=True, stop=True))
                cx.group(te, fns, reads=[bst, self.bf("smalls")], writes=[psums[b][1]])
            for b in range(4):
                for j in range(4):
                    ds = 4 * b + j
                    cx.op(ve, lambda ds=ds, b=b, j=j, i=i: ve.tensor_copy(
                        lnA[:, ds * HALF + i * 128: ds * HALF + i * 128 + 128],
                        psums[b][0][:, j * 128:(j + 1) * 128]),
                        reads=[psums[b][1]], writes=[blnA])

    def router(self, half, ps_y0, ps_y1):
        nc, cx = self.nc, self.cx
        sy, sc, ve, te = self.sy, self.sc, self.ve, self.te
        lnA, blnA = self.lnA_ap()
        rt = self.RG["rt"]
        aff = self.v(self.A2, rt, 128, 18, F32)
        eq1 = self.v(self.A2, rt + 80, 128, 18, F32)
        eq2 = self.v(self.A2, rt + 160, 128, 18, F32)
        G = self.v(self.A2, rt + 240, 128, 32, BF16)
        Gf = self.v(self.A2, rt + 368, 128, 18, F32)
        m1 = self.v(self.A2, rt + 304, 128, 1, F32)
        m2 = self.v(self.A2, rt + 312, 128, 1, F32)
        p1 = self.v(self.A2, rt + 320, 128, 1, F32)
        p2 = self.v(self.A2, rt + 328, 128, 1, F32)
        ss = self.v(self.A2, rt + 336, 128, 1, F32)
        brt = self.bf("A2_rt")
        bgb = self.bf("gate_b")
        ps_r = ps_y0[:, 0:16]
        ps_g = ps_y1[0:NEX, 0:128]
        bpr, bpg = self.bf("ps_y0"), self.bf("ps_y1")
        for tb in range(8):
            cx.group(te, [(lambda c=c, tb=tb: te.matmul(
                ps_r, lnA[:, c * HALF + tb * 128: c * HALF + tb * 128 + 128],
                self.wa_bf[:, c * 16:(c + 1) * 16],
                start=(c == 0), stop=(c == NCH - 1))) for c in range(NCH)],
                reads=[blnA, self.bf("wa_bf")], writes=[bpr])
            cx.op(sc, lambda: sc.activation(aff[:, 0:16], ps_r, AF.Sigmoid),
                  reads=[bpr], writes=[brt])
            for f in [
                lambda: ve.reduce_max(m1, aff[:, 0:16], axis=AXX),
                lambda: ve.tensor_scalar(eq1[:, 0:16], aff[:, 0:16], m1, None, op0=ALU.is_equal),
                lambda: ve.scalar_tensor_tensor(eq2[:, 0:16], eq1[:, 0:16], -1e9, aff[:, 0:16],
                                                op0=ALU.mult, op1=ALU.add),
                lambda: ve.reduce_max(m2, eq2[:, 0:16], axis=AXX),
                lambda: ve.tensor_add(ss, m1, m2),
                lambda: ve.reciprocal(ss, ss),
                lambda: ve.tensor_mul(p1, m1, ss),
                lambda: ve.tensor_mul(p2, m2, ss),
                lambda: ve.tensor_scalar(eq2[:, 0:16], eq2[:, 0:16], m2, None, op0=ALU.is_equal),
                lambda: ve.tensor_scalar(Gf[:, 0:16], eq1[:, 0:16], p1, None, op0=ALU.mult),
                lambda: ve.scalar_tensor_tensor(G[:, 0:16], eq2[:, 0:16], p2, Gf[:, 0:16],
                                                op0=ALU.mult, op1=ALU.add),
                lambda: ve.memset(G[:, 16:18], 1.0),
            ]:
                cx.op(ve, f, reads=[brt], writes=[brt])
            cx.op(te, lambda: te.matmul(ps_g, G[:, 0:18], self.ident_bf[:, :],
                                        start=True, stop=True),
                  reads=[brt, self.bf("smalls")], writes=[bpg])
            cx.op(ve, lambda tb=tb: ve.tensor_copy(
                self.gate_b[:, tb * 128:(tb + 1) * 128], ps_g),
                reads=[bpg], writes=[bgb])
            if "gates" in self.dbg:
                cx.dma(sy, lambda tb=tb, half=half: sy.dma_start(
                    out=self.dbg["gates"][0:NEX, half * HALF + tb * 128: half * HALF + tb * 128 + 128],
                    in_=self.gate_b[:, tb * 128:(tb + 1) * 128]), reads=[bgb], ch="d")

    def acc_ap(self):
        return self.v(self.A1, 0, 128, NCH * HALF, F32), self.bf("A1_acc")

    def acc_init(self, half, ps0, b0, ps1, b1):
        cx, ve, te = self.cx, self.ve, self.te
        lnA, blnA = self.lnA_ap()
        acc, bacc = self.acc_ap()
        bp = [b0, b1]
        for ds in range(NCH):
            for tcc in range(2):
                ip = (ds * 2 + tcc) % 2
                ps = ps0 if ip == 0 else ps1
                cx.op(te, lambda ds=ds, tcc=tcc, ps=ps: te.matmul(
                    ps[:], self.b2e[:, ds * 128:(ds + 1) * 128],
                    self.gate_b[:, tcc * 512:(tcc + 1) * 512], start=True, stop=True),
                    reads=[self.bf("smalls"), self.bf("gate_b")], writes=[bp[ip]])
                cx.op(ve, lambda ds=ds, tcc=tcc, ps=ps: ve.scalar_tensor_tensor(
                    acc[:, ds * HALF + tcc * 512: ds * HALF + tcc * 512 + 512],
                    lnA[:, ds * HALF + tcc * 512: ds * HALF + tcc * 512 + 512],
                    0.5, ps[:], op0=ALU.mult, op1=ALU.add),
                    reads=[bp[ip], blnA], writes=[bacc])

    def expert_group(self, gi, psums):
        (ps_G0, ps_G1, ps_W0, ps_W1, ps_y0, ps_y1) = psums
        cx = self.cx
        sy, sc, ve, te, gp = self.sy, self.sc, self.ve, self.te, self.gp
        e0, e1 = GROUPS[gi]
        routed = e0 < NE
        lnA, blnA = self.lnA_ap()
        acc, bacc = self.acc_ap()
        yh = self.v(self.A3, 32768, 128, 16 * HALF, BF16)
        byh = self.bf("A3_yh")
        gwf, bgwf, gwb, bgwb, w2f, bw2f, w2b, bw2b, gbc, bgbc, ffa, bffa = \
            [], [], [], [], [], [], [], [], [], [], [], []
        for u in range(2):
            a, b_ = self.reg(f"gwf{u}", 128, 2048, F32); gwf.append(a); bgwf.append(b_)
            a, b_ = self.reg(f"gwb{u}", 128, 2048, BF16); gwb.append(a); bgwb.append(b_)
            a, b_ = self.reg(f"w2f{u}", 128, 1024, F32); w2f.append(a); bw2f.append(b_)
            a, b_ = self.reg(f"w2b{u}", 128, 1024, BF16); w2b.append(a); bw2b.append(b_)
            a, b_ = self.reg(f"gbc{u}", 128, 1024, BF16); gbc.append(a); bgbc.append(b_)
            a, b_ = self.reg(f"ffa{u}", 128, 512, BF16); ffa.append(a); bffa.append(b_)
        bG = [self.bf("ps_G0"), self.bf("ps_G1")]
        bW = [self.bf("ps_W0"), self.bf("ps_W1")]
        bY = [self.bf("ps_y0"), self.bf("ps_y1")]
        if routed:
            for k, e in enumerate((e0, e1)):
                for tcc in range(2):
                    row = self.r_abf
                    cx.dma(sy, lambda e=e, tcc=tcc: sy.dma_start(
                        out=row[:, 0:512],
                        in_=self.gate_b[e:e + 1, tcc * 512:(tcc + 1) * 512]),
                        reads=[self.bf("gate_b")], writes=[self.bf("rows")], ch="b")
                    cx.op(te, lambda: te.matmul(ps_y0[:], self.ones_bf[0:1, :], row[:, 0:512],
                                                start=True, stop=True),
                          reads=[self.bf("rows"), self.bf("smalls")], writes=[bY[0]])
                    cx.op(ve, lambda k=k, tcc=tcc: ve.tensor_copy(
                        gbc[k][:, tcc * 512:(tcc + 1) * 512], ps_y0[:]),
                        reads=[bY[0]], writes=[bgbc[k]])
        for si in range(16):
            ei = (e0, e1)[si // 8]
            fs = si % 8
            row0 = (ei * NF + fs) * 128
            for dh2 in range(2):
                u = dh2
                cx.dma(sy, lambda row0=row0, dh2=dh2, u=u: sy.dma_start(
                    out=gwf[u][:, 0:1024],
                    in_=self.i["g"][row0:row0 + 128, dh2 * 1024:(dh2 + 1) * 1024]),
                    writes=[bgwf[u]], ch="c")
                cx.dma(sy, lambda row0=row0, dh2=dh2, u=u: sy.dma_start(
                    out=gwf[u][:, 1024:2048],
                    in_=self.i["w1"][row0:row0 + 128, dh2 * 1024:(dh2 + 1) * 1024]),
                    writes=[bgwf[u]], ch="d")
                cx.op(gp, lambda u=u: gp.tensor_copy(gwb[u][:, :], gwf[u][:, :]),
                      reads=[bgwf[u]], writes=[bgwb[u]])
                fns = []
                for tcc in range(2):
                    for cc in range(8):
                        c = dh2 * 8 + cc
                        fns.append(lambda cc=cc, c=c, u=u, tcc=tcc: te.matmul(
                            (ps_G0 if tcc == 0 else ps_G1)[:],
                            gwb[u][:, cc * 128:(cc + 1) * 128],
                            lnA[:, c * HALF + tcc * 512: c * HALF + tcc * 512 + 512],
                            start=(c == 0), stop=(c == NCH - 1)))
                        fns.append(lambda cc=cc, c=c, u=u, tcc=tcc: te.matmul(
                            (ps_W0 if tcc == 0 else ps_W1)[:],
                            gwb[u][:, 1024 + cc * 128: 1024 + (cc + 1) * 128],
                            lnA[:, c * HALF + tcc * 512: c * HALF + tcc * 512 + 512],
                            start=(c == 0), stop=(c == NCH - 1)))
                cx.group(te, fns, reads=[bgwb[u], blnA],
                         writes=[bG[0], bG[1], bW[0], bW[1]])
            for tcc in range(2):
                psG = ps_G0 if tcc == 0 else ps_G1
                psW = ps_W0 if tcc == 0 else ps_W1
                ysl = yh[:, si * HALF + tcc * 512: si * HALF + tcc * 512 + 512]
                cx.op(sc, lambda psG=psG, ysl=ysl, ei=ei, fs=fs: sc.activation(
                    ysl, psG[:], AF.Gelu, bias=self.bias_col("gb", ei * NF + fs), scale=1.0),
                    reads=[bG[tcc], self.bf("smalls")], writes=[byh])
                cx.op(ve, lambda psW=psW, tcc=tcc, ei=ei, fs=fs: ve.tensor_scalar(
                    ffa[tcc][:, :], psW[:], self.bias_col("b1", ei * NF + fs), None, op0=ALU.add),
                    reads=[bW[tcc], self.bf("smalls")], writes=[bffa[tcc]])
                if routed:
                    k = si // 8
                    cx.op(ve, lambda tcc=tcc, k=k: ve.tensor_tensor(
                        ffa[tcc][:, :], ffa[tcc][:, :],
                        gbc[k][:, tcc * 512:(tcc + 1) * 512], op=ALU.mult),
                        reads=[bffa[tcc], bgbc[k]], writes=[bffa[tcc]])
                cx.op(ve, lambda tcc=tcc, ysl=ysl: ve.tensor_tensor(
                    ysl, ysl, ffa[tcc][:, :], op=ALU.mult),
                    reads=[bffa[tcc], byh], writes=[byh])
        col0 = e0 * NF * 128
        for ds in range(NCH):
            for w2h in range(2):
                u = w2h
                cx.dma(sy, lambda ds=ds, w2h=w2h, u=u, col0=col0: sy.dma_start(
                    out=w2f[u][:, :],
                    in_=self.i["w2"][ds * 128:(ds + 1) * 128,
                                     col0 + w2h * 1024: col0 + (w2h + 1) * 1024]),
                    writes=[bw2f[u]], ch="e")
                cx.op(gp, lambda u=u: gp.tensor_scalar(w2b[u][:, :], w2f[u][:, :], 0.5, None,
                                                       op0=ALU.mult),
                      reads=[bw2f[u]], writes=[bw2b[u]])
                fns = []
                for tcc in range(2):
                    for blk in range(8):
                        gblk = w2h * 8 + blk
                        fns.append(lambda blk=blk, gblk=gblk, u=u, tcc=tcc, ds=ds: te.matmul(
                            (ps_y0 if tcc == 0 else ps_y1)[:],
                            w2b[u][:, blk * 128:(blk + 1) * 128],
                            yh[:, gblk * HALF + tcc * 512: gblk * HALF + tcc * 512 + 512],
                            start=(gblk == 0), stop=(gblk == 15)))
                cx.group(te, fns, reads=[bw2b[u], byh], writes=[bY[0], bY[1]])
            for tcc in range(2):
                psY = ps_y0 if tcc == 0 else ps_y1
                asl = acc[:, ds * HALF + tcc * 512: ds * HALF + tcc * 512 + 512]
                cx.op(ve, lambda psY=psY, asl=asl: ve.tensor_tensor(asl, asl, psY[:], op=ALU.add),
                      reads=[bY[tcc]], writes=[bacc])

    def combine(self, half, call, xsp_in, bspin, xsp_out, bspout, psums,
                last=False, dbg_key=None):
        cx, ve = self.cx, self.ve
        acc, bacc = self.acc_ap()
        inc_bf = self.v(self.A3, 32768, 128, NCH * HALF, BF16)
        binc = self.bf("A3_yh")
        for q in range(4):
            cx.op(ve, lambda q=q: ve.tensor_copy(
                inc_bf[:, q * 4 * HALF:(q + 1) * 4 * HALF],
                acc[:, q * 4 * HALF:(q + 1) * 4 * HALF]), reads=[bacc], writes=[binc])
        dbg_x = self.dbg.get(dbg_key) if dbg_key else None
        dbg_ln = self.dbg.get("ln2") if (call == 0 and not last) else None
        for i in range(8):
            tt = half * 8 + i
            self.tile_unit(tt, xsp_in, bspin, xsp_out, bspout, inc_bf, binc,
                           psums, do_ln=(not last), dbg_x=dbg_x, dbg_ln=dbg_ln)

    def ffn_call(self, call, xsp_in, bspin, xsp_out, bspout, last=False):
        nc = self.nc
        with (
            nc.psum_tensor("f%d_G0" % call, [128, 512], F32) as ps_G0,
            nc.psum_tensor("f%d_G1" % call, [128, 512], F32) as ps_G1,
            nc.psum_tensor("f%d_W0" % call, [128, 512], F32) as ps_W0,
            nc.psum_tensor("f%d_W1" % call, [128, 512], F32) as ps_W1,
            nc.psum_tensor("f%d_y0" % call, [128, 512], F32) as ps_y0,
            nc.psum_tensor("f%d_y1" % call, [128, 512], F32) as ps_y1,
        ):
            psums = (ps_G0, ps_G1, ps_W0, ps_W1, ps_y0, ps_y1)
            ps4 = [(ps_G0, self.bf("ps_G0")), (ps_G1, self.bf("ps_G1")),
                   (ps_W0, self.bf("ps_W0")), (ps_W1, self.bf("ps_W1"))]
            for half in range(2):
                self.load_lnA(half, ps4)
                self.router(half, ps_y0, ps_y1)
                self.acc_init(half, ps_y0, self.bf("ps_y0"), ps_y1, self.bf("ps_y1"))
                for gi in range(9):
                    self.expert_group(gi, psums)
                self.combine(half, call, xsp_in, bspin, xsp_out, bspout, ps4,
                             last=last, dbg_key=("x1" if call == 0 else None))


    def qrot_ap(self):
        return self.v(self.A1, 0, 128, NH * TOWN, BF16), self.bf("A1_acc")

    def attn_proj(self, half, ps4, ps_q):
        nc, cx = self.nc, self.cx
        sy, sc, ve, te, gp = self.sy, self.sc, self.ve, self.te, self.gp
        lnA, blnA = self.lnA_ap()
        qrot, bqr = self.qrot_ap()
        wf = [self.v(self.A2, i * 8192, 128, 2048, F32) for i in range(2)]
        bwf = [self.bf("A2_gwf%d" % i) for i in range(2)]
        wb = [self.v(self.A2, 16384 + i * 4096, 128, 2048, BF16) for i in range(2)]
        bwb = [self.bf("at_wb%d" % i) for i in range(2)]
        tabs = {}
        for ti, nm in enumerate(["cosq", "sinq", "cosk", "sink"]):
            tabs[nm] = (self.v(self.A2, 24576 + ti * 4096, 128, 1024, F32),
                        self.bf("at_" + nm))
        qraw = self.v(self.A2, 40960, 128, 1024, F32)
        bqraw = self.bf("at_qraw")
        qsw = self.v(self.A2, 45056, 128, 1024, F32)
        bqsw = self.bf("at_qsw")
        u1 = self.v(self.A2, 49152, 128, 1024, F32)
        bu1 = self.bf("at_u1")
        kst = self.v(self.A2, 53248, 128, 1024, BF16)
        bkst = self.bf("at_kst")
        vst = self.v(self.A2, 55296, 128, 1024, BF16)
        bvst = self.bf("at_vst")
        vt_sb = self.v(self.A3, 32768, 128, 8 * 2048, BF16)
        bvt = self.bf("A3_yh")
        bkvin = self.bf("dram_kvin")
        kv = self.d_kvin.ap()
        for nm in ["cosq", "sinq", "cosk", "sink"]:
            cx.dma(sy, lambda nm=nm: sy.dma_start(
                out=tabs[nm][0], in_=self.i[nm][:, half * HALF:(half + 1) * HALF]),
                writes=[tabs[nm][1]], ch="b")
        for j in range(NH):
            for ti, (wnm, bias_nm) in enumerate([("wq", "bq"), ("wk", "bk"), ("wv", "bv")]):
                u = (j * 3 + ti) % 2
                cx.dma(sy, lambda wnm=wnm, j=j, u=u: sy.dma_start(
                    out=wf[u], in_=self.i[wnm][j * 128:(j + 1) * 128, :]),
                    writes=[bwf[u]], ch="c" if ti != 1 else "d")
                cx.op(gp, lambda u=u: gp.tensor_copy(wb[u][:, :], wf[u][:, :]),
                      reads=[bwf[u]], writes=[bwb[u]])
                for tcc in range(2):
                    ip = tcc % 2
                    ps = ps_q[ip]
                    fns = [(lambda c=c, u=u, tcc=tcc, ps=ps: te.matmul(
                        ps[0][:], wb[u][:, c * 128:(c + 1) * 128],
                        lnA[:, c * HALF + tcc * 512: c * HALF + tcc * 512 + 512],
                        start=(c == 0), stop=(c == NCH - 1))) for c in range(NCH)]
                    cx.group(te, fns, reads=[bwb[u], blnA], writes=[ps[1]])
                    if ti == 2:
                        cx.op(sc, lambda ps=ps, tcc=tcc, j=j: sc.activation(
                            vst[:, tcc * 512:(tcc + 1) * 512], ps[0][:], AF.Identity,
                            bias=self.bias_col("bv", j), scale=1.0),
                            reads=[ps[1], self.bf("smalls")], writes=[bvst])
                    else:
                        cx.op(sc, lambda ps=ps, tcc=tcc, j=j, bias_nm=bias_nm: sc.activation(
                            qraw[:, tcc * 512:(tcc + 1) * 512], ps[0][:], AF.Identity,
                            bias=self.bias_col(bias_nm, j), scale=1.0),
                            reads=[ps[1], self.bf("smalls")], writes=[bqraw])
                if ti == 2:
                    for tl in range(8):
                        ps = ps_q[tl % 2]
                        cx.op(te, lambda tl=tl, ps=ps: te.matmul(
                            ps[0][:, 0:128], vst[:, tl * 128:(tl + 1) * 128],
                            self.ident_bf[:, :], start=True, stop=True),
                            reads=[bvst, self.bf("smalls")], writes=[ps[1]])
                        cx.op(ve, lambda tl=tl, j=j, ps=ps: ve.tensor_copy(
                            vt_sb[:, tl * 2048 + j * 128: tl * 2048 + j * 128 + 128],
                            ps[0][:, 0:128]), reads=[ps[1]], writes=[bvt])
                else:
                    cx.dma(sy, lambda: sy.dma_start(out=qsw[0:64, :], in_=qraw[64:128, :]),
                           reads=[bqraw], writes=[bqsw], ch="e")
                    cx.dma(sy, lambda: sy.dma_start(out=qsw[64:128, :], in_=qraw[0:64, :]),
                           reads=[bqraw], writes=[bqsw], ch="e")
                    cos = tabs["cosq" if ti == 0 else "cosk"]
                    sin = tabs["sinq" if ti == 0 else "sink"]
                    cx.op(ve, lambda cos=cos: ve.tensor_mul(u1, qraw, cos[0]),
                          reads=[bqraw, cos[1]], writes=[bu1])
                    cx.op(ve, lambda sin=sin: ve.tensor_mul(qsw, qsw, sin[0]),
                          reads=[bqsw, sin[1]], writes=[bqsw])
                    if ti == 0:
                        cx.op(ve, lambda j=j, half=half: ve.tensor_add(
                            qrot[:, j * TOWN + half * HALF: j * TOWN + half * HALF + HALF],
                            u1, qsw), reads=[bu1, bqsw], writes=[bqr])
                    else:
                        cx.op(ve, lambda: ve.tensor_add(kst, u1, qsw),
                              reads=[bu1, bqsw], writes=[bkst])
                        cx.dma(sy, lambda j=j, half=half: sy.dma_start(
                            out=kv[j * 128:(j + 1) * 128, half * HALF:(half + 1) * HALF],
                            in_=kst), reads=[bkst], writes=[bkvin], ch="f")
        for tl in range(8):
            cx.dma(sy, lambda tl=tl, half=half: sy.dma_start(
                out=kv[D + half * HALF + tl * 128: D + half * HALF + (tl + 1) * 128, :],
                in_=vt_sb[:, tl * 2048:(tl + 1) * 2048]),
                reads=[bvt], writes=[bkvin], ch="f")

    def attn_inner(self):
        nc, cx = self.nc, self.cx
        sy, sc, ve, te, gp = self.sy, self.sc, self.ve, self.te, self.gp
        qrot, bqr = self.qrot_ap()
        ao = self.v(self.A2, 0, 128, NH * TOWN, BF16)
        bao = self.bf("at_ao")
        kk = [self.v(self.A3, i * 8192, 128, 4096, BF16) for i in range(2)]
        bkk = [self.bf("at_k%d" % i) for i in range(2)]
        vv = [self.v(self.A3, 16384 + i * 8192, 128, 4096, BF16) for i in range(2)]
        bvv = [self.bf("at_v%d" % i) for i in range(2)]
        cm = self.v(self.A3, 32768, 128, 2 * 8 * 512, BF16)
        bcm = self.bf("at_cm")
        pp = [self.v(self.A3, 49152 + i * 2048, 128, 1024, BF16) for i in range(2)]
        bpp = [self.bf("at_p%d" % i) for i in range(2)]
        bc = [self.v(self.A3, 53248 + i * 2048, 128, 512, F32) for i in range(2)]
        bbc = [self.bf("at_bc%d" % i) for i in range(2)]
        dro = self.A3[0:1, 57344:59392].bitcast(F32)
        drb = self.A3[0:1, 59392:60416].bitcast(BF16)
        bdr = self.bf("at_dr")
        bkvo = self.bf("dram_kvout")
        kvo = self.d_kvout.ap()
        cx.dma(sy, lambda: sy.dma_start(out=cm, in_=self.i["cmask"]), writes=[bcm], ch="b")
        with (
            nc.psum_tensor("a_s0", [128, 1024], F32) as ps_s0,
            nc.psum_tensor("a_s1", [128, 1024], F32) as ps_s1,
            nc.psum_tensor("a_o", [128, 512], F32) as ps_o,
            nc.psum_tensor("a_d", [128, 512], F32) as ps_d,
            nc.psum_tensor("a_b", [128, 512], F32) as ps_b,
        ):
            bs = [self.bf("a_s0"), self.bf("a_s1")]
            pss = [ps_s0, ps_s1]
            bo, bd, bb = self.bf("a_o"), self.bf("a_d"), self.bf("a_b")
            def load_head(h):
                u = h % 2
                for g in range(8):
                    r, lc = CHUNK_OWNER[g]
                    cx.dma(sy, lambda h=h, u=u, g=g, r=r, lc=lc: sy.dma_start(
                        out=kk[u][:, g * 512:(g + 1) * 512],
                        in_=kvo[r * 2 * D + h * 128: r * 2 * D + (h + 1) * 128,
                                lc * 512:(lc + 1) * 512]),
                        reads=[bkvo], writes=[bkk[u]], ch="g")
                    cx.dma(sy, lambda h=h, u=u, g=g, r=r, lc=lc: sy.dma_start(
                        out=vv[u][:, g * 512:(g + 1) * 512].rearrange(
                            "p (kb f) -> p kb f", kb=4),
                        in_=kvo[r * 2 * D + D + lc * 512: r * 2 * D + D + (lc + 1) * 512,
                                h * 128:(h + 1) * 128].rearrange(
                            "(kb p) f -> p kb f", p=128)),
                        reads=[bkvo], writes=[bvv[u]], ch="h")
            load_head(0)
            for h in range(NH):
                u = h % 2
                if h + 1 < NH:
                    load_head(h + 1)
                for j in range(NSLOT):
                    npair = EXT[j] * 2
                    qsl = qrot[:, h * TOWN + j * 512: h * TOWN + (j + 1) * 512]
                    for pi in range(npair):
                        sp = pss[pi % 2]
                        bsp = bs[pi % 2]
                        fns = []
                        for half2 in range(2):
                            kb = pi * 2 + half2
                            fns.append(lambda kb=kb, half2=half2, u=u, sp=sp, qsl=qsl:
                                       te.matmul(sp[:, half2 * 512:(half2 + 1) * 512],
                                                 kk[u][:, kb * 128:(kb + 1) * 128],
                                                 qsl, start=True, stop=True))
                        cx.group(te, fns, reads=[bkk[u], bqr], writes=[bsp])
                        pb = pp[pi % 2]
                        bpb = bpp[pi % 2]
                        cx.op(sc, lambda sp=sp, pb=pb: sc.activation(pb, sp[:], AF.Exp),
                              reads=[bsp], writes=[bpb])
                        trail = pi - (npair - 4)
                        if trail >= 0:
                            mo = (j % 2) * 4096 + trail * 1024
                            cx.op(ve, lambda pb=pb, mo=mo: ve.tensor_mul(
                                pb, pb, cm[:, mo:mo + 1024]),
                                reads=[bpb, bcm], writes=[bpb])
                        fns = []
                        for half2 in range(2):
                            kb = pi * 2 + half2
                            fns.append(lambda kb=kb, half2=half2, pb=pb, pi=pi, j=j:
                                       te.matmul(ps_d[:], self.ones_bf[:, :],
                                                 pb[:, half2 * 512:(half2 + 1) * 512],
                                                 start=(pi == 0 and half2 == 0),
                                                 stop=(pi == npair - 1 and half2 == 1)))
                            fns.append(lambda kb=kb, half2=half2, pb=pb, pi=pi, u=u:
                                       te.matmul(ps_o[:], vv[u][:, kb * 128:(kb + 1) * 128],
                                                 pb[:, half2 * 512:(half2 + 1) * 512],
                                                 start=(pi == 0 and half2 == 0),
                                                 stop=(pi == npair - 1 and half2 == 1)))
                        cx.group(te, fns, reads=[bpb, bvv[u], self.bf("smalls")],
                                 writes=[bd, bo])
                    cx.op(ve, lambda: ve.reciprocal(dro[:, 0:512], ps_d[0:1, 0:512]),
                          reads=[bd], writes=[bdr])
                    cx.op(ve, lambda: ve.tensor_copy(drb[:, 0:512], dro[:, 0:512]),
                          reads=[bdr], writes=[bdr])
                    cx.op(te, lambda: te.matmul(ps_b[:], self.ones_bf[0:1, :],
                                                drb[:, 0:512], start=True, stop=True),
                          reads=[bdr, self.bf("smalls")], writes=[bb])
                    ib = j % 2
                    cx.op(ve, lambda ib=ib: ve.tensor_copy(bc[ib], ps_b[:]),
                          reads=[bb], writes=[bbc[ib]])
                    cx.op(ve, lambda ib=ib, h=h, j=j: ve.tensor_tensor(
                        ao[:, h * TOWN + j * 512: h * TOWN + (j + 1) * 512],
                        ps_o[:], bc[ib], op=ALU.mult),
                        reads=[bo, bbc[ib]], writes=[bao])
                    if "aout" in self.dbg:
                        cx.dma(sy, lambda h=h, j=j: sy.dma_start(
                            out=self.dbg["aout"][h * 128:(h + 1) * 128,
                                                 j * 512:(j + 1) * 512],
                            in_=ao[:, h * TOWN + j * 512: h * TOWN + (j + 1) * 512]),
                            reads=[bao], ch="d")

    def attn_wo_combine(self, ps4):
        nc, cx = self.nc, self.cx
        sy, sc, ve, te, gp = self.sy, self.sc, self.ve, self.te, self.gp
        ao = self.v(self.A2, 0, 128, NH * TOWN, BF16)
        bao = self.bf("at_ao")
        inc_bf = self.v(self.A1, 0, 128, NCH * TOWN, BF16)
        binc = self.bf("A1_acc")
        wf = [self.v(self.A3, i * 8192, 128, 2048, F32) for i in range(2)]
        bwf = [self.bf("at_wof%d" % i) for i in range(2)]
        wb = [self.v(self.A3, 16384 + i * 4096, 128, 2048, BF16) for i in range(2)]
        bwb = [self.bf("at_wob%d" % i) for i in range(2)]
        with (
            nc.psum_tensor("w_p0", [128, 512], F32) as pw0,
            nc.psum_tensor("w_p1", [128, 512], F32) as pw1,
        ):
            bpw = [self.bf("w_p0"), self.bf("w_p1")]
            pws = [pw0, pw1]
            for ds in range(NCH):
                u = ds % 2
                cx.dma(sy, lambda ds=ds, u=u: sy.dma_start(
                    out=wf[u], in_=self.i["wo"][ds * 128:(ds + 1) * 128, :]),
                    writes=[bwf[u]], ch="c")
                cx.op(gp, lambda u=u: gp.tensor_copy(wb[u][:, :], wf[u][:, :]),
                      reads=[bwf[u]], writes=[bwb[u]])
                for tcc in range(4):
                    ip = tcc % 2
                    fns = [(lambda hh=hh, u=u, tcc=tcc, ip=ip: te.matmul(
                        pws[ip][:], wb[u][:, hh * 128:(hh + 1) * 128],
                        ao[:, hh * TOWN + tcc * 512: hh * TOWN + tcc * 512 + 512],
                        start=(hh == 0), stop=(hh == NH - 1))) for hh in range(NH)]
                    cx.group(te, fns, reads=[bwb[u], bao], writes=[bpw[ip]])
                    cx.op(sc, lambda ds=ds, tcc=tcc, ip=ip: sc.activation(
                        inc_bf[:, ds * TOWN + tcc * 512: ds * TOWN + tcc * 512 + 512],
                        pws[ip][:], AF.Identity, bias=self.bias_col("bo", ds), scale=1.0),
                        reads=[bpw[ip], self.bf("smalls")], writes=[binc])
        self.cx.join([self.sy, self.sc, self.ve, self.te, self.gp])
        dbg_x = self.dbg.get("x2")
        for tt in range(16):
            self.tile_unit_span(tt, self.d_xsp[0], self.bf("dram_xsp0"),
                                self.d_xsp[1], self.bf("dram_xsp1"), inc_bf, binc,
                                TOWN, ps4, do_ln=True, dbg_x=dbg_x, dbg_ln=None)

    def attention(self):
        nc = self.nc
        with (
            nc.psum_tensor("ap_a", [128, 512], F32) as pa,
            nc.psum_tensor("ap_b", [128, 512], F32) as pb,
            nc.psum_tensor("ap_c", [128, 512], F32) as pc,
            nc.psum_tensor("ap_d", [128, 512], F32) as pd,
        ):
            ps4 = [(pa, self.bf("ap_a")), (pb, self.bf("ap_b")),
                   (pc, self.bf("ap_c")), (pd, self.bf("ap_d"))]
            for half in range(2):
                self.load_lnA(half, ps4)
                self.attn_proj(half, ps4, ps_q=[ps4[0], ps4[1]])
            self.cx.cc(self.gp, lambda: self.gp.collective_compute(
                "AllGather", ALU.bypass,
                replica_groups=[[0, 1], [2, 3], [4, 5], [6, 7]],
                ins=[self.d_kvin.ap()], outs=[self.d_kvout.ap()]),
                reads=[self.bf("dram_kvin")], writes=[self.bf("dram_kvout")])
        self.epoch()
        self.attn_inner()
        self.epoch()
        with (
            nc.psum_tensor("aq_a", [128, 512], F32) as pa,
            nc.psum_tensor("aq_b", [128, 512], F32) as pb,
            nc.psum_tensor("aq_c", [128, 512], F32) as pc,
            nc.psum_tensor("aq_d", [128, 512], F32) as pd,
        ):
            ps4 = [(pa, self.bf("aq_a")), (pb, self.bf("aq_b")),
                   (pc, self.bf("aq_c")), (pd, self.bf("aq_d"))]
            self.attn_wo_combine(ps4)

    def epoch(self):
        self.cx.join([self.sy, self.sc, self.ve, self.te, self.gp])

    def build(self):
        nc = self.nc
        self.setup()
        self.epoch()
        with (
            nc.psum_tensor("p_a", [128, 512], F32) as pa,
            nc.psum_tensor("p_b", [128, 512], F32) as pb,
            nc.psum_tensor("p_c", [128, 512], F32) as pc,
            nc.psum_tensor("p_d", [128, 512], F32) as pd,
        ):
            ps4 = [(pa, self.bf("pre_a")), (pb, self.bf("pre_b")),
                   (pc, self.bf("pre_c")), (pd, self.bf("pre_d"))]
            self.pre_phase(ps4)
        if self.dev.startswith("pre"):
            self.finish()
            return self.nc
        self.epoch()
        self.ffn_call(0, self.i["x"], None, self.d_xsp[0], self.bf("dram_xsp0"))
        if self.dev == "ffn1":
            self.finish()
            return self.nc
        self.epoch()
        self.attention()
        if self.dev == "attn":
            self.finish()
            return self.nc
        if self.dev == "fulld":
            pass
        self.epoch()
        self.ffn_call(1, self.d_xsp[1], self.bf("dram_xsp1"), self.o_out, None,
                      last=True)
        self.finish()
        return self.nc

    def finish(self):
        self.cx.join([self.sy])
        self.stk.close()


def kernel(**inputs):
    shared = _host_layouts(inputs)
    in_maps = [{**shared, **_percore(inputs, c)} for c in range(8)]
    b = B(DEV_UPTO)
    nc = b.build()
    res = run_bass_kernel_spmd(nc, in_maps, core_ids=list(range(8)))
    kernel.last_results = res
    out = np.empty((4, SEQ, D), dtype=np.float32)
    for c in range(8):
        bidx, par = c // 2, c % 2
        chunks = CHUNKS_EVEN if par == 0 else CHUNKS_ODD
        shard = res.results[c]["out"]
        for i, g in enumerate(chunks):
            out[bidx, g * 512:(g + 1) * 512, :] = shard[i * 512:(i + 1) * 512, :]
    return out


# version 02
# speedup vs baseline: 1.4380x; 1.4380x over previous
import os
import numpy as np
import ml_dtypes
from contextlib import ExitStack

import concourse.bass as bass
import concourse.mybir as mybir
from concourse.bass_utils import run_bass_kernel_spmd

F32 = mybir.dt.float32
BF16 = mybir.dt.bfloat16
AF = mybir.ActivationFunctionType
ALU = mybir.AluOpType
AXX = mybir.AxisListType.X

D = 2048
NCH = 16
NH = 16
DH = 128
NF = 8
NE = 16
NEX = 18
TOWN = 2048
HALF = 1024
SEQ = 4096
NSLOT = 4
EXT = [2, 4, 6, 8]
CHUNKS_EVEN = [0, 3, 4, 7]
CHUNKS_ODD = [1, 2, 5, 6]
CHUNK_OWNER = {g: (0, i) for i, g in enumerate(CHUNKS_EVEN)}
CHUNK_OWNER.update({g: (1, i) for i, g in enumerate(CHUNKS_ODD)})
GROUPS = [(2 * i, 2 * i + 1) for i in range(9)]

DEV_UPTO = "full"
SKIP = set(os.environ.get("KSKIP", "").split(",")) - {""}


def _host_layouts(inp):
    f32 = np.float32
    out = {}

    def fmw(W):
        return np.ascontiguousarray(
            np.asarray(W).reshape(NCH, 128, NCH, 128).transpose(2, 1, 0, 3).reshape(D, D)
        ).astype(f32, copy=False)

    out["wq"] = fmw(inp["L0_attn_Wq"])
    out["wk"] = fmw(inp["L0_attn_Wk"])
    out["wv"] = fmw(inp["L0_attn_Wv"])
    out["wo"] = fmw(inp["L0_attn_Wo"])

    g_all = np.concatenate([np.asarray(inp["L0_ffn_rg"]), np.asarray(inp["L0_ffn_sg"])], 0)
    w1_all = np.concatenate([np.asarray(inp["L0_ffn_rw1"]), np.asarray(inp["L0_ffn_sw1"])], 0)
    w2_all = np.concatenate([np.asarray(inp["L0_ffn_rw2"]), np.asarray(inp["L0_ffn_sw2"])], 0)

    def gw(g):
        return np.ascontiguousarray(
            g.reshape(NEX, NCH, 128, NF, 128).transpose(0, 3, 2, 1, 4).reshape(NEX * NF * 128, D)
        ).astype(f32, copy=False)

    out["g"] = gw(g_all)
    out["w1"] = gw(w1_all)
    out["w2"] = np.ascontiguousarray(
        w2_all.reshape(NEX, NF, 128, NCH, 128).transpose(3, 2, 0, 1, 4).reshape(D, NEX * NF * 128)
    ).astype(f32, copy=False)
    out["wa"] = np.ascontiguousarray(np.asarray(inp["L0_ffn_wa"])).astype(f32, copy=False)

    gb_all = np.concatenate([np.asarray(inp["L0_ffn_rgb"]), np.asarray(inp["L0_ffn_sgb"])], 0)
    b1_all = np.concatenate([np.asarray(inp["L0_ffn_rb1"]), np.asarray(inp["L0_ffn_sb1"])], 0)
    out["gb"] = np.ascontiguousarray(
        gb_all.reshape(NEX, NF, 128).transpose(2, 0, 1).reshape(128, NEX * NF)).astype(f32, copy=False)
    out["b1"] = np.ascontiguousarray(
        b1_all.reshape(NEX, NF, 128).transpose(2, 0, 1).reshape(128, NEX * NF)).astype(f32, copy=False)
    out["b2"] = np.ascontiguousarray(
        np.concatenate([np.asarray(inp["L0_ffn_rb2"]), np.asarray(inp["L0_ffn_sb2"])], 0)
    ).astype(f32, copy=False)
    for nm, key in [("bq", "L0_attn_bq"), ("bk", "L0_attn_bk"),
                    ("bv", "L0_attn_bv"), ("bo", "L0_attn_bo")]:
        out[nm] = np.ascontiguousarray(np.asarray(inp[key]).reshape(NCH, 128).T).astype(f32, copy=False)
    return out


def _percore(inp, core):
    f32 = np.float32
    b, par = core // 2, core % 2
    chunks = CHUNKS_EVEN if par == 0 else CHUNKS_ODD
    x = np.asarray(inp["L0_attn_x"])
    rows = np.concatenate([np.arange(g * 512, (g + 1) * 512) for g in chunks])
    xs = np.ascontiguousarray(x[b][rows]).astype(f32, copy=False)

    inv = (1.0 / 10000.0 ** (np.arange(0, DH, 2, dtype=f32) / DH)).astype(f32)
    th = np.einsum("t,i->it", rows.astype(f32), inv).astype(f32)
    cos = np.cos(np.concatenate([th, th], 0)).astype(f32)
    sin = np.sin(np.concatenate([th, th], 0)).astype(f32)
    sinf = np.concatenate([-sin[:64], sin[64:]], 0)
    s = np.float32(DH ** -0.5)

    def pattern(kind):
        m = np.zeros((128, 8 * 512), f32)
        for blk in range(8):
            ch, r = blk // 4, blk % 4
            col = blk * 512
            kk = np.arange(128)[:, None] + r * 128
            qq = np.arange(512)[None, :]
            if kind == 0:
                if ch == 0:
                    m[:, col:col + 512] = (qq >= kk)
            else:
                m[:, col:col + 512] = 1.0 if ch == 0 else (qq >= kk)
        return m

    sel = np.zeros((128, 2 * 8 * 512), f32)
    for jj in range(2):
        gq = chunks[jj]
        diag_at = gq - EXT[jj] + 2
        sel[:, jj * 4096:(jj + 1) * 4096] = pattern(1 if diag_at == 1 else 0)
    return {"x": xs, "cosq": np.ascontiguousarray(cos * s), "sinq": np.ascontiguousarray(sinf * s),
            "cosk": np.ascontiguousarray(cos), "sink": np.ascontiguousarray(sinf),
            "cmask": np.ascontiguousarray(sel.astype(ml_dtypes.bfloat16))}


class Sem:
    def __init__(self, nc, name, stack, is_dma=False):
        self.h = stack.enter_context(nc.semaphore(name))
        self.n = 0
        self.name = name
        self.is_dma = is_dma


class Buf:
    __slots__ = ("name", "w", "readers")

    def __init__(self, name):
        self.name = name
        self.w = None
        self.readers = []


class Ctx:

    def __init__(self, nc, stack):
        self.nc = nc
        self.eng_sems = {}
        self.waited = {}
        self.te_id = id(nc.tensor)
        for eng, nm in [(nc.tensor, "te"), (nc.scalar, "sc"), (nc.vector, "ve"),
                        (nc.gpsimd, "gp")]:
            self.eng_sems[id(eng)] = Sem(nc, "e_" + nm, stack)
        self.dma_sems = {nm: Sem(nc, "d_" + nm, stack, is_dma=True) for nm in
                         ["a", "b", "c", "d", "e", "f", "g", "h"]}
        self.cc_sem = Sem(nc, "cc", stack, is_dma=True)

    def _wait(self, eng, semobj, cnt):
        if cnt <= 0:
            return
        if id(eng) == self.te_id and self.eng_sems.get(self.te_id) is semobj:
            return
        if semobj.is_dma:
            cnt = semobj.n
        key = (id(eng), id(semobj))
        if self.waited.get(key, 0) >= cnt:
            return
        eng.wait_ge(semobj.h, cnt)
        self.waited[key] = cnt

    def _pre(self, eng, reads, writes, is_dma):
        need = {}
        for b in reads:
            if b.w:
                s, c = b.w
                need[id(s)] = (s, max(c, need.get(id(s), (s, 0))[1]))
        for b in writes:
            if b.w:
                s, c = b.w
                need[id(s)] = (s, max(c, need.get(id(s), (s, 0))[1]))
            for s, c in b.readers:
                need[id(s)] = (s, max(c, need.get(id(s), (s, 0))[1]))
        for s, c in need.values():
            self._wait(eng, s, c)

    def _post(self, inst, sem, reads, writes, k):
        inst.then_inc(sem.h, k)
        sem.n += k
        for b in writes:
            b.w = (sem, sem.n)
            b.readers = []
        for b in reads:
            b.readers.append((sem, sem.n))
        return sem.n

    def op(self, eng, fn, reads=(), writes=()):
        self._pre(eng, reads, writes, False)
        return self._post(fn(), self.eng_sems[id(eng)], reads, writes, 1)

    def group(self, eng, fns, reads=(), writes=()):
        self._pre(eng, reads, writes, False)
        inst = None
        for f in fns:
            inst = f()
        return self._post(inst, self.eng_sems[id(eng)], reads, writes, 1)

    def dma(self, q, fn, reads=(), writes=(), ch="a"):
        sem = self.dma_sems[ch]
        self._wait(q, sem, sem.n)
        self._pre(q, reads, writes, True)
        return self._post(fn(), sem, reads, writes, 16)

    def cc(self, q, fn, reads=(), writes=()):
        self._pre(q, reads, writes, True)
        return self._post(fn(), self.cc_sem, reads, writes, 1)

    def join(self, engines):
        for e in engines:
            for s in list(self.eng_sems.values()) + list(self.dma_sems.values()) + [self.cc_sem]:
                self._wait(e, s, s.n)


class B:
    def __init__(self, dev="full"):
        self.dev = dev
        nc = bass.Bass("TRN2", target_bir_lowering=False, num_devices=8)
        self.nc = nc
        self.stk = ExitStack()
        self.sy, self.sc, self.ve, self.te, self.gp = nc.sync, nc.scalar, nc.vector, nc.tensor, nc.gpsimd
        self.cx = Ctx(nc, self.stk)

        dt = nc.dram_tensor
        A = lambda name, shape, d=F32, kind="ExternalInput": dt(name, shape, d, kind=kind).ap()
        self.i = {
            "x": A("x", [TOWN, D]),
            "wq": A("wq", [D, D]), "wk": A("wk", [D, D]), "wv": A("wv", [D, D]),
            "wo": A("wo", [D, D]),
            "g": A("g", [NEX * NF * 128, D]), "w1": A("w1", [NEX * NF * 128, D]),
            "w2": A("w2", [D, NEX * NF * 128]),
            "wa": A("wa", [D, NE]),
            "gb": A("gb", [128, NEX * NF]), "b1": A("b1", [128, NEX * NF]),
            "b2": A("b2", [NEX, D]),
            "bq": A("bq", [128, NCH]), "bk": A("bk", [128, NCH]),
            "bv": A("bv", [128, NCH]), "bo": A("bo", [128, NCH]),
            "cosq": A("cosq", [128, TOWN]), "sinq": A("sinq", [128, TOWN]),
            "cosk": A("cosk", [128, TOWN]), "sink": A("sink", [128, TOWN]),
            "cmask": A("cmask", [128, 2 * 8 * 512], BF16),
        }
        self.o_out = A("out", [TOWN, D], F32, "ExternalOutput")
        self.d_xsp = [dt("xsp0", [TOWN, D], F32).ap(), dt("xsp1", [TOWN, D], F32).ap()]
        self.d_ln = dt("lnsp", [TOWN, D], BF16).ap()
        self.d_kvin = dt("kvin", [2 * D, TOWN], BF16)
        self.d_kvout = dt("kvout", [4 * D, TOWN], BF16)

        self.dbg = {}
        if self.dev != "full":
            for nm, shape, d in [("ln1", [TOWN, D], BF16), ("gates", [NEX, TOWN], BF16),
                                 ("x1", [TOWN, D], F32), ("q", [D, TOWN], BF16),
                                 ("aout", [D, TOWN], BF16), ("x2", [TOWN, D], F32),
                                 ("ln2", [TOWN, D], BF16)]:
                self.dbg[nm] = A("dbg_" + nm, shape, d, "ExternalOutput")

        sb = self.stk.enter_context
        U8 = mybir.dt.uint8
        self.A1 = sb(nc.sbuf_tensor("A1", [128, 65536], U8))
        self.A2 = sb(nc.sbuf_tensor("A2", [128, 65536], U8))
        self.A3 = sb(nc.sbuf_tensor("A3", [128, 65536], U8))
        self.SM = sb(nc.sbuf_tensor("SM", [128, 8704], U8))
        self.bufs = {}

        self.bias = self.v(self.SM, 0, 128, 352, F32)
        self.b2e = self.v(self.SM, 1408, NEX, 2048, BF16)
        self.gate_b = self.v(self.SM, 5504, NEX, 1024, BF16)
        self.ones_bf = self.v(self.SM, 7552, 128, 128, BF16)
        self.ident_bf = self.v(self.SM, 7808, 128, 128, BF16)
        self.col = [self.v(self.SM, 8064 + 4 * i, 128, 1, F32) for i in range(4)]
        self.wa_bf = self.v(self.SM, 8192, 128, 256, BF16)

    def bf(self, name):
        if name not in self.bufs:
            self.bufs[name] = Buf(name)
        return self.bufs[name]

    def v(self, arena, off, parts, cols, dtype):
        isz = {F32: 4, BF16: 2}[dtype]
        return arena[0:parts, off:off + cols * isz].bitcast(dtype)

    def bias_col(self, which, idx):
        offs = {"bq": 0, "bk": NCH, "bv": 2 * NCH, "bo": 3 * NCH,
                "gb": 4 * NCH, "b1": 4 * NCH + NEX * NF}
        o = offs[which] + idx
        return self.bias[:, o:o + 1]

    RG = {
        "gwf0": 0, "gwf1": 8192,
        "w2f0": 16384, "w2f1": 20480,
        "gwb0": 24576, "gwb1": 28672,
        "w2b0": 32768, "w2b1": 34816,
        "gbc0": 36864, "gbc1": 38912,
        "ffa0": 40960, "ffa1": 41984,
        "rt": 43008,
        "ct_x0": 45056, "ct_x1": 53248,
        "ct_xb": 61440,
    }

    def reg(self, name, parts, cols, dtype):
        return self.v(self.A2, self.RG[name], parts, cols, dtype), self.bf("A2_" + name)

    def setup(self):
        cx, sy, gp, ve = self.cx, self.sy, self.gp, self.ve
        bS = self.bf("smalls")
        cx.op(gp, lambda: gp.memset(self.ident_bf[:], 1.0), writes=[bS])
        cx.op(gp, lambda: gp.affine_select(self.ident_bf[:], self.ident_bf[:],
                                           pattern=[[1, 128]], compare_op=ALU.is_equal,
                                           fill=0.0, base=0, channel_multiplier=-1),
              reads=[bS], writes=[bS])
        cx.op(ve, lambda: ve.memset(self.ones_bf[:], 1.0), writes=[bS])
        o = 0
        for nm in ["bq", "bk", "bv", "bo", "gb", "b1"]:
            src = self.i[nm]
            w = NCH if nm in ("bq", "bk", "bv", "bo") else NEX * NF
            cx.dma(sy, lambda src=src, o=o, w=w: sy.dma_start(out=self.bias[:, o:o + w], in_=src),
                   writes=[bS], ch="a")
            o += w
        b2f = self.v(self.A3, 0, NEX, 2048, F32)
        bT = self.bf("A3_setup")
        cx.dma(sy, lambda: sy.dma_start(out=b2f, in_=self.i["b2"]), writes=[bT], ch="a")
        cx.op(ve, lambda: ve.tensor_scalar(self.b2e[:, :], b2f, 0.5, None, op0=ALU.mult),
              reads=[bT], writes=[bS])
        rt = self.RG["rt"]
        self.r_abf = self.A2[0:1, rt + 512: rt + 1536].bitcast(BF16)
        wa_f = self.v(self.A3, 16384, 128, 256, F32)
        cx.dma(sy, lambda: sy.dma_start(
            out=wa_f.rearrange("p (c e) -> p c e", c=NCH),
            in_=self.i["wa"].rearrange("(c p) e -> p c e", p=128)), writes=[bT], ch="a")
        cx.op(ve, lambda: ve.tensor_copy(self.wa_bf, wa_f), reads=[bT], writes=[self.bf("wa_bf")])

    def tile_unit(self, tt, x_src, bxsrc, x_dst, bxdst, inc_bf, binc, psums,
                  do_ln, dbg_x=None, dbg_ln=None):
        return self.tile_unit_span(tt, x_src, bxsrc, x_dst, bxdst, inc_bf, binc,
                                   HALF, psums, do_ln, dbg_x, dbg_ln)

    def tile_unit_span(self, tt, x_src, bxsrc, x_dst, bxdst, inc_bf, binc, span,
                       psums, do_ln, dbg_x=None, dbg_ln=None):
        cx = self.cx
        sy, sc, ve, te = self.sy, self.sc, self.ve, self.te
        xt, bxt = self.reg("ct_x%d" % (tt % 2), 128, 2048, F32)
        xb, bxb = self.reg("ct_xb", 128, 2048, BF16)
        sq0, bsq0 = self.reg("ffa0", 128, 512, BF16)
        sq1, bsq1 = self.reg("ffa1", 128, 512, BF16)
        sqs = [(sq0, bsq0), (sq1, bsq1)]
        tloc = tt % (span // 128)
        cx.dma(sy, lambda: sy.dma_start(out=xt, in_=x_src[tt * 128:(tt + 1) * 128, :]),
               reads=[bxsrc] if bxsrc else [], writes=[bxt], ch="g")
        if inc_bf is not None:
            for b in range(4):
                fns = []
                for j in range(4):
                    ds = 4 * b + j
                    fns.append(lambda ds=ds, b=b, j=j: te.matmul(
                        psums[b][0][:, j * 128:(j + 1) * 128],
                        inc_bf[:, ds * span + tloc * 128: ds * span + tloc * 128 + 128],
                        self.ident_bf[:, :], start=True, stop=True))
                cx.group(te, fns, reads=[binc, self.bf("smalls")], writes=[psums[b][1]])
            for b in range(4):
                cx.op(ve, lambda b=b: ve.tensor_tensor(
                    xt[:, b * 512:(b + 1) * 512], xt[:, b * 512:(b + 1) * 512],
                    psums[b][0][:, 0:512], op=ALU.add),
                    reads=[psums[b][1], bxt], writes=[bxt])
        if x_dst is not None:
            cx.dma(sy, lambda: sy.dma_start(out=x_dst[tt * 128:(tt + 1) * 128, :], in_=xt),
                   reads=[bxt], writes=[bxdst] if bxdst else [], ch="c")
        if dbg_x is not None:
            cx.dma(sy, lambda: sy.dma_start(out=dbg_x[tt * 128:(tt + 1) * 128, :], in_=xt),
                   reads=[bxt], ch="d")
        if not do_ln:
            return
        cx.op(ve, lambda: ve.tensor_copy(xb, xt), reads=[bxt], writes=[bxb])
        c0, c1, c2, c3 = self.col
        bR = self.bf("cols")
        for ch in range(4):
            sq, bsq = sqs[ch % 2]
            cx.op(sc, lambda ch=ch, sq=sq: sc.activation(
                sq, xb[:, ch * 512:(ch + 1) * 512], AF.Square), reads=[bxb], writes=[bsq])
            if ch == 0:
                cx.op(ve, lambda: ve.reduce_sum(c0, xb[:, 0:512], axis=AXX),
                      reads=[bxb], writes=[bR])
                cx.op(ve, lambda sq=sq: ve.reduce_sum(c1, sq, axis=AXX),
                      reads=[bsq], writes=[bR])
            else:
                cx.op(ve, lambda ch=ch: ve.reduce_sum(c2, xb[:, ch * 512:(ch + 1) * 512],
                                                      axis=AXX), reads=[bxb], writes=[bR])
                cx.op(ve, lambda: ve.tensor_add(c0, c0, c2), reads=[bR], writes=[bR])
                cx.op(ve, lambda sq=sq: ve.reduce_sum(c3, sq, axis=AXX),
                      reads=[bsq], writes=[bR])
                cx.op(ve, lambda: ve.tensor_add(c1, c1, c3), reads=[bR], writes=[bR])
        cx.op(ve, lambda: ve.tensor_scalar(c0, c0, 1.0 / D, None, op0=ALU.mult),
              reads=[bR], writes=[bR])
        cx.op(ve, lambda: ve.tensor_scalar(c1, c1, 1.0 / D, None, op0=ALU.mult),
              reads=[bR], writes=[bR])
        cx.op(ve, lambda: ve.tensor_mul(c2, c0, c0), reads=[bR], writes=[bR])
        cx.op(ve, lambda: ve.tensor_sub(c1, c1, c2), reads=[bR], writes=[bR])
        cx.op(ve, lambda: ve.tensor_scalar(c1, c1, 1e-5, None, op0=ALU.add),
              reads=[bR], writes=[bR])
        cx.op(sc, lambda: sc.activation(c1, c1, AF.Sqrt), reads=[bR], writes=[bR])
        cx.op(ve, lambda: ve.reciprocal(c1, c1), reads=[bR], writes=[bR])
        cx.op(ve, lambda: ve.tensor_mul(c0, c0, c1), reads=[bR], writes=[bR])
        cx.op(ve, lambda: ve.tensor_scalar(c0, c0, -1.0, None, op0=ALU.mult),
              reads=[bR], writes=[bR])
        cx.op(ve, lambda: ve.tensor_scalar(xb, xb, c1, c0, op0=ALU.mult, op1=ALU.add),
              reads=[bR, bxb], writes=[bxb])
        cx.dma(sy, lambda: sy.dma_start(out=self.d_ln[tt * 128:(tt + 1) * 128, :], in_=xb),
               reads=[bxb], writes=[self.bf("dram_ln")], ch="f")
        if dbg_ln is not None:
            cx.dma(sy, lambda: sy.dma_start(out=dbg_ln[tt * 128:(tt + 1) * 128, :], in_=xb),
                   reads=[bxb], ch="d")

    def pre_phase(self, psums):
        for tt in range(16):
            self.tile_unit(tt, self.i["x"], None, None, None, None, None, psums,
                           do_ln=True, dbg_ln=self.dbg.get("ln1"))

    def lnA_ap(self):
        return self.v(self.A3, 0, 128, NCH * HALF, BF16), self.bf("A3_lnA")

    def load_lnA(self, half, psums):
        cx, sy, te, ve = self.cx, self.sy, self.te, self.ve
        lnA, blnA = self.lnA_ap()
        bln = self.bf("dram_ln")
        for i in range(8):
            tt = half * 8 + i
            st, bst = self.reg("ct_x%d" % (i % 2), 128, 4096, BF16)
            cx.dma(sy, lambda tt=tt, st=st: sy.dma_start(
                out=st[:, 0:2048], in_=self.d_ln[tt * 128:(tt + 1) * 128, :]),
                reads=[bln], writes=[bst], ch="g")
            for b in range(4):
                fns = []
                for j in range(4):
                    ds = 4 * b + j
                    fns.append(lambda ds=ds, b=b, j=j, st=st: te.matmul(
                        psums[b][0][:, j * 128:(j + 1) * 128],
                        st[:, ds * 128:(ds + 1) * 128],
                        self.ident_bf[:, :], start=True, stop=True))
                cx.group(te, fns, reads=[bst, self.bf("smalls")], writes=[psums[b][1]])
            for b in range(4):
                for j in range(4):
                    ds = 4 * b + j
                    cx.op(ve, lambda ds=ds, b=b, j=j, i=i: ve.tensor_copy(
                        lnA[:, ds * HALF + i * 128: ds * HALF + i * 128 + 128],
                        psums[b][0][:, j * 128:(j + 1) * 128]),
                        reads=[psums[b][1]], writes=[blnA])

    def router(self, half, ps_y0, ps_y1):
        nc, cx = self.nc, self.cx
        sy, sc, ve, te = self.sy, self.sc, self.ve, self.te
        lnA, blnA = self.lnA_ap()
        rt = self.RG["rt"]
        aff = self.v(self.A2, rt, 128, 18, F32)
        eq1 = self.v(self.A2, rt + 80, 128, 18, F32)
        eq2 = self.v(self.A2, rt + 160, 128, 18, F32)
        G = self.v(self.A2, rt + 240, 128, 32, BF16)
        Gf = self.v(self.A2, rt + 368, 128, 18, F32)
        m1 = self.v(self.A2, rt + 304, 128, 1, F32)
        m2 = self.v(self.A2, rt + 312, 128, 1, F32)
        p1 = self.v(self.A2, rt + 320, 128, 1, F32)
        p2 = self.v(self.A2, rt + 328, 128, 1, F32)
        ss = self.v(self.A2, rt + 336, 128, 1, F32)
        brt = self.bf("A2_rt")
        bgb = self.bf("gate_b")
        ps_r = ps_y0[:, 0:16]
        ps_g = ps_y1[0:NEX, 0:128]
        bpr, bpg = self.bf("ps_y0"), self.bf("ps_y1")
        for tb in range(8):
            cx.group(te, [(lambda c=c, tb=tb: te.matmul(
                ps_r, lnA[:, c * HALF + tb * 128: c * HALF + tb * 128 + 128],
                self.wa_bf[:, c * 16:(c + 1) * 16],
                start=(c == 0), stop=(c == NCH - 1))) for c in range(NCH)],
                reads=[blnA, self.bf("wa_bf")], writes=[bpr])
            cx.op(sc, lambda: sc.activation(aff[:, 0:16], ps_r, AF.Sigmoid),
                  reads=[bpr], writes=[brt])
            for f in [
                lambda: ve.reduce_max(m1, aff[:, 0:16], axis=AXX),
                lambda: ve.tensor_scalar(eq1[:, 0:16], aff[:, 0:16], m1, None, op0=ALU.is_equal),
                lambda: ve.scalar_tensor_tensor(eq2[:, 0:16], eq1[:, 0:16], -1e9, aff[:, 0:16],
                                                op0=ALU.mult, op1=ALU.add),
                lambda: ve.reduce_max(m2, eq2[:, 0:16], axis=AXX),
                lambda: ve.tensor_add(ss, m1, m2),
                lambda: ve.reciprocal(ss, ss),
                lambda: ve.tensor_mul(p1, m1, ss),
                lambda: ve.tensor_mul(p2, m2, ss),
                lambda: ve.tensor_scalar(eq2[:, 0:16], eq2[:, 0:16], m2, None, op0=ALU.is_equal),
                lambda: ve.tensor_scalar(Gf[:, 0:16], eq1[:, 0:16], p1, None, op0=ALU.mult),
                lambda: ve.scalar_tensor_tensor(G[:, 0:16], eq2[:, 0:16], p2, Gf[:, 0:16],
                                                op0=ALU.mult, op1=ALU.add),
                lambda: ve.memset(G[:, 16:18], 1.0),
            ]:
                cx.op(ve, f, reads=[brt], writes=[brt])
            cx.op(te, lambda: te.matmul(ps_g, G[:, 0:18], self.ident_bf[:, :],
                                        start=True, stop=True),
                  reads=[brt, self.bf("smalls")], writes=[bpg])
            cx.op(ve, lambda tb=tb: ve.tensor_copy(
                self.gate_b[:, tb * 128:(tb + 1) * 128], ps_g),
                reads=[bpg], writes=[bgb])
            if "gates" in self.dbg:
                cx.dma(sy, lambda tb=tb, half=half: sy.dma_start(
                    out=self.dbg["gates"][0:NEX, half * HALF + tb * 128: half * HALF + tb * 128 + 128],
                    in_=self.gate_b[:, tb * 128:(tb + 1) * 128]), reads=[bgb], ch="d")

    def acc_ap(self):
        return self.v(self.A1, 0, 128, NCH * HALF, F32), self.bf("A1_acc")

    def acc_init(self, half, ps0, b0, ps1, b1):
        cx, ve, te = self.cx, self.ve, self.te
        lnA, blnA = self.lnA_ap()
        acc, bacc = self.acc_ap()
        bp = [b0, b1]
        for ds in range(NCH):
            for tcc in range(2):
                ip = (ds * 2 + tcc) % 2
                ps = ps0 if ip == 0 else ps1
                cx.op(te, lambda ds=ds, tcc=tcc, ps=ps: te.matmul(
                    ps[:], self.b2e[:, ds * 128:(ds + 1) * 128],
                    self.gate_b[:, tcc * 512:(tcc + 1) * 512], start=True, stop=True),
                    reads=[self.bf("smalls"), self.bf("gate_b")], writes=[bp[ip]])
                cx.op(ve, lambda ds=ds, tcc=tcc, ps=ps: ve.scalar_tensor_tensor(
                    acc[:, ds * HALF + tcc * 512: ds * HALF + tcc * 512 + 512],
                    lnA[:, ds * HALF + tcc * 512: ds * HALF + tcc * 512 + 512],
                    0.5, ps[:], op0=ALU.mult, op1=ALU.add),
                    reads=[bp[ip], blnA], writes=[bacc])

    def expert_group(self, gi, psums):
        (ps_G0, ps_G1, ps_W0, ps_W1, ps_y0, ps_y1) = psums
        cx = self.cx
        sy, sc, ve, te, gp = self.sy, self.sc, self.ve, self.te, self.gp
        e0, e1 = GROUPS[gi]
        routed = e0 < NE
        lnA, blnA = self.lnA_ap()
        acc, bacc = self.acc_ap()
        yh = self.v(self.A3, 32768, 128, 16 * HALF, BF16)
        byh = self.bf("A3_yh")
        gwf, bgwf, gwb, bgwb, w2f, bw2f, w2b, bw2b, gbc, bgbc, ffa, bffa = \
            [], [], [], [], [], [], [], [], [], [], [], []
        for u in range(2):
            a, b_ = self.reg(f"gwf{u}", 128, 2048, F32); gwf.append(a); bgwf.append(b_)
            a, b_ = self.reg(f"gwb{u}", 128, 2048, BF16); gwb.append(a); bgwb.append(b_)
            a, b_ = self.reg(f"w2f{u}", 128, 1024, F32); w2f.append(a); bw2f.append(b_)
            a, b_ = self.reg(f"w2b{u}", 128, 1024, BF16); w2b.append(a); bw2b.append(b_)
            a, b_ = self.reg(f"gbc{u}", 128, 1024, BF16); gbc.append(a); bgbc.append(b_)
            a, b_ = self.reg(f"ffa{u}", 128, 512, BF16); ffa.append(a); bffa.append(b_)
        bG = [self.bf("ps_G0"), self.bf("ps_G1")]
        bW = [self.bf("ps_W0"), self.bf("ps_W1")]
        bY = [self.bf("ps_y0"), self.bf("ps_y1")]
        if routed:
            for k, e in enumerate((e0, e1)):
                for tcc in range(2):
                    row = self.r_abf
                    cx.dma(sy, lambda e=e, tcc=tcc: sy.dma_start(
                        out=row[:, 0:512],
                        in_=self.gate_b[e:e + 1, tcc * 512:(tcc + 1) * 512]),
                        reads=[self.bf("gate_b")], writes=[self.bf("rows")], ch="b")
                    cx.op(te, lambda: te.matmul(ps_y0[:], self.ones_bf[0:1, :], row[:, 0:512],
                                                start=True, stop=True),
                          reads=[self.bf("rows"), self.bf("smalls")], writes=[bY[0]])
                    cx.op(ve, lambda k=k, tcc=tcc: ve.tensor_copy(
                        gbc[k][:, tcc * 512:(tcc + 1) * 512], ps_y0[:]),
                        reads=[bY[0]], writes=[bgbc[k]])
        for si in range(16):
            ei = (e0, e1)[si // 8]
            fs = si % 8
            row0 = (ei * NF + fs) * 128
            for dh2 in range(2):
                u = dh2
                cx.dma(sy, lambda row0=row0, dh2=dh2, u=u: sy.dma_start(
                    out=gwf[u][:, 0:1024],
                    in_=self.i["g"][row0:row0 + 128, dh2 * 1024:(dh2 + 1) * 1024]),
                    writes=[bgwf[u]], ch="c")
                cx.dma(sy, lambda row0=row0, dh2=dh2, u=u: sy.dma_start(
                    out=gwf[u][:, 1024:2048],
                    in_=self.i["w1"][row0:row0 + 128, dh2 * 1024:(dh2 + 1) * 1024]),
                    writes=[bgwf[u]], ch="d")
                cx.op(sc, lambda u=u: sc.activation(gwb[u][:, :], gwf[u][:, :], AF.Identity),
                      reads=[bgwf[u]], writes=[bgwb[u]])
                fns = []
                for tcc in range(2):
                    for cc in range(8):
                        c = dh2 * 8 + cc
                        fns.append(lambda cc=cc, c=c, u=u, tcc=tcc: te.matmul(
                            (ps_G0 if tcc == 0 else ps_G1)[:],
                            gwb[u][:, cc * 128:(cc + 1) * 128],
                            lnA[:, c * HALF + tcc * 512: c * HALF + tcc * 512 + 512],
                            start=(c == 0), stop=(c == NCH - 1)))
                        fns.append(lambda cc=cc, c=c, u=u, tcc=tcc: te.matmul(
                            (ps_W0 if tcc == 0 else ps_W1)[:],
                            gwb[u][:, 1024 + cc * 128: 1024 + (cc + 1) * 128],
                            lnA[:, c * HALF + tcc * 512: c * HALF + tcc * 512 + 512],
                            start=(c == 0), stop=(c == NCH - 1)))
                cx.group(te, fns, reads=[bgwb[u], blnA],
                         writes=[bG[0], bG[1], bW[0], bW[1]])
            for tcc in range(2):
                psG = ps_G0 if tcc == 0 else ps_G1
                psW = ps_W0 if tcc == 0 else ps_W1
                ysl = yh[:, si * HALF + tcc * 512: si * HALF + tcc * 512 + 512]
                cx.op(sc, lambda psG=psG, ysl=ysl, ei=ei, fs=fs: sc.activation(
                    ysl, psG[:], AF.Gelu, bias=self.bias_col("gb", ei * NF + fs), scale=1.0),
                    reads=[bG[tcc], self.bf("smalls")], writes=[byh])
                cx.op(ve, lambda psW=psW, tcc=tcc, ei=ei, fs=fs: ve.tensor_scalar(
                    ffa[tcc][:, :], psW[:], self.bias_col("b1", ei * NF + fs), None, op0=ALU.add),
                    reads=[bW[tcc], self.bf("smalls")], writes=[bffa[tcc]])
                if routed:
                    k = si // 8
                    cx.op(ve, lambda tcc=tcc, k=k: ve.tensor_tensor(
                        ffa[tcc][:, :], ffa[tcc][:, :],
                        gbc[k][:, tcc * 512:(tcc + 1) * 512], op=ALU.mult),
                        reads=[bffa[tcc], bgbc[k]], writes=[bffa[tcc]])
                cx.op(ve, lambda tcc=tcc, ysl=ysl: ve.tensor_tensor(
                    ysl, ysl, ffa[tcc][:, :], op=ALU.mult),
                    reads=[bffa[tcc], byh], writes=[byh])
        col0 = e0 * NF * 128
        for ds in range(NCH):
            for w2h in range(2):
                u = w2h
                cx.dma(sy, lambda ds=ds, w2h=w2h, u=u, col0=col0: sy.dma_start(
                    out=w2f[u][:, :],
                    in_=self.i["w2"][ds * 128:(ds + 1) * 128,
                                     col0 + w2h * 1024: col0 + (w2h + 1) * 1024]),
                    writes=[bw2f[u]], ch="e")
                cx.op(sc, lambda u=u: sc.activation(w2b[u][:, :], w2f[u][:, :], AF.Identity,
                                                    scale=0.5),
                      reads=[bw2f[u]], writes=[bw2b[u]])
                fns = []
                for tcc in range(2):
                    for blk in range(8):
                        gblk = w2h * 8 + blk
                        fns.append(lambda blk=blk, gblk=gblk, u=u, tcc=tcc, ds=ds: te.matmul(
                            (ps_y0 if tcc == 0 else ps_y1)[:],
                            w2b[u][:, blk * 128:(blk + 1) * 128],
                            yh[:, gblk * HALF + tcc * 512: gblk * HALF + tcc * 512 + 512],
                            start=(gblk == 0), stop=(gblk == 15)))
                cx.group(te, fns, reads=[bw2b[u], byh], writes=[bY[0], bY[1]])
            for tcc in range(2):
                psY = ps_y0 if tcc == 0 else ps_y1
                asl = acc[:, ds * HALF + tcc * 512: ds * HALF + tcc * 512 + 512]
                cx.op(ve, lambda psY=psY, asl=asl: ve.tensor_tensor(asl, asl, psY[:], op=ALU.add),
                      reads=[bY[tcc]], writes=[bacc])

    def combine(self, half, call, xsp_in, bspin, xsp_out, bspout, psums,
                last=False, dbg_key=None):
        cx, ve = self.cx, self.ve
        acc, bacc = self.acc_ap()
        inc_bf = self.v(self.A3, 32768, 128, NCH * HALF, BF16)
        binc = self.bf("A3_yh")
        for q in range(4):
            cx.op(ve, lambda q=q: ve.tensor_copy(
                inc_bf[:, q * 4 * HALF:(q + 1) * 4 * HALF],
                acc[:, q * 4 * HALF:(q + 1) * 4 * HALF]), reads=[bacc], writes=[binc])
        dbg_x = self.dbg.get(dbg_key) if dbg_key else None
        dbg_ln = self.dbg.get("ln2") if (call == 0 and not last) else None
        for i in range(8):
            tt = half * 8 + i
            self.tile_unit(tt, xsp_in, bspin, xsp_out, bspout, inc_bf, binc,
                           psums, do_ln=(not last), dbg_x=dbg_x, dbg_ln=dbg_ln)

    def ffn_call(self, call, xsp_in, bspin, xsp_out, bspout, last=False):
        nc = self.nc
        with (
            nc.psum_tensor("f%d_G0" % call, [128, 512], F32) as ps_G0,
            nc.psum_tensor("f%d_G1" % call, [128, 512], F32) as ps_G1,
            nc.psum_tensor("f%d_W0" % call, [128, 512], F32) as ps_W0,
            nc.psum_tensor("f%d_W1" % call, [128, 512], F32) as ps_W1,
            nc.psum_tensor("f%d_y0" % call, [128, 512], F32) as ps_y0,
            nc.psum_tensor("f%d_y1" % call, [128, 512], F32) as ps_y1,
        ):
            psums = (ps_G0, ps_G1, ps_W0, ps_W1, ps_y0, ps_y1)
            ps4 = [(ps_G0, self.bf("ps_G0")), (ps_G1, self.bf("ps_G1")),
                   (ps_W0, self.bf("ps_W0")), (ps_W1, self.bf("ps_W1"))]
            for half in range(2):
                self.load_lnA(half, ps4)
                self.router(half, ps_y0, ps_y1)
                self.acc_init(half, ps_y0, self.bf("ps_y0"), ps_y1, self.bf("ps_y1"))
                for gi in range(9):
                    self.expert_group(gi, psums)
                self.combine(half, call, xsp_in, bspin, xsp_out, bspout, ps4,
                             last=last, dbg_key=("x1" if call == 0 else None))


    def qrot_ap(self):
        return self.v(self.A1, 0, 128, NH * TOWN, BF16), self.bf("A1_acc")

    def attn_proj(self, half, ps4, ps_q):
        nc, cx = self.nc, self.cx
        sy, sc, ve, te, gp = self.sy, self.sc, self.ve, self.te, self.gp
        lnA, blnA = self.lnA_ap()
        qrot, bqr = self.qrot_ap()
        wf = [self.v(self.A2, i * 8192, 128, 2048, F32) for i in range(2)]
        bwf = [self.bf("A2_gwf%d" % i) for i in range(2)]
        wb = [self.v(self.A2, 16384 + i * 4096, 128, 2048, BF16) for i in range(2)]
        bwb = [self.bf("at_wb%d" % i) for i in range(2)]
        tabs = {}
        for ti, nm in enumerate(["cosq", "sinq", "cosk", "sink"]):
            tabs[nm] = (self.v(self.A2, 24576 + ti * 4096, 128, 1024, F32),
                        self.bf("at_" + nm))
        qraw = self.v(self.A2, 40960, 128, 1024, F32)
        bqraw = self.bf("at_qraw")
        qsw = self.v(self.A2, 45056, 128, 1024, F32)
        bqsw = self.bf("at_qsw")
        u1 = self.v(self.A2, 49152, 128, 1024, F32)
        bu1 = self.bf("at_u1")
        kst = self.v(self.A2, 53248, 128, 1024, BF16)
        bkst = self.bf("at_kst")
        vst = self.v(self.A2, 55296, 128, 1024, BF16)
        bvst = self.bf("at_vst")
        vt_sb = self.v(self.A3, 32768, 128, 8 * 2048, BF16)
        bvt = self.bf("A3_yh")
        bkvin = self.bf("dram_kvin")
        kv = self.d_kvin.ap()
        for nm in ["cosq", "sinq", "cosk", "sink"]:
            cx.dma(sy, lambda nm=nm: sy.dma_start(
                out=tabs[nm][0], in_=self.i[nm][:, half * HALF:(half + 1) * HALF]),
                writes=[tabs[nm][1]], ch="b")
        for j in range(NH):
            for ti, (wnm, bias_nm) in enumerate([("wq", "bq"), ("wk", "bk"), ("wv", "bv")]):
                u = (j * 3 + ti) % 2
                cx.dma(sy, lambda wnm=wnm, j=j, u=u: sy.dma_start(
                    out=wf[u], in_=self.i[wnm][j * 128:(j + 1) * 128, :]),
                    writes=[bwf[u]], ch="c" if ti != 1 else "d")
                cx.op(sc, lambda u=u: sc.activation(wb[u][:, :], wf[u][:, :], AF.Identity),
                      reads=[bwf[u]], writes=[bwb[u]])
                for tcc in range(2):
                    ip = tcc % 2
                    ps = ps_q[ip]
                    fns = [(lambda c=c, u=u, tcc=tcc, ps=ps: te.matmul(
                        ps[0][:], wb[u][:, c * 128:(c + 1) * 128],
                        lnA[:, c * HALF + tcc * 512: c * HALF + tcc * 512 + 512],
                        start=(c == 0), stop=(c == NCH - 1))) for c in range(NCH)]
                    cx.group(te, fns, reads=[bwb[u], blnA], writes=[ps[1]])
                    if ti == 2:
                        cx.op(sc, lambda ps=ps, tcc=tcc, j=j: sc.activation(
                            vst[:, tcc * 512:(tcc + 1) * 512], ps[0][:], AF.Identity,
                            bias=self.bias_col("bv", j), scale=1.0),
                            reads=[ps[1], self.bf("smalls")], writes=[bvst])
                    else:
                        cx.op(sc, lambda ps=ps, tcc=tcc, j=j, bias_nm=bias_nm: sc.activation(
                            qraw[:, tcc * 512:(tcc + 1) * 512], ps[0][:], AF.Identity,
                            bias=self.bias_col(bias_nm, j), scale=1.0),
                            reads=[ps[1], self.bf("smalls")], writes=[bqraw])
                if ti == 2:
                    for tl in range(8):
                        ps = ps_q[tl % 2]
                        cx.op(te, lambda tl=tl, ps=ps: te.matmul(
                            ps[0][:, 0:128], vst[:, tl * 128:(tl + 1) * 128],
                            self.ident_bf[:, :], start=True, stop=True),
                            reads=[bvst, self.bf("smalls")], writes=[ps[1]])
                        cx.op(ve, lambda tl=tl, j=j, ps=ps: ve.tensor_copy(
                            vt_sb[:, tl * 2048 + j * 128: tl * 2048 + j * 128 + 128],
                            ps[0][:, 0:128]), reads=[ps[1]], writes=[bvt])
                else:
                    cx.dma(sy, lambda: sy.dma_start(out=qsw[0:64, :], in_=qraw[64:128, :]),
                           reads=[bqraw], writes=[bqsw], ch="e")
                    cx.dma(sy, lambda: sy.dma_start(out=qsw[64:128, :], in_=qraw[0:64, :]),
                           reads=[bqraw], writes=[bqsw], ch="e")
                    cos = tabs["cosq" if ti == 0 else "cosk"]
                    sin = tabs["sinq" if ti == 0 else "sink"]
                    cx.op(ve, lambda cos=cos: ve.tensor_mul(u1, qraw, cos[0]),
                          reads=[bqraw, cos[1]], writes=[bu1])
                    cx.op(ve, lambda sin=sin: ve.tensor_mul(qsw, qsw, sin[0]),
                          reads=[bqsw, sin[1]], writes=[bqsw])
                    if ti == 0:
                        cx.op(ve, lambda j=j, half=half: ve.tensor_add(
                            qrot[:, j * TOWN + half * HALF: j * TOWN + half * HALF + HALF],
                            u1, qsw), reads=[bu1, bqsw], writes=[bqr])
                    else:
                        cx.op(ve, lambda: ve.tensor_add(kst, u1, qsw),
                              reads=[bu1, bqsw], writes=[bkst])
                        cx.dma(sy, lambda j=j, half=half: sy.dma_start(
                            out=kv[j * 128:(j + 1) * 128, half * HALF:(half + 1) * HALF],
                            in_=kst), reads=[bkst], writes=[bkvin], ch="f")
        for tl in range(8):
            cx.dma(sy, lambda tl=tl, half=half: sy.dma_start(
                out=kv[D + half * HALF + tl * 128: D + half * HALF + (tl + 1) * 128, :],
                in_=vt_sb[:, tl * 2048:(tl + 1) * 2048]),
                reads=[bvt], writes=[bkvin], ch="f")

    def attn_inner(self):
        nc, cx = self.nc, self.cx
        sy, sc, ve, te, gp = self.sy, self.sc, self.ve, self.te, self.gp
        qrot, bqr = self.qrot_ap()
        ao = self.v(self.A2, 0, 128, NH * TOWN, BF16)
        bao = self.bf("at_ao")
        kk = [self.v(self.A3, i * 8192, 128, 4096, BF16) for i in range(2)]
        bkk = [self.bf("at_k%d" % i) for i in range(2)]
        vv = [self.v(self.A3, 16384 + i * 8192, 128, 4096, BF16) for i in range(2)]
        bvv = [self.bf("at_v%d" % i) for i in range(2)]
        cm = self.v(self.A3, 32768, 128, 2 * 8 * 512, BF16)
        bcm = self.bf("at_cm")
        pp = [self.v(self.A3, 49152 + i * 2048, 128, 1024, BF16) for i in range(2)]
        bpp = [self.bf("at_p%d" % i) for i in range(2)]
        bc = [self.v(self.A3, 53248 + i * 2048, 128, 512, F32) for i in range(2)]
        bbc = [self.bf("at_bc%d" % i) for i in range(2)]
        dro = self.A3[0:1, 57344:59392].bitcast(F32)
        drb = self.A3[0:1, 59392:60416].bitcast(BF16)
        bdr = self.bf("at_dr")
        bkvo = self.bf("dram_kvout")
        kvo = self.d_kvout.ap()
        cx.dma(sy, lambda: sy.dma_start(out=cm, in_=self.i["cmask"]), writes=[bcm], ch="b")
        with (
            nc.psum_tensor("a_s0", [128, 1024], F32) as ps_s0,
            nc.psum_tensor("a_s1", [128, 1024], F32) as ps_s1,
            nc.psum_tensor("a_o", [128, 512], F32) as ps_o,
            nc.psum_tensor("a_d", [128, 512], F32) as ps_d,
            nc.psum_tensor("a_b", [128, 512], F32) as ps_b,
        ):
            bs = [self.bf("a_s0"), self.bf("a_s1")]
            pss = [ps_s0, ps_s1]
            bo, bd, bb = self.bf("a_o"), self.bf("a_d"), self.bf("a_b")
            def load_head(h):
                u = h % 2
                for g in range(8):
                    r, lc = CHUNK_OWNER[g]
                    cx.dma(sy, lambda h=h, u=u, g=g, r=r, lc=lc: sy.dma_start(
                        out=kk[u][:, g * 512:(g + 1) * 512],
                        in_=kvo[r * 2 * D + h * 128: r * 2 * D + (h + 1) * 128,
                                lc * 512:(lc + 1) * 512]),
                        reads=[bkvo], writes=[bkk[u]], ch="g")
                    cx.dma(sy, lambda h=h, u=u, g=g, r=r, lc=lc: sy.dma_start(
                        out=vv[u][:, g * 512:(g + 1) * 512].rearrange(
                            "p (kb f) -> p kb f", kb=4),
                        in_=kvo[r * 2 * D + D + lc * 512: r * 2 * D + D + (lc + 1) * 512,
                                h * 128:(h + 1) * 128].rearrange(
                            "(kb p) f -> p kb f", p=128)),
                        reads=[bkvo], writes=[bvv[u]], ch="h")
            load_head(0)
            for h in range(NH):
                u = h % 2
                if h + 1 < NH:
                    load_head(h + 1)
                for j in range(NSLOT):
                    npair = EXT[j] * 2
                    qsl = qrot[:, h * TOWN + j * 512: h * TOWN + (j + 1) * 512]
                    for pi in range(npair):
                        sp = pss[pi % 2]
                        bsp = bs[pi % 2]
                        fns = []
                        for half2 in range(2):
                            kb = pi * 2 + half2
                            fns.append(lambda kb=kb, half2=half2, u=u, sp=sp, qsl=qsl:
                                       te.matmul(sp[:, half2 * 512:(half2 + 1) * 512],
                                                 kk[u][:, kb * 128:(kb + 1) * 128],
                                                 qsl, start=True, stop=True))
                        cx.group(te, fns, reads=[bkk[u], bqr], writes=[bsp])
                        pb = pp[pi % 2]
                        bpb = bpp[pi % 2]
                        cx.op(sc, lambda sp=sp, pb=pb: sc.activation(pb, sp[:], AF.Exp),
                              reads=[bsp], writes=[bpb])
                        trail = pi - (npair - 4)
                        if trail >= 0:
                            mo = (j % 2) * 4096 + trail * 1024
                            cx.op(ve, lambda pb=pb, mo=mo: ve.tensor_mul(
                                pb, pb, cm[:, mo:mo + 1024]),
                                reads=[bpb, bcm], writes=[bpb])
                        fns = []
                        for half2 in range(2):
                            kb = pi * 2 + half2
                            fns.append(lambda kb=kb, half2=half2, pb=pb, pi=pi, j=j:
                                       te.matmul(ps_d[:], self.ones_bf[:, :],
                                                 pb[:, half2 * 512:(half2 + 1) * 512],
                                                 start=(pi == 0 and half2 == 0),
                                                 stop=(pi == npair - 1 and half2 == 1)))
                            fns.append(lambda kb=kb, half2=half2, pb=pb, pi=pi, u=u:
                                       te.matmul(ps_o[:], vv[u][:, kb * 128:(kb + 1) * 128],
                                                 pb[:, half2 * 512:(half2 + 1) * 512],
                                                 start=(pi == 0 and half2 == 0),
                                                 stop=(pi == npair - 1 and half2 == 1)))
                        cx.group(te, fns, reads=[bpb, bvv[u], self.bf("smalls")],
                                 writes=[bd, bo])
                    cx.op(ve, lambda: ve.reciprocal(dro[:, 0:512], ps_d[0:1, 0:512]),
                          reads=[bd], writes=[bdr])
                    cx.op(ve, lambda: ve.tensor_copy(drb[:, 0:512], dro[:, 0:512]),
                          reads=[bdr], writes=[bdr])
                    cx.op(te, lambda: te.matmul(ps_b[:], self.ones_bf[0:1, :],
                                                drb[:, 0:512], start=True, stop=True),
                          reads=[bdr, self.bf("smalls")], writes=[bb])
                    ib = j % 2
                    cx.op(ve, lambda ib=ib: ve.tensor_copy(bc[ib], ps_b[:]),
                          reads=[bb], writes=[bbc[ib]])
                    cx.op(ve, lambda ib=ib, h=h, j=j: ve.tensor_tensor(
                        ao[:, h * TOWN + j * 512: h * TOWN + (j + 1) * 512],
                        ps_o[:], bc[ib], op=ALU.mult),
                        reads=[bo, bbc[ib]], writes=[bao])
                    if "aout" in self.dbg:
                        cx.dma(sy, lambda h=h, j=j: sy.dma_start(
                            out=self.dbg["aout"][h * 128:(h + 1) * 128,
                                                 j * 512:(j + 1) * 512],
                            in_=ao[:, h * TOWN + j * 512: h * TOWN + (j + 1) * 512]),
                            reads=[bao], ch="d")

    def attn_wo_combine(self, ps4):
        nc, cx = self.nc, self.cx
        sy, sc, ve, te, gp = self.sy, self.sc, self.ve, self.te, self.gp
        ao = self.v(self.A2, 0, 128, NH * TOWN, BF16)
        bao = self.bf("at_ao")
        inc_bf = self.v(self.A1, 0, 128, NCH * TOWN, BF16)
        binc = self.bf("A1_acc")
        wf = [self.v(self.A3, i * 8192, 128, 2048, F32) for i in range(2)]
        bwf = [self.bf("at_wof%d" % i) for i in range(2)]
        wb = [self.v(self.A3, 16384 + i * 4096, 128, 2048, BF16) for i in range(2)]
        bwb = [self.bf("at_wob%d" % i) for i in range(2)]
        with (
            nc.psum_tensor("w_p0", [128, 512], F32) as pw0,
            nc.psum_tensor("w_p1", [128, 512], F32) as pw1,
        ):
            bpw = [self.bf("w_p0"), self.bf("w_p1")]
            pws = [pw0, pw1]
            for ds in range(NCH):
                u = ds % 2
                cx.dma(sy, lambda ds=ds, u=u: sy.dma_start(
                    out=wf[u], in_=self.i["wo"][ds * 128:(ds + 1) * 128, :]),
                    writes=[bwf[u]], ch="c")
                cx.op(sc, lambda u=u: sc.activation(wb[u][:, :], wf[u][:, :], AF.Identity),
                      reads=[bwf[u]], writes=[bwb[u]])
                for tcc in range(4):
                    ip = tcc % 2
                    fns = [(lambda hh=hh, u=u, tcc=tcc, ip=ip: te.matmul(
                        pws[ip][:], wb[u][:, hh * 128:(hh + 1) * 128],
                        ao[:, hh * TOWN + tcc * 512: hh * TOWN + tcc * 512 + 512],
                        start=(hh == 0), stop=(hh == NH - 1))) for hh in range(NH)]
                    cx.group(te, fns, reads=[bwb[u], bao], writes=[bpw[ip]])
                    cx.op(sc, lambda ds=ds, tcc=tcc, ip=ip: sc.activation(
                        inc_bf[:, ds * TOWN + tcc * 512: ds * TOWN + tcc * 512 + 512],
                        pws[ip][:], AF.Identity, bias=self.bias_col("bo", ds), scale=1.0),
                        reads=[bpw[ip], self.bf("smalls")], writes=[binc])
        self.cx.join([self.sy, self.sc, self.ve, self.te, self.gp])
        dbg_x = self.dbg.get("x2")
        for tt in range(16):
            self.tile_unit_span(tt, self.d_xsp[0], self.bf("dram_xsp0"),
                                self.d_xsp[1], self.bf("dram_xsp1"), inc_bf, binc,
                                TOWN, ps4, do_ln=True, dbg_x=dbg_x, dbg_ln=None)

    def attention(self):
        nc = self.nc
        with (
            nc.psum_tensor("ap_a", [128, 512], F32) as pa,
            nc.psum_tensor("ap_b", [128, 512], F32) as pb,
            nc.psum_tensor("ap_c", [128, 512], F32) as pc,
            nc.psum_tensor("ap_d", [128, 512], F32) as pd,
        ):
            ps4 = [(pa, self.bf("ap_a")), (pb, self.bf("ap_b")),
                   (pc, self.bf("ap_c")), (pd, self.bf("ap_d"))]
            for half in range(2):
                self.load_lnA(half, ps4)
                self.attn_proj(half, ps4, ps_q=[ps4[0], ps4[1]])
            self.cx.cc(self.gp, lambda: self.gp.collective_compute(
                "AllGather", ALU.bypass,
                replica_groups=[[0, 1], [2, 3], [4, 5], [6, 7]],
                ins=[self.d_kvin.ap()], outs=[self.d_kvout.ap()]),
                reads=[self.bf("dram_kvin")], writes=[self.bf("dram_kvout")])
        self.epoch()
        self.attn_inner()
        self.epoch()
        with (
            nc.psum_tensor("aq_a", [128, 512], F32) as pa,
            nc.psum_tensor("aq_b", [128, 512], F32) as pb,
            nc.psum_tensor("aq_c", [128, 512], F32) as pc,
            nc.psum_tensor("aq_d", [128, 512], F32) as pd,
        ):
            ps4 = [(pa, self.bf("aq_a")), (pb, self.bf("aq_b")),
                   (pc, self.bf("aq_c")), (pd, self.bf("aq_d"))]
            self.attn_wo_combine(ps4)

    def epoch(self):
        self.cx.join([self.sy, self.sc, self.ve, self.te, self.gp])

    def build(self):
        nc = self.nc
        self.setup()
        self.epoch()
        with (
            nc.psum_tensor("p_a", [128, 512], F32) as pa,
            nc.psum_tensor("p_b", [128, 512], F32) as pb,
            nc.psum_tensor("p_c", [128, 512], F32) as pc,
            nc.psum_tensor("p_d", [128, 512], F32) as pd,
        ):
            ps4 = [(pa, self.bf("pre_a")), (pb, self.bf("pre_b")),
                   (pc, self.bf("pre_c")), (pd, self.bf("pre_d"))]
            self.pre_phase(ps4)
        if self.dev.startswith("pre"):
            self.finish()
            return self.nc
        self.epoch()
        self.ffn_call(0, self.i["x"], None, self.d_xsp[0], self.bf("dram_xsp0"))
        if self.dev == "ffn1":
            self.finish()
            return self.nc
        self.epoch()
        self.attention()
        if self.dev == "attn":
            self.finish()
            return self.nc
        if self.dev == "fulld":
            pass
        self.epoch()
        self.ffn_call(1, self.d_xsp[1], self.bf("dram_xsp1"), self.o_out, None,
                      last=True)
        self.finish()
        return self.nc

    def finish(self):
        self.cx.join([self.sy])
        self.stk.close()


def kernel(**inputs):
    shared = _host_layouts(inputs)
    in_maps = [{**shared, **_percore(inputs, c)} for c in range(8)]
    b = B(DEV_UPTO)
    nc = b.build()
    res = run_bass_kernel_spmd(nc, in_maps, core_ids=list(range(8)))
    kernel.last_results = res
    out = np.empty((4, SEQ, D), dtype=np.float32)
    for c in range(8):
        bidx, par = c // 2, c % 2
        chunks = CHUNKS_EVEN if par == 0 else CHUNKS_ODD
        shard = res.results[c]["out"]
        for i, g in enumerate(chunks):
            out[bidx, g * 512:(g + 1) * 512, :] = shard[i * 512:(i + 1) * 512, :]
    return out


# version 03
# speedup vs baseline: 1.6712x; 1.1621x over previous
import os
import numpy as np
import ml_dtypes
from contextlib import ExitStack

import concourse.bass as bass
import concourse.mybir as mybir
from concourse.bass_utils import run_bass_kernel_spmd

F32 = mybir.dt.float32
BF16 = mybir.dt.bfloat16
AF = mybir.ActivationFunctionType
ALU = mybir.AluOpType
AXX = mybir.AxisListType.X

D = 2048
NCH = 16
NH = 16
DH = 128
NF = 8
NE = 16
NEX = 18
TOWN = 2048
HALF = 1024
SEQ = 4096
NSLOT = 4
EXT = [2, 4, 6, 8]
CHUNKS_EVEN = [0, 3, 4, 7]
CHUNKS_ODD = [1, 2, 5, 6]
CHUNK_OWNER = {g: (0, i) for i, g in enumerate(CHUNKS_EVEN)}
CHUNK_OWNER.update({g: (1, i) for i, g in enumerate(CHUNKS_ODD)})
GROUPS = [(2 * i, 2 * i + 1) for i in range(9)]

DEV_UPTO = "full"
SKIP = set(os.environ.get("KSKIP", "").split(",")) - {""}


def _host_layouts(inp):
    f32 = np.float32
    out = {}

    def fmw(W):
        return np.ascontiguousarray(
            np.asarray(W).reshape(NCH, 128, NCH, 128).transpose(2, 1, 0, 3).reshape(D, D)
        ).astype(f32, copy=False)

    out["wq"] = fmw(inp["L0_attn_Wq"])
    out["wk"] = fmw(inp["L0_attn_Wk"])
    out["wv"] = fmw(inp["L0_attn_Wv"])
    out["wo"] = fmw(inp["L0_attn_Wo"])

    g_all = np.concatenate([np.asarray(inp["L0_ffn_rg"]), np.asarray(inp["L0_ffn_sg"])], 0)
    w1_all = np.concatenate([np.asarray(inp["L0_ffn_rw1"]), np.asarray(inp["L0_ffn_sw1"])], 0)
    w2_all = np.concatenate([np.asarray(inp["L0_ffn_rw2"]), np.asarray(inp["L0_ffn_sw2"])], 0)

    def gw(g):
        return np.ascontiguousarray(
            g.reshape(NEX, NCH, 128, NF, 128).transpose(0, 3, 2, 1, 4).reshape(NEX * NF * 128, D)
        ).astype(f32, copy=False)

    out["g"] = gw(g_all)
    out["w1"] = gw(w1_all)
    out["w2"] = np.ascontiguousarray(
        w2_all.reshape(NEX, NF, 128, NCH, 128).transpose(3, 2, 0, 1, 4).reshape(D, NEX * NF * 128)
    ).astype(f32, copy=False)
    out["wa"] = np.ascontiguousarray(np.asarray(inp["L0_ffn_wa"])).astype(f32, copy=False)

    gb_all = np.concatenate([np.asarray(inp["L0_ffn_rgb"]), np.asarray(inp["L0_ffn_sgb"])], 0)
    b1_all = np.concatenate([np.asarray(inp["L0_ffn_rb1"]), np.asarray(inp["L0_ffn_sb1"])], 0)
    out["gb"] = np.ascontiguousarray(
        gb_all.reshape(NEX, NF, 128).transpose(2, 0, 1).reshape(128, NEX * NF)).astype(f32, copy=False)
    out["b1"] = np.ascontiguousarray(
        b1_all.reshape(NEX, NF, 128).transpose(2, 0, 1).reshape(128, NEX * NF)).astype(f32, copy=False)
    out["b2"] = np.ascontiguousarray(
        np.concatenate([np.asarray(inp["L0_ffn_rb2"]), np.asarray(inp["L0_ffn_sb2"])], 0)
    ).astype(f32, copy=False)
    for nm, key in [("bq", "L0_attn_bq"), ("bk", "L0_attn_bk"),
                    ("bv", "L0_attn_bv"), ("bo", "L0_attn_bo")]:
        out[nm] = np.ascontiguousarray(np.asarray(inp[key]).reshape(NCH, 128).T).astype(f32, copy=False)
    return out


def _percore(inp, core):
    f32 = np.float32
    b, par = core // 2, core % 2
    chunks = CHUNKS_EVEN if par == 0 else CHUNKS_ODD
    x = np.asarray(inp["L0_attn_x"])
    rows = np.concatenate([np.arange(g * 512, (g + 1) * 512) for g in chunks])
    xs = np.ascontiguousarray(x[b][rows]).astype(f32, copy=False)

    inv = (1.0 / 10000.0 ** (np.arange(0, DH, 2, dtype=f32) / DH)).astype(f32)
    th = np.einsum("t,i->it", rows.astype(f32), inv).astype(f32)
    cos = np.cos(np.concatenate([th, th], 0)).astype(f32)
    sin = np.sin(np.concatenate([th, th], 0)).astype(f32)
    sinf = np.concatenate([-sin[:64], sin[64:]], 0)
    s = np.float32(DH ** -0.5)

    def pattern(kind):
        m = np.zeros((128, 8 * 512), f32)
        for blk in range(8):
            ch, r = blk // 4, blk % 4
            col = blk * 512
            kk = np.arange(128)[:, None] + r * 128
            qq = np.arange(512)[None, :]
            if kind == 0:
                if ch == 0:
                    m[:, col:col + 512] = (qq >= kk)
            else:
                m[:, col:col + 512] = 1.0 if ch == 0 else (qq >= kk)
        return m

    sel = np.zeros((128, 2 * 8 * 512), f32)
    for jj in range(2):
        gq = chunks[jj]
        diag_at = gq - EXT[jj] + 2
        sel[:, jj * 4096:(jj + 1) * 4096] = pattern(1 if diag_at == 1 else 0)
    return {"x": xs, "cosq": np.ascontiguousarray(cos * s), "sinq": np.ascontiguousarray(sinf * s),
            "cosk": np.ascontiguousarray(cos), "sink": np.ascontiguousarray(sinf),
            "cmask": np.ascontiguousarray(sel.astype(ml_dtypes.bfloat16))}


class Sem:
    def __init__(self, nc, name, stack, is_dma=False):
        self.h = stack.enter_context(nc.semaphore(name))
        self.n = 0
        self.name = name
        self.is_dma = is_dma


class Buf:
    __slots__ = ("name", "w", "readers")

    def __init__(self, name):
        self.name = name
        self.w = None
        self.readers = []


class Ctx:

    def __init__(self, nc, stack):
        self.nc = nc
        self.eng_sems = {}
        self.waited = {}
        self.te_id = id(nc.tensor)
        for eng, nm in [(nc.tensor, "te"), (nc.scalar, "sc"), (nc.vector, "ve"),
                        (nc.gpsimd, "gp")]:
            self.eng_sems[id(eng)] = Sem(nc, "e_" + nm, stack)
        self.dma_sems = {nm: Sem(nc, "d_" + nm, stack, is_dma=True) for nm in
                         ["a", "b", "c", "d", "e", "f", "g", "h"]}
        self.cc_sem = Sem(nc, "cc", stack, is_dma=True)

    def _wait(self, eng, semobj, cnt):
        if cnt <= 0:
            return
        if id(eng) == self.te_id and self.eng_sems.get(self.te_id) is semobj:
            return
        if semobj.is_dma:
            cnt = semobj.n
        key = (id(eng), id(semobj))
        if self.waited.get(key, 0) >= cnt:
            return
        eng.wait_ge(semobj.h, cnt)
        self.waited[key] = cnt

    def _pre(self, eng, reads, writes, is_dma):
        need = {}
        for b in reads:
            if b.w:
                s, c = b.w
                need[id(s)] = (s, max(c, need.get(id(s), (s, 0))[1]))
        for b in writes:
            if b.w:
                s, c = b.w
                need[id(s)] = (s, max(c, need.get(id(s), (s, 0))[1]))
            for s, c in b.readers:
                need[id(s)] = (s, max(c, need.get(id(s), (s, 0))[1]))
        for s, c in need.values():
            self._wait(eng, s, c)

    def _post(self, inst, sem, reads, writes, k):
        inst.then_inc(sem.h, k)
        sem.n += k
        for b in writes:
            b.w = (sem, sem.n)
            b.readers = []
        for b in reads:
            b.readers.append((sem, sem.n))
        return sem.n

    def op(self, eng, fn, reads=(), writes=()):
        self._pre(eng, reads, writes, False)
        return self._post(fn(), self.eng_sems[id(eng)], reads, writes, 1)

    def group(self, eng, fns, reads=(), writes=()):
        self._pre(eng, reads, writes, False)
        inst = None
        for f in fns:
            inst = f()
        return self._post(inst, self.eng_sems[id(eng)], reads, writes, 1)

    def dma(self, q, fn, reads=(), writes=(), ch="a"):
        sem = self.dma_sems[ch]
        self._wait(q, sem, sem.n)
        self._pre(q, reads, writes, True)
        return self._post(fn(), sem, reads, writes, 16)

    def cc(self, q, fn, reads=(), writes=()):
        self._pre(q, reads, writes, True)
        return self._post(fn(), self.cc_sem, reads, writes, 1)

    def join(self, engines):
        for e in engines:
            for s in list(self.eng_sems.values()) + list(self.dma_sems.values()) + [self.cc_sem]:
                self._wait(e, s, s.n)


class B:
    def __init__(self, dev="full"):
        self.dev = dev
        nc = bass.Bass("TRN2", target_bir_lowering=False, num_devices=8)
        self.nc = nc
        self.stk = ExitStack()
        self.sy, self.sc, self.ve, self.te, self.gp = nc.sync, nc.scalar, nc.vector, nc.tensor, nc.gpsimd
        self.cx = Ctx(nc, self.stk)

        dt = nc.dram_tensor
        A = lambda name, shape, d=F32, kind="ExternalInput": dt(name, shape, d, kind=kind).ap()
        self.i = {
            "x": A("x", [TOWN, D]),
            "wq": A("wq", [D, D]), "wk": A("wk", [D, D]), "wv": A("wv", [D, D]),
            "wo": A("wo", [D, D]),
            "g": A("g", [NEX * NF * 128, D]), "w1": A("w1", [NEX * NF * 128, D]),
            "w2": A("w2", [D, NEX * NF * 128]),
            "wa": A("wa", [D, NE]),
            "gb": A("gb", [128, NEX * NF]), "b1": A("b1", [128, NEX * NF]),
            "b2": A("b2", [NEX, D]),
            "bq": A("bq", [128, NCH]), "bk": A("bk", [128, NCH]),
            "bv": A("bv", [128, NCH]), "bo": A("bo", [128, NCH]),
            "cosq": A("cosq", [128, TOWN]), "sinq": A("sinq", [128, TOWN]),
            "cosk": A("cosk", [128, TOWN]), "sink": A("sink", [128, TOWN]),
            "cmask": A("cmask", [128, 2 * 8 * 512], BF16),
        }
        self.o_out = A("out", [TOWN, D], F32, "ExternalOutput")
        self.d_xsp = [dt("xsp0", [TOWN, D], F32).ap(), dt("xsp1", [TOWN, D], F32).ap()]
        self.d_ln = dt("lnsp", [TOWN, D], BF16).ap()
        self.d_kvin = dt("kvin", [2 * D, TOWN], BF16)
        self.d_kvout = dt("kvout", [4 * D, TOWN], BF16)

        self.dbg = {}
        if self.dev != "full":
            for nm, shape, d in [("ln1", [TOWN, D], BF16), ("gates", [NEX, TOWN], BF16),
                                 ("x1", [TOWN, D], F32), ("q", [D, TOWN], BF16),
                                 ("aout", [D, TOWN], BF16), ("x2", [TOWN, D], F32),
                                 ("ln2", [TOWN, D], BF16)]:
                self.dbg[nm] = A("dbg_" + nm, shape, d, "ExternalOutput")

        sb = self.stk.enter_context
        U8 = mybir.dt.uint8
        self.A1 = sb(nc.sbuf_tensor("A1", [128, 65536], U8))
        self.A2 = sb(nc.sbuf_tensor("A2", [128, 65536], U8))
        self.A3 = sb(nc.sbuf_tensor("A3", [128, 65536], U8))
        self.SM = sb(nc.sbuf_tensor("SM", [128, 8704], U8))
        self.bufs = {}

        self.bias = self.v(self.SM, 0, 128, 352, F32)
        self.b2e = self.v(self.SM, 1408, NEX, 2048, BF16)
        self.gate_b = self.v(self.SM, 5504, NEX, 1024, BF16)
        self.ones_bf = self.v(self.SM, 7552, 128, 128, BF16)
        self.ident_bf = self.v(self.SM, 7808, 128, 128, BF16)
        self.col = [self.v(self.SM, 8064 + 4 * i, 128, 1, F32) for i in range(4)]
        self.wa_bf = self.v(self.SM, 8192, 128, 256, BF16)

    def bf(self, name):
        if name not in self.bufs:
            self.bufs[name] = Buf(name)
        return self.bufs[name]

    def v(self, arena, off, parts, cols, dtype):
        isz = {F32: 4, BF16: 2}[dtype]
        return arena[0:parts, off:off + cols * isz].bitcast(dtype)

    def bias_col(self, which, idx):
        offs = {"bq": 0, "bk": NCH, "bv": 2 * NCH, "bo": 3 * NCH,
                "gb": 4 * NCH, "b1": 4 * NCH + NEX * NF}
        o = offs[which] + idx
        return self.bias[:, o:o + 1]

    RG = {
        "gwf0": 0, "gwf1": 8192,
        "w2f0": 16384, "w2f1": 20480,
        "gwb0": 24576, "gwb1": 28672,
        "w2b0": 32768, "w2b1": 34816,
        "gbc0": 36864, "gbc1": 38912,
        "ffa0": 40960, "ffa1": 41984,
        "rt": 43008,
        "ct_x0": 45056, "ct_x1": 53248,
        "ct_xb": 61440,
    }

    def reg(self, name, parts, cols, dtype):
        return self.v(self.A2, self.RG[name], parts, cols, dtype), self.bf("A2_" + name)

    def setup(self):
        cx, sy, gp, ve = self.cx, self.sy, self.gp, self.ve
        bS = self.bf("smalls")
        cx.op(gp, lambda: gp.memset(self.ident_bf[:], 1.0), writes=[bS])
        cx.op(gp, lambda: gp.affine_select(self.ident_bf[:], self.ident_bf[:],
                                           pattern=[[1, 128]], compare_op=ALU.is_equal,
                                           fill=0.0, base=0, channel_multiplier=-1),
              reads=[bS], writes=[bS])
        cx.op(ve, lambda: ve.memset(self.ones_bf[:], 1.0), writes=[bS])
        o = 0
        for nm in ["bq", "bk", "bv", "bo", "gb", "b1"]:
            src = self.i[nm]
            w = NCH if nm in ("bq", "bk", "bv", "bo") else NEX * NF
            cx.dma(sy, lambda src=src, o=o, w=w: sy.dma_start(out=self.bias[:, o:o + w], in_=src),
                   writes=[bS], ch="a")
            o += w
        b2f = self.v(self.A3, 0, NEX, 2048, F32)
        bT = self.bf("A3_setup")
        cx.dma(sy, lambda: sy.dma_start(out=b2f, in_=self.i["b2"]), writes=[bT], ch="a")
        cx.op(ve, lambda: ve.tensor_scalar(self.b2e[:, :], b2f, 0.5, None, op0=ALU.mult),
              reads=[bT], writes=[bS])
        rt = self.RG["rt"]
        self.r_abf = self.A2[0:1, rt + 512: rt + 1536].bitcast(BF16)
        wa_f = self.v(self.A3, 16384, 128, 256, F32)
        cx.dma(sy, lambda: sy.dma_start(
            out=wa_f.rearrange("p (c e) -> p c e", c=NCH),
            in_=self.i["wa"].rearrange("(c p) e -> p c e", p=128)), writes=[bT], ch="a")
        cx.op(ve, lambda: ve.tensor_copy(self.wa_bf, wa_f), reads=[bT], writes=[self.bf("wa_bf")])

    def tile_unit(self, tt, x_src, bxsrc, x_dst, bxdst, inc_bf, binc, psums,
                  do_ln, dbg_x=None, dbg_ln=None):
        return self.tile_unit_span(tt, x_src, bxsrc, x_dst, bxdst, inc_bf, binc,
                                   HALF, psums, do_ln, dbg_x, dbg_ln)

    def tile_unit_span(self, tt, x_src, bxsrc, x_dst, bxdst, inc_bf, binc, span,
                       psums, do_ln, dbg_x=None, dbg_ln=None):
        cx = self.cx
        sy, sc, ve, te = self.sy, self.sc, self.ve, self.te
        xt, bxt = self.reg("ct_x%d" % (tt % 2), 128, 2048, F32)
        xb, bxb = self.reg("ct_xb", 128, 2048, BF16)
        sq0, bsq0 = self.reg("ffa0", 128, 512, BF16)
        sq1, bsq1 = self.reg("ffa1", 128, 512, BF16)
        sqs = [(sq0, bsq0), (sq1, bsq1)]
        tloc = tt % (span // 128)
        cx.dma(sy, lambda: sy.dma_start(out=xt, in_=x_src[tt * 128:(tt + 1) * 128, :]),
               reads=[bxsrc] if bxsrc else [], writes=[bxt], ch="g")
        if inc_bf is not None:
            for b in range(4):
                fns = []
                for j in range(4):
                    ds = 4 * b + j
                    fns.append(lambda ds=ds, b=b, j=j: te.matmul(
                        psums[b][0][:, j * 128:(j + 1) * 128],
                        inc_bf[:, ds * span + tloc * 128: ds * span + tloc * 128 + 128],
                        self.ident_bf[:, :], start=True, stop=True))
                cx.group(te, fns, reads=[binc, self.bf("smalls")], writes=[psums[b][1]])
            for b in range(4):
                cx.op(ve, lambda b=b: ve.tensor_tensor(
                    xt[:, b * 512:(b + 1) * 512], xt[:, b * 512:(b + 1) * 512],
                    psums[b][0][:, 0:512], op=ALU.add),
                    reads=[psums[b][1], bxt], writes=[bxt])
        if x_dst is not None:
            cx.dma(sy, lambda: sy.dma_start(out=x_dst[tt * 128:(tt + 1) * 128, :], in_=xt),
                   reads=[bxt], writes=[bxdst] if bxdst else [], ch="c")
        if dbg_x is not None:
            cx.dma(sy, lambda: sy.dma_start(out=dbg_x[tt * 128:(tt + 1) * 128, :], in_=xt),
                   reads=[bxt], ch="d")
        if not do_ln:
            return
        cx.op(ve, lambda: ve.tensor_copy(xb, xt), reads=[bxt], writes=[bxb])
        c0, c1, c2, c3 = self.col
        bR = self.bf("cols")
        for ch in range(4):
            sq, bsq = sqs[ch % 2]
            cx.op(sc, lambda ch=ch, sq=sq: sc.activation(
                sq, xb[:, ch * 512:(ch + 1) * 512], AF.Square), reads=[bxb], writes=[bsq])
            if ch == 0:
                cx.op(ve, lambda: ve.reduce_sum(c0, xb[:, 0:512], axis=AXX),
                      reads=[bxb], writes=[bR])
                cx.op(ve, lambda sq=sq: ve.reduce_sum(c1, sq, axis=AXX),
                      reads=[bsq], writes=[bR])
            else:
                cx.op(ve, lambda ch=ch: ve.reduce_sum(c2, xb[:, ch * 512:(ch + 1) * 512],
                                                      axis=AXX), reads=[bxb], writes=[bR])
                cx.op(ve, lambda: ve.tensor_add(c0, c0, c2), reads=[bR], writes=[bR])
                cx.op(ve, lambda sq=sq: ve.reduce_sum(c3, sq, axis=AXX),
                      reads=[bsq], writes=[bR])
                cx.op(ve, lambda: ve.tensor_add(c1, c1, c3), reads=[bR], writes=[bR])
        cx.op(ve, lambda: ve.tensor_scalar(c0, c0, 1.0 / D, None, op0=ALU.mult),
              reads=[bR], writes=[bR])
        cx.op(ve, lambda: ve.tensor_scalar(c1, c1, 1.0 / D, None, op0=ALU.mult),
              reads=[bR], writes=[bR])
        cx.op(ve, lambda: ve.tensor_mul(c2, c0, c0), reads=[bR], writes=[bR])
        cx.op(ve, lambda: ve.tensor_sub(c1, c1, c2), reads=[bR], writes=[bR])
        cx.op(ve, lambda: ve.tensor_scalar(c1, c1, 1e-5, None, op0=ALU.add),
              reads=[bR], writes=[bR])
        cx.op(sc, lambda: sc.activation(c1, c1, AF.Sqrt), reads=[bR], writes=[bR])
        cx.op(ve, lambda: ve.reciprocal(c1, c1), reads=[bR], writes=[bR])
        cx.op(ve, lambda: ve.tensor_mul(c0, c0, c1), reads=[bR], writes=[bR])
        cx.op(ve, lambda: ve.tensor_scalar(c0, c0, -1.0, None, op0=ALU.mult),
              reads=[bR], writes=[bR])
        cx.op(ve, lambda: ve.tensor_scalar(xb, xb, c1, c0, op0=ALU.mult, op1=ALU.add),
              reads=[bR, bxb], writes=[bxb])
        cx.dma(sy, lambda: sy.dma_start(out=self.d_ln[tt * 128:(tt + 1) * 128, :], in_=xb),
               reads=[bxb], writes=[self.bf("dram_ln")], ch="f")
        if dbg_ln is not None:
            cx.dma(sy, lambda: sy.dma_start(out=dbg_ln[tt * 128:(tt + 1) * 128, :], in_=xb),
                   reads=[bxb], ch="d")

    def pre_phase(self, psums):
        for tt in range(16):
            self.tile_unit(tt, self.i["x"], None, None, None, None, None, psums,
                           do_ln=True, dbg_ln=self.dbg.get("ln1"))

    def lnA_ap(self):
        return self.v(self.A3, 0, 128, NCH * HALF, BF16), self.bf("A3_lnA")

    def load_lnA(self, half, psums):
        cx, sy, te, ve = self.cx, self.sy, self.te, self.ve
        lnA, blnA = self.lnA_ap()
        bln = self.bf("dram_ln")
        for i in range(8):
            tt = half * 8 + i
            st, bst = self.reg("ct_x%d" % (i % 2), 128, 4096, BF16)
            cx.dma(sy, lambda tt=tt, st=st: sy.dma_start(
                out=st[:, 0:2048], in_=self.d_ln[tt * 128:(tt + 1) * 128, :]),
                reads=[bln], writes=[bst], ch="g")
            for b in range(4):
                fns = []
                for j in range(4):
                    ds = 4 * b + j
                    fns.append(lambda ds=ds, b=b, j=j, st=st: te.matmul(
                        psums[b][0][:, j * 128:(j + 1) * 128],
                        st[:, ds * 128:(ds + 1) * 128],
                        self.ident_bf[:, :], start=True, stop=True))
                cx.group(te, fns, reads=[bst, self.bf("smalls")], writes=[psums[b][1]])
            for b in range(4):
                for j in range(4):
                    ds = 4 * b + j
                    cx.op(ve, lambda ds=ds, b=b, j=j, i=i: ve.tensor_copy(
                        lnA[:, ds * HALF + i * 128: ds * HALF + i * 128 + 128],
                        psums[b][0][:, j * 128:(j + 1) * 128]),
                        reads=[psums[b][1]], writes=[blnA])

    def router(self, half, ps_y0, ps_y1):
        nc, cx = self.nc, self.cx
        sy, sc, ve, te = self.sy, self.sc, self.ve, self.te
        lnA, blnA = self.lnA_ap()
        rt = self.RG["rt"]
        aff = self.v(self.A2, rt, 128, 18, F32)
        eq1 = self.v(self.A2, rt + 80, 128, 18, F32)
        eq2 = self.v(self.A2, rt + 160, 128, 18, F32)
        G = self.v(self.A2, rt + 240, 128, 32, BF16)
        Gf = self.v(self.A2, rt + 368, 128, 18, F32)
        m1 = self.v(self.A2, rt + 304, 128, 1, F32)
        m2 = self.v(self.A2, rt + 312, 128, 1, F32)
        p1 = self.v(self.A2, rt + 320, 128, 1, F32)
        p2 = self.v(self.A2, rt + 328, 128, 1, F32)
        ss = self.v(self.A2, rt + 336, 128, 1, F32)
        brt = self.bf("A2_rt")
        bgb = self.bf("gate_b")
        ps_r = ps_y0[:, 0:16]
        ps_g = ps_y1[0:NEX, 0:128]
        bpr, bpg = self.bf("ps_y0"), self.bf("ps_y1")
        for tb in range(8):
            cx.group(te, [(lambda c=c, tb=tb: te.matmul(
                ps_r, lnA[:, c * HALF + tb * 128: c * HALF + tb * 128 + 128],
                self.wa_bf[:, c * 16:(c + 1) * 16],
                start=(c == 0), stop=(c == NCH - 1))) for c in range(NCH)],
                reads=[blnA, self.bf("wa_bf")], writes=[bpr])
            cx.op(sc, lambda: sc.activation(aff[:, 0:16], ps_r, AF.Sigmoid),
                  reads=[bpr], writes=[brt])
            for f in [
                lambda: ve.reduce_max(m1, aff[:, 0:16], axis=AXX),
                lambda: ve.tensor_scalar(eq1[:, 0:16], aff[:, 0:16], m1, None, op0=ALU.is_equal),
                lambda: ve.scalar_tensor_tensor(eq2[:, 0:16], eq1[:, 0:16], -1e9, aff[:, 0:16],
                                                op0=ALU.mult, op1=ALU.add),
                lambda: ve.reduce_max(m2, eq2[:, 0:16], axis=AXX),
                lambda: ve.tensor_add(ss, m1, m2),
                lambda: ve.reciprocal(ss, ss),
                lambda: ve.tensor_mul(p1, m1, ss),
                lambda: ve.tensor_mul(p2, m2, ss),
                lambda: ve.tensor_scalar(eq2[:, 0:16], eq2[:, 0:16], m2, None, op0=ALU.is_equal),
                lambda: ve.tensor_scalar(Gf[:, 0:16], eq1[:, 0:16], p1, None, op0=ALU.mult),
                lambda: ve.scalar_tensor_tensor(G[:, 0:16], eq2[:, 0:16], p2, Gf[:, 0:16],
                                                op0=ALU.mult, op1=ALU.add),
                lambda: ve.memset(G[:, 16:18], 1.0),
            ]:
                cx.op(ve, f, reads=[brt], writes=[brt])
            cx.op(te, lambda: te.matmul(ps_g, G[:, 0:18], self.ident_bf[:, :],
                                        start=True, stop=True),
                  reads=[brt, self.bf("smalls")], writes=[bpg])
            cx.op(ve, lambda tb=tb: ve.tensor_copy(
                self.gate_b[:, tb * 128:(tb + 1) * 128], ps_g),
                reads=[bpg], writes=[bgb])
            if "gates" in self.dbg:
                cx.dma(sy, lambda tb=tb, half=half: sy.dma_start(
                    out=self.dbg["gates"][0:NEX, half * HALF + tb * 128: half * HALF + tb * 128 + 128],
                    in_=self.gate_b[:, tb * 128:(tb + 1) * 128]), reads=[bgb], ch="d")

    def acc_ap(self):
        return self.v(self.A1, 0, 128, NCH * HALF, F32), self.bf("A1_acc")

    def acc_init(self, half, ps0, b0, ps1, b1):
        cx, ve, te = self.cx, self.ve, self.te
        lnA, blnA = self.lnA_ap()
        acc, bacc = self.acc_ap()
        bp = [b0, b1]
        for ds in range(NCH):
            for tcc in range(2):
                ip = (ds * 2 + tcc) % 2
                ps = ps0 if ip == 0 else ps1
                cx.op(te, lambda ds=ds, tcc=tcc, ps=ps: te.matmul(
                    ps[:], self.b2e[:, ds * 128:(ds + 1) * 128],
                    self.gate_b[:, tcc * 512:(tcc + 1) * 512], start=True, stop=True),
                    reads=[self.bf("smalls"), self.bf("gate_b")], writes=[bp[ip]])
                cx.op(ve, lambda ds=ds, tcc=tcc, ps=ps: ve.scalar_tensor_tensor(
                    acc[:, ds * HALF + tcc * 512: ds * HALF + tcc * 512 + 512],
                    lnA[:, ds * HALF + tcc * 512: ds * HALF + tcc * 512 + 512],
                    0.5, ps[:], op0=ALU.mult, op1=ALU.add),
                    reads=[bp[ip], blnA], writes=[bacc])

    def expert_group(self, gi, psums):
        (ps_G0, ps_G1, ps_W0, ps_W1, ps_y0, ps_y1) = psums
        cx = self.cx
        sy, sc, ve, te, gp = self.sy, self.sc, self.ve, self.te, self.gp
        e0, e1 = GROUPS[gi]
        routed = e0 < NE
        lnA, blnA = self.lnA_ap()
        acc, bacc = self.acc_ap()
        yh = self.v(self.A3, 32768, 128, 16 * HALF, BF16)
        byh = self.bf("A3_yh")
        gwf, bgwf, gwb, bgwb, w2f, bw2f, w2b, bw2b, gbc, bgbc, ffa, bffa = \
            [], [], [], [], [], [], [], [], [], [], [], []
        for u in range(2):
            a, b_ = self.reg(f"gwf{u}", 128, 2048, F32); gwf.append(a); bgwf.append(b_)
            a, b_ = self.reg(f"gwb{u}", 128, 2048, BF16); gwb.append(a); bgwb.append(b_)
            a, b_ = self.reg(f"w2f{u}", 128, 1024, F32); w2f.append(a); bw2f.append(b_)
            a, b_ = self.reg(f"w2b{u}", 128, 1024, BF16); w2b.append(a); bw2b.append(b_)
            a, b_ = self.reg(f"gbc{u}", 128, 1024, BF16); gbc.append(a); bgbc.append(b_)
            a, b_ = self.reg(f"ffa{u}", 128, 512, BF16); ffa.append(a); bffa.append(b_)
        pool = [((ps_G0, self.bf("ps_G0")), (ps_G1, self.bf("ps_G1"))),
                ((ps_W0, self.bf("ps_W0")), (ps_W1, self.bf("ps_W1"))),
                ((ps_y0, self.bf("ps_y0")), (ps_y1, self.bf("ps_y1")))]
        if routed:
            for k, e in enumerate((e0, e1)):
                for tcc in range(2):
                    row = self.r_abf
                    pb = pool[(k * 2 + tcc) % 3][0]
                    cx.dma(sy, lambda e=e, tcc=tcc: sy.dma_start(
                        out=row[:, 0:512],
                        in_=self.gate_b[e:e + 1, tcc * 512:(tcc + 1) * 512]),
                        reads=[self.bf("gate_b")], writes=[self.bf("rows")], ch="b")
                    cx.op(te, lambda pb=pb: te.matmul(pb[0][:], self.ones_bf[0:1, :],
                                                      row[:, 0:512], start=True, stop=True),
                          reads=[self.bf("rows"), self.bf("smalls")], writes=[pb[1]])
                    cx.op(ve, lambda k=k, tcc=tcc, pb=pb: ve.tensor_copy(
                        gbc[k][:, tcc * 512:(tcc + 1) * 512], pb[0][:]),
                        reads=[pb[1]], writes=[bgbc[k]])
        for si in range(16):
            ei = (e0, e1)[si // 8]
            fs = si % 8
            row0 = (ei * NF + fs) * 128
            ctx = [pool[(si * 2 + tcc) % 3] for tcc in range(2)]
            for dh2 in range(2):
                u = dh2
                cx.dma(sy, lambda row0=row0, dh2=dh2, u=u: sy.dma_start(
                    out=gwf[u][:, 0:1024],
                    in_=self.i["g"][row0:row0 + 128, dh2 * 1024:(dh2 + 1) * 1024]),
                    writes=[bgwf[u]], ch="c")
                cx.dma(sy, lambda row0=row0, dh2=dh2, u=u: sy.dma_start(
                    out=gwf[u][:, 1024:2048],
                    in_=self.i["w1"][row0:row0 + 128, dh2 * 1024:(dh2 + 1) * 1024]),
                    writes=[bgwf[u]], ch="d")
                cx.op(sc, lambda u=u: sc.activation(gwb[u][:, :], gwf[u][:, :], AF.Identity),
                      reads=[bgwf[u]], writes=[bgwb[u]])
                for tcc in range(2):
                    (psG, bG_), (psW, bW_) = ctx[tcc]
                    fns = []
                    for cc in range(8):
                        c = dh2 * 8 + cc
                        fns.append(lambda cc=cc, c=c, u=u, tcc=tcc, psG=psG: te.matmul(
                            psG[:], gwb[u][:, cc * 128:(cc + 1) * 128],
                            lnA[:, c * HALF + tcc * 512: c * HALF + tcc * 512 + 512],
                            start=(c == 0), stop=(c == NCH - 1)))
                        fns.append(lambda cc=cc, c=c, u=u, tcc=tcc, psW=psW: te.matmul(
                            psW[:], gwb[u][:, 1024 + cc * 128: 1024 + (cc + 1) * 128],
                            lnA[:, c * HALF + tcc * 512: c * HALF + tcc * 512 + 512],
                            start=(c == 0), stop=(c == NCH - 1)))
                    cx.group(te, fns, reads=[bgwb[u], blnA], writes=[bG_, bW_])
            for tcc in range(2):
                (psG, bG_), (psW, bW_) = ctx[tcc]
                ysl = yh[:, si * HALF + tcc * 512: si * HALF + tcc * 512 + 512]
                cx.op(sc, lambda psG=psG, ysl=ysl, ei=ei, fs=fs: sc.activation(
                    ysl, psG[:], AF.Gelu, bias=self.bias_col("gb", ei * NF + fs), scale=1.0),
                    reads=[bG_, self.bf("smalls")], writes=[byh])
                cx.op(ve, lambda psW=psW, tcc=tcc, ei=ei, fs=fs: ve.tensor_scalar(
                    ffa[tcc][:, :], psW[:], self.bias_col("b1", ei * NF + fs), None, op0=ALU.add),
                    reads=[bW_, self.bf("smalls")], writes=[bffa[tcc]])
                if routed:
                    k = si // 8
                    cx.op(ve, lambda tcc=tcc, k=k: ve.tensor_tensor(
                        ffa[tcc][:, :], ffa[tcc][:, :],
                        gbc[k][:, tcc * 512:(tcc + 1) * 512], op=ALU.mult),
                        reads=[bffa[tcc], bgbc[k]], writes=[bffa[tcc]])
                cx.op(ve, lambda tcc=tcc, ysl=ysl: ve.tensor_tensor(
                    ysl, ysl, ffa[tcc][:, :], op=ALU.mult),
                    reads=[bffa[tcc], byh], writes=[byh])
        col0 = e0 * NF * 128
        for ds in range(NCH):
            ctx = [pool[(ds * 2 + tcc) % 3] for tcc in range(2)]
            for w2h in range(2):
                u = w2h
                cx.dma(sy, lambda ds=ds, w2h=w2h, u=u, col0=col0: sy.dma_start(
                    out=w2f[u][:, :],
                    in_=self.i["w2"][ds * 128:(ds + 1) * 128,
                                     col0 + w2h * 1024: col0 + (w2h + 1) * 1024]),
                    writes=[bw2f[u]], ch="e")
                cx.op(sc, lambda u=u: sc.activation(w2b[u][:, :], w2f[u][:, :], AF.Identity,
                                                    scale=0.5),
                      reads=[bw2f[u]], writes=[bw2b[u]])
                for tcc in range(2):
                    psY, bY_ = ctx[tcc][0]
                    fns = []
                    for blk in range(8):
                        gblk = w2h * 8 + blk
                        fns.append(lambda blk=blk, gblk=gblk, u=u, tcc=tcc, psY=psY: te.matmul(
                            psY[:], w2b[u][:, blk * 128:(blk + 1) * 128],
                            yh[:, gblk * HALF + tcc * 512: gblk * HALF + tcc * 512 + 512],
                            start=(gblk == 0), stop=(gblk == 15)))
                    cx.group(te, fns, reads=[bw2b[u], byh], writes=[bY_])
            for tcc in range(2):
                psY, bY_ = ctx[tcc][0]
                asl = acc[:, ds * HALF + tcc * 512: ds * HALF + tcc * 512 + 512]
                cx.op(ve, lambda psY=psY, asl=asl: ve.tensor_tensor(asl, asl, psY[:], op=ALU.add),
                      reads=[bY_], writes=[bacc])

    def combine(self, half, call, xsp_in, bspin, xsp_out, bspout, psums,
                last=False, dbg_key=None):
        cx, ve = self.cx, self.ve
        acc, bacc = self.acc_ap()
        inc_bf = self.v(self.A3, 32768, 128, NCH * HALF, BF16)
        binc = self.bf("A3_yh")
        for q in range(4):
            cx.op(ve, lambda q=q: ve.tensor_copy(
                inc_bf[:, q * 4 * HALF:(q + 1) * 4 * HALF],
                acc[:, q * 4 * HALF:(q + 1) * 4 * HALF]), reads=[bacc], writes=[binc])
        dbg_x = self.dbg.get(dbg_key) if dbg_key else None
        dbg_ln = self.dbg.get("ln2") if (call == 0 and not last) else None
        for i in range(8):
            tt = half * 8 + i
            self.tile_unit(tt, xsp_in, bspin, xsp_out, bspout, inc_bf, binc,
                           psums, do_ln=(not last), dbg_x=dbg_x, dbg_ln=dbg_ln)

    def ffn_call(self, call, xsp_in, bspin, xsp_out, bspout, last=False):
        nc = self.nc
        with (
            nc.psum_tensor("f%d_G0" % call, [128, 512], F32) as ps_G0,
            nc.psum_tensor("f%d_G1" % call, [128, 512], F32) as ps_G1,
            nc.psum_tensor("f%d_W0" % call, [128, 512], F32) as ps_W0,
            nc.psum_tensor("f%d_W1" % call, [128, 512], F32) as ps_W1,
            nc.psum_tensor("f%d_y0" % call, [128, 512], F32) as ps_y0,
            nc.psum_tensor("f%d_y1" % call, [128, 512], F32) as ps_y1,
        ):
            psums = (ps_G0, ps_G1, ps_W0, ps_W1, ps_y0, ps_y1)
            ps4 = [(ps_G0, self.bf("ps_G0")), (ps_G1, self.bf("ps_G1")),
                   (ps_W0, self.bf("ps_W0")), (ps_W1, self.bf("ps_W1"))]
            for half in range(2):
                self.load_lnA(half, ps4)
                self.router(half, ps_y0, ps_y1)
                self.acc_init(half, ps_y0, self.bf("ps_y0"), ps_y1, self.bf("ps_y1"))
                for gi in range(9):
                    self.expert_group(gi, psums)
                self.combine(half, call, xsp_in, bspin, xsp_out, bspout, ps4,
                             last=last, dbg_key=("x1" if call == 0 else None))


    def qrot_ap(self):
        return self.v(self.A1, 0, 128, NH * TOWN, BF16), self.bf("A1_acc")

    def attn_proj(self, half, ps4, ps_q):
        nc, cx = self.nc, self.cx
        sy, sc, ve, te, gp = self.sy, self.sc, self.ve, self.te, self.gp
        lnA, blnA = self.lnA_ap()
        qrot, bqr = self.qrot_ap()
        wf = [self.v(self.A2, i * 8192, 128, 2048, F32) for i in range(2)]
        bwf = [self.bf("A2_gwf%d" % i) for i in range(2)]
        wb = [self.v(self.A2, 16384 + i * 4096, 128, 2048, BF16) for i in range(2)]
        bwb = [self.bf("at_wb%d" % i) for i in range(2)]
        tabs = {}
        for ti, nm in enumerate(["cosq", "sinq", "cosk", "sink"]):
            tabs[nm] = (self.v(self.A2, 24576 + ti * 4096, 128, 1024, F32),
                        self.bf("at_" + nm))
        qraw = self.v(self.A2, 40960, 128, 1024, F32)
        bqraw = self.bf("at_qraw")
        qsw = self.v(self.A2, 45056, 128, 1024, F32)
        bqsw = self.bf("at_qsw")
        u1 = self.v(self.A2, 49152, 128, 1024, F32)
        bu1 = self.bf("at_u1")
        kst = self.v(self.A2, 53248, 128, 1024, BF16)
        bkst = self.bf("at_kst")
        vst = self.v(self.A2, 55296, 128, 1024, BF16)
        bvst = self.bf("at_vst")
        vt_sb = self.v(self.A3, 32768, 128, 8 * 2048, BF16)
        bvt = self.bf("A3_yh")
        bkvin = self.bf("dram_kvin")
        kv = self.d_kvin.ap()
        for nm in ["cosq", "sinq", "cosk", "sink"]:
            cx.dma(sy, lambda nm=nm: sy.dma_start(
                out=tabs[nm][0], in_=self.i[nm][:, half * HALF:(half + 1) * HALF]),
                writes=[tabs[nm][1]], ch="b")
        for j in range(NH):
            for ti, (wnm, bias_nm) in enumerate([("wq", "bq"), ("wk", "bk"), ("wv", "bv")]):
                u = (j * 3 + ti) % 2
                cx.dma(sy, lambda wnm=wnm, j=j, u=u: sy.dma_start(
                    out=wf[u], in_=self.i[wnm][j * 128:(j + 1) * 128, :]),
                    writes=[bwf[u]], ch="c" if ti != 1 else "d")
                cx.op(sc, lambda u=u: sc.activation(wb[u][:, :], wf[u][:, :], AF.Identity),
                      reads=[bwf[u]], writes=[bwb[u]])
                for tcc in range(2):
                    ip = tcc % 2
                    ps = ps_q[ip]
                    fns = [(lambda c=c, u=u, tcc=tcc, ps=ps: te.matmul(
                        ps[0][:], wb[u][:, c * 128:(c + 1) * 128],
                        lnA[:, c * HALF + tcc * 512: c * HALF + tcc * 512 + 512],
                        start=(c == 0), stop=(c == NCH - 1))) for c in range(NCH)]
                    cx.group(te, fns, reads=[bwb[u], blnA], writes=[ps[1]])
                    if ti == 2:
                        cx.op(sc, lambda ps=ps, tcc=tcc, j=j: sc.activation(
                            vst[:, tcc * 512:(tcc + 1) * 512], ps[0][:], AF.Identity,
                            bias=self.bias_col("bv", j), scale=1.0),
                            reads=[ps[1], self.bf("smalls")], writes=[bvst])
                    else:
                        cx.op(sc, lambda ps=ps, tcc=tcc, j=j, bias_nm=bias_nm: sc.activation(
                            qraw[:, tcc * 512:(tcc + 1) * 512], ps[0][:], AF.Identity,
                            bias=self.bias_col(bias_nm, j), scale=1.0),
                            reads=[ps[1], self.bf("smalls")], writes=[bqraw])
                if ti == 2:
                    for tl in range(8):
                        ps = ps_q[tl % 2]
                        cx.op(te, lambda tl=tl, ps=ps: te.matmul(
                            ps[0][:, 0:128], vst[:, tl * 128:(tl + 1) * 128],
                            self.ident_bf[:, :], start=True, stop=True),
                            reads=[bvst, self.bf("smalls")], writes=[ps[1]])
                        cx.op(ve, lambda tl=tl, j=j, ps=ps: ve.tensor_copy(
                            vt_sb[:, tl * 2048 + j * 128: tl * 2048 + j * 128 + 128],
                            ps[0][:, 0:128]), reads=[ps[1]], writes=[bvt])
                else:
                    cx.dma(sy, lambda: sy.dma_start(out=qsw[0:64, :], in_=qraw[64:128, :]),
                           reads=[bqraw], writes=[bqsw], ch="e")
                    cx.dma(sy, lambda: sy.dma_start(out=qsw[64:128, :], in_=qraw[0:64, :]),
                           reads=[bqraw], writes=[bqsw], ch="e")
                    cos = tabs["cosq" if ti == 0 else "cosk"]
                    sin = tabs["sinq" if ti == 0 else "sink"]
                    cx.op(ve, lambda cos=cos: ve.tensor_mul(u1, qraw, cos[0]),
                          reads=[bqraw, cos[1]], writes=[bu1])
                    cx.op(ve, lambda sin=sin: ve.tensor_mul(qsw, qsw, sin[0]),
                          reads=[bqsw, sin[1]], writes=[bqsw])
                    if ti == 0:
                        cx.op(ve, lambda j=j, half=half: ve.tensor_add(
                            qrot[:, j * TOWN + half * HALF: j * TOWN + half * HALF + HALF],
                            u1, qsw), reads=[bu1, bqsw], writes=[bqr])
                    else:
                        cx.op(ve, lambda: ve.tensor_add(kst, u1, qsw),
                              reads=[bu1, bqsw], writes=[bkst])
                        cx.dma(sy, lambda j=j, half=half: sy.dma_start(
                            out=kv[j * 128:(j + 1) * 128, half * HALF:(half + 1) * HALF],
                            in_=kst), reads=[bkst], writes=[bkvin], ch="f")
        for tl in range(8):
            cx.dma(sy, lambda tl=tl, half=half: sy.dma_start(
                out=kv[D + half * HALF + tl * 128: D + half * HALF + (tl + 1) * 128, :],
                in_=vt_sb[:, tl * 2048:(tl + 1) * 2048]),
                reads=[bvt], writes=[bkvin], ch="f")

    def attn_inner(self):
        nc, cx = self.nc, self.cx
        sy, sc, ve, te, gp = self.sy, self.sc, self.ve, self.te, self.gp
        qrot, bqr = self.qrot_ap()
        ao = self.v(self.A2, 0, 128, NH * TOWN, BF16)
        bao = self.bf("at_ao")
        kk = [self.v(self.A3, i * 8192, 128, 4096, BF16) for i in range(2)]
        bkk = [self.bf("at_k%d" % i) for i in range(2)]
        vv = [self.v(self.A3, 16384 + i * 8192, 128, 4096, BF16) for i in range(2)]
        bvv = [self.bf("at_v%d" % i) for i in range(2)]
        cm = self.v(self.A3, 32768, 128, 2 * 8 * 512, BF16)
        bcm = self.bf("at_cm")
        pp = [self.v(self.A3, 49152 + i * 2048, 128, 1024, BF16) for i in range(2)]
        bpp = [self.bf("at_p%d" % i) for i in range(2)]
        bc = [self.v(self.A3, 53248 + i * 2048, 128, 512, F32) for i in range(2)]
        bbc = [self.bf("at_bc%d" % i) for i in range(2)]
        dro = self.A3[0:1, 57344:59392].bitcast(F32)
        drb = self.A3[0:1, 59392:60416].bitcast(BF16)
        bdr = self.bf("at_dr")
        bkvo = self.bf("dram_kvout")
        kvo = self.d_kvout.ap()
        cx.dma(sy, lambda: sy.dma_start(out=cm, in_=self.i["cmask"]), writes=[bcm], ch="b")
        with (
            nc.psum_tensor("a_s0", [128, 1024], F32) as ps_s0,
            nc.psum_tensor("a_s1", [128, 1024], F32) as ps_s1,
            nc.psum_tensor("a_o", [128, 512], F32) as ps_o,
            nc.psum_tensor("a_d", [128, 512], F32) as ps_d,
            nc.psum_tensor("a_b", [128, 512], F32) as ps_b,
        ):
            bs = [self.bf("a_s0"), self.bf("a_s1")]
            pss = [ps_s0, ps_s1]
            bo, bd, bb = self.bf("a_o"), self.bf("a_d"), self.bf("a_b")
            def load_head(h):
                u = h % 2
                for g in range(8):
                    r, lc = CHUNK_OWNER[g]
                    cx.dma(sy, lambda h=h, u=u, g=g, r=r, lc=lc: sy.dma_start(
                        out=kk[u][:, g * 512:(g + 1) * 512],
                        in_=kvo[r * 2 * D + h * 128: r * 2 * D + (h + 1) * 128,
                                lc * 512:(lc + 1) * 512]),
                        reads=[bkvo], writes=[bkk[u]], ch="g")
                    cx.dma(sy, lambda h=h, u=u, g=g, r=r, lc=lc: sy.dma_start(
                        out=vv[u][:, g * 512:(g + 1) * 512].rearrange(
                            "p (kb f) -> p kb f", kb=4),
                        in_=kvo[r * 2 * D + D + lc * 512: r * 2 * D + D + (lc + 1) * 512,
                                h * 128:(h + 1) * 128].rearrange(
                            "(kb p) f -> p kb f", p=128)),
                        reads=[bkvo], writes=[bvv[u]], ch="h")
            load_head(0)
            for h in range(NH):
                u = h % 2
                if h + 1 < NH:
                    load_head(h + 1)
                for j in range(NSLOT):
                    npair = EXT[j] * 2
                    qsl = qrot[:, h * TOWN + j * 512: h * TOWN + (j + 1) * 512]
                    for pi in range(npair):
                        sp = pss[pi % 2]
                        bsp = bs[pi % 2]
                        fns = []
                        for half2 in range(2):
                            kb = pi * 2 + half2
                            fns.append(lambda kb=kb, half2=half2, u=u, sp=sp, qsl=qsl:
                                       te.matmul(sp[:, half2 * 512:(half2 + 1) * 512],
                                                 kk[u][:, kb * 128:(kb + 1) * 128],
                                                 qsl, start=True, stop=True))
                        cx.group(te, fns, reads=[bkk[u], bqr], writes=[bsp])
                        pb = pp[pi % 2]
                        bpb = bpp[pi % 2]
                        cx.op(sc, lambda sp=sp, pb=pb: sc.activation(pb, sp[:], AF.Exp),
                              reads=[bsp], writes=[bpb])
                        trail = pi - (npair - 4)
                        if trail >= 0:
                            mo = (j % 2) * 4096 + trail * 1024
                            cx.op(ve, lambda pb=pb, mo=mo: ve.tensor_mul(
                                pb, pb, cm[:, mo:mo + 1024]),
                                reads=[bpb, bcm], writes=[bpb])
                        fns = []
                        for half2 in range(2):
                            kb = pi * 2 + half2
                            fns.append(lambda kb=kb, half2=half2, pb=pb, pi=pi, j=j:
                                       te.matmul(ps_d[:], self.ones_bf[:, :],
                                                 pb[:, half2 * 512:(half2 + 1) * 512],
                                                 start=(pi == 0 and half2 == 0),
                                                 stop=(pi == npair - 1 and half2 == 1)))
                            fns.append(lambda kb=kb, half2=half2, pb=pb, pi=pi, u=u:
                                       te.matmul(ps_o[:], vv[u][:, kb * 128:(kb + 1) * 128],
                                                 pb[:, half2 * 512:(half2 + 1) * 512],
                                                 start=(pi == 0 and half2 == 0),
                                                 stop=(pi == npair - 1 and half2 == 1)))
                        cx.group(te, fns, reads=[bpb, bvv[u], self.bf("smalls")],
                                 writes=[bd, bo])
                    cx.op(ve, lambda: ve.reciprocal(dro[:, 0:512], ps_d[0:1, 0:512]),
                          reads=[bd], writes=[bdr])
                    cx.op(ve, lambda: ve.tensor_copy(drb[:, 0:512], dro[:, 0:512]),
                          reads=[bdr], writes=[bdr])
                    cx.op(te, lambda: te.matmul(ps_b[:], self.ones_bf[0:1, :],
                                                drb[:, 0:512], start=True, stop=True),
                          reads=[bdr, self.bf("smalls")], writes=[bb])
                    ib = j % 2
                    cx.op(ve, lambda ib=ib: ve.tensor_copy(bc[ib], ps_b[:]),
                          reads=[bb], writes=[bbc[ib]])
                    cx.op(ve, lambda ib=ib, h=h, j=j: ve.tensor_tensor(
                        ao[:, h * TOWN + j * 512: h * TOWN + (j + 1) * 512],
                        ps_o[:], bc[ib], op=ALU.mult),
                        reads=[bo, bbc[ib]], writes=[bao])
                    if "aout" in self.dbg:
                        cx.dma(sy, lambda h=h, j=j: sy.dma_start(
                            out=self.dbg["aout"][h * 128:(h + 1) * 128,
                                                 j * 512:(j + 1) * 512],
                            in_=ao[:, h * TOWN + j * 512: h * TOWN + (j + 1) * 512]),
                            reads=[bao], ch="d")

    def attn_wo_combine(self, ps4):
        nc, cx = self.nc, self.cx
        sy, sc, ve, te, gp = self.sy, self.sc, self.ve, self.te, self.gp
        ao = self.v(self.A2, 0, 128, NH * TOWN, BF16)
        bao = self.bf("at_ao")
        inc_bf = self.v(self.A1, 0, 128, NCH * TOWN, BF16)
        binc = self.bf("A1_acc")
        wf = [self.v(self.A3, i * 8192, 128, 2048, F32) for i in range(2)]
        bwf = [self.bf("at_wof%d" % i) for i in range(2)]
        wb = [self.v(self.A3, 16384 + i * 4096, 128, 2048, BF16) for i in range(2)]
        bwb = [self.bf("at_wob%d" % i) for i in range(2)]
        with (
            nc.psum_tensor("w_p0", [128, 512], F32) as pw0,
            nc.psum_tensor("w_p1", [128, 512], F32) as pw1,
        ):
            bpw = [self.bf("w_p0"), self.bf("w_p1")]
            pws = [pw0, pw1]
            for ds in range(NCH):
                u = ds % 2
                cx.dma(sy, lambda ds=ds, u=u: sy.dma_start(
                    out=wf[u], in_=self.i["wo"][ds * 128:(ds + 1) * 128, :]),
                    writes=[bwf[u]], ch="c")
                cx.op(sc, lambda u=u: sc.activation(wb[u][:, :], wf[u][:, :], AF.Identity),
                      reads=[bwf[u]], writes=[bwb[u]])
                for tcc in range(4):
                    ip = tcc % 2
                    fns = [(lambda hh=hh, u=u, tcc=tcc, ip=ip: te.matmul(
                        pws[ip][:], wb[u][:, hh * 128:(hh + 1) * 128],
                        ao[:, hh * TOWN + tcc * 512: hh * TOWN + tcc * 512 + 512],
                        start=(hh == 0), stop=(hh == NH - 1))) for hh in range(NH)]
                    cx.group(te, fns, reads=[bwb[u], bao], writes=[bpw[ip]])
                    cx.op(sc, lambda ds=ds, tcc=tcc, ip=ip: sc.activation(
                        inc_bf[:, ds * TOWN + tcc * 512: ds * TOWN + tcc * 512 + 512],
                        pws[ip][:], AF.Identity, bias=self.bias_col("bo", ds), scale=1.0),
                        reads=[bpw[ip], self.bf("smalls")], writes=[binc])
        self.cx.join([self.sy, self.sc, self.ve, self.te, self.gp])
        dbg_x = self.dbg.get("x2")
        for tt in range(16):
            self.tile_unit_span(tt, self.d_xsp[0], self.bf("dram_xsp0"),
                                self.d_xsp[1], self.bf("dram_xsp1"), inc_bf, binc,
                                TOWN, ps4, do_ln=True, dbg_x=dbg_x, dbg_ln=None)

    def attention(self):
        nc = self.nc
        with (
            nc.psum_tensor("ap_a", [128, 512], F32) as pa,
            nc.psum_tensor("ap_b", [128, 512], F32) as pb,
            nc.psum_tensor("ap_c", [128, 512], F32) as pc,
            nc.psum_tensor("ap_d", [128, 512], F32) as pd,
        ):
            ps4 = [(pa, self.bf("ap_a")), (pb, self.bf("ap_b")),
                   (pc, self.bf("ap_c")), (pd, self.bf("ap_d"))]
            for half in range(2):
                self.load_lnA(half, ps4)
                self.attn_proj(half, ps4, ps_q=[ps4[0], ps4[1]])
            self.cx.cc(self.gp, lambda: self.gp.collective_compute(
                "AllGather", ALU.bypass,
                replica_groups=[[0, 1], [2, 3], [4, 5], [6, 7]],
                ins=[self.d_kvin.ap()], outs=[self.d_kvout.ap()]),
                reads=[self.bf("dram_kvin")], writes=[self.bf("dram_kvout")])
        self.epoch()
        self.attn_inner()
        self.epoch()
        with (
            nc.psum_tensor("aq_a", [128, 512], F32) as pa,
            nc.psum_tensor("aq_b", [128, 512], F32) as pb,
            nc.psum_tensor("aq_c", [128, 512], F32) as pc,
            nc.psum_tensor("aq_d", [128, 512], F32) as pd,
        ):
            ps4 = [(pa, self.bf("aq_a")), (pb, self.bf("aq_b")),
                   (pc, self.bf("aq_c")), (pd, self.bf("aq_d"))]
            self.attn_wo_combine(ps4)

    def epoch(self):
        self.cx.join([self.sy, self.sc, self.ve, self.te, self.gp])

    def build(self):
        nc = self.nc
        self.setup()
        self.epoch()
        with (
            nc.psum_tensor("p_a", [128, 512], F32) as pa,
            nc.psum_tensor("p_b", [128, 512], F32) as pb,
            nc.psum_tensor("p_c", [128, 512], F32) as pc,
            nc.psum_tensor("p_d", [128, 512], F32) as pd,
        ):
            ps4 = [(pa, self.bf("pre_a")), (pb, self.bf("pre_b")),
                   (pc, self.bf("pre_c")), (pd, self.bf("pre_d"))]
            self.pre_phase(ps4)
        if self.dev.startswith("pre"):
            self.finish()
            return self.nc
        self.epoch()
        self.ffn_call(0, self.i["x"], None, self.d_xsp[0], self.bf("dram_xsp0"))
        if self.dev == "ffn1":
            self.finish()
            return self.nc
        self.epoch()
        self.attention()
        if self.dev == "attn":
            self.finish()
            return self.nc
        if self.dev == "fulld":
            pass
        self.epoch()
        self.ffn_call(1, self.d_xsp[1], self.bf("dram_xsp1"), self.o_out, None,
                      last=True)
        self.finish()
        return self.nc

    def finish(self):
        self.cx.join([self.sy])
        self.stk.close()


def kernel(**inputs):
    shared = _host_layouts(inputs)
    in_maps = [{**shared, **_percore(inputs, c)} for c in range(8)]
    b = B(DEV_UPTO)
    nc = b.build()
    res = run_bass_kernel_spmd(nc, in_maps, core_ids=list(range(8)))
    kernel.last_results = res
    out = np.empty((4, SEQ, D), dtype=np.float32)
    for c in range(8):
        bidx, par = c // 2, c % 2
        chunks = CHUNKS_EVEN if par == 0 else CHUNKS_ODD
        shard = res.results[c]["out"]
        for i, g in enumerate(chunks):
            out[bidx, g * 512:(g + 1) * 512, :] = shard[i * 512:(i + 1) * 512, :]
    return out


# version 04
# speedup vs baseline: 1.7038x; 1.0195x over previous
import os
import numpy as np
import ml_dtypes
from contextlib import ExitStack

import concourse.bass as bass
import concourse.mybir as mybir
from concourse.bass_utils import run_bass_kernel_spmd

F32 = mybir.dt.float32
BF16 = mybir.dt.bfloat16
AF = mybir.ActivationFunctionType
ALU = mybir.AluOpType
AXX = mybir.AxisListType.X

D = 2048
NCH = 16
NH = 16
DH = 128
NF = 8
NE = 16
NEX = 18
TOWN = 2048
HALF = 1024
SEQ = 4096
NSLOT = 4
EXT = [2, 4, 6, 8]
CHUNKS_EVEN = [0, 3, 4, 7]
CHUNKS_ODD = [1, 2, 5, 6]
CHUNK_OWNER = {g: (0, i) for i, g in enumerate(CHUNKS_EVEN)}
CHUNK_OWNER.update({g: (1, i) for i, g in enumerate(CHUNKS_ODD)})
GROUPS = [(2 * i, 2 * i + 1) for i in range(9)]

DEV_UPTO = "full"
SKIP = set(os.environ.get("KSKIP", "").split(",")) - {""}


def _host_layouts(inp):
    f32 = np.float32
    out = {}

    def fmw(W):
        return np.ascontiguousarray(
            np.asarray(W).reshape(NCH, 128, NCH, 128).transpose(2, 1, 0, 3).reshape(D, D)
        ).astype(f32, copy=False)

    out["wq"] = fmw(inp["L0_attn_Wq"])
    out["wk"] = fmw(inp["L0_attn_Wk"])
    out["wv"] = fmw(inp["L0_attn_Wv"])
    out["wo"] = fmw(inp["L0_attn_Wo"])

    g_all = np.concatenate([np.asarray(inp["L0_ffn_rg"]), np.asarray(inp["L0_ffn_sg"])], 0)
    w1_all = np.concatenate([np.asarray(inp["L0_ffn_rw1"]), np.asarray(inp["L0_ffn_sw1"])], 0)
    w2_all = np.concatenate([np.asarray(inp["L0_ffn_rw2"]), np.asarray(inp["L0_ffn_sw2"])], 0)

    def gw(g):
        return np.ascontiguousarray(
            g.reshape(NEX, NCH, 128, NF, 128).transpose(0, 3, 2, 1, 4).reshape(NEX * NF * 128, D)
        ).astype(f32, copy=False)

    out["g"] = gw(g_all)
    out["w1"] = gw(w1_all)
    out["w2"] = np.ascontiguousarray(
        w2_all.reshape(NEX, NF, 128, NCH, 128).transpose(3, 2, 0, 1, 4).reshape(D, NEX * NF * 128)
    ).astype(f32, copy=False)
    out["wa"] = np.ascontiguousarray(np.asarray(inp["L0_ffn_wa"])).astype(f32, copy=False)

    gb_all = np.concatenate([np.asarray(inp["L0_ffn_rgb"]), np.asarray(inp["L0_ffn_sgb"])], 0)
    b1_all = np.concatenate([np.asarray(inp["L0_ffn_rb1"]), np.asarray(inp["L0_ffn_sb1"])], 0)
    out["gb"] = np.ascontiguousarray(
        gb_all.reshape(NEX, NF, 128).transpose(2, 0, 1).reshape(128, NEX * NF)).astype(f32, copy=False)
    out["b1"] = np.ascontiguousarray(
        b1_all.reshape(NEX, NF, 128).transpose(2, 0, 1).reshape(128, NEX * NF)).astype(f32, copy=False)
    out["b2"] = np.ascontiguousarray(
        np.concatenate([np.asarray(inp["L0_ffn_rb2"]), np.asarray(inp["L0_ffn_sb2"])], 0)
    ).astype(f32, copy=False)
    for nm, key in [("bq", "L0_attn_bq"), ("bk", "L0_attn_bk"),
                    ("bv", "L0_attn_bv"), ("bo", "L0_attn_bo")]:
        out[nm] = np.ascontiguousarray(np.asarray(inp[key]).reshape(NCH, 128).T).astype(f32, copy=False)
    return out


def _percore(inp, core):
    f32 = np.float32
    b, par = core // 2, core % 2
    chunks = CHUNKS_EVEN if par == 0 else CHUNKS_ODD
    x = np.asarray(inp["L0_attn_x"])
    rows = np.concatenate([np.arange(g * 512, (g + 1) * 512) for g in chunks])
    xs = np.ascontiguousarray(x[b][rows]).astype(f32, copy=False)

    inv = (1.0 / 10000.0 ** (np.arange(0, DH, 2, dtype=f32) / DH)).astype(f32)
    th = np.einsum("t,i->it", rows.astype(f32), inv).astype(f32)
    cos = np.cos(np.concatenate([th, th], 0)).astype(f32)
    sin = np.sin(np.concatenate([th, th], 0)).astype(f32)
    sinf = np.concatenate([-sin[:64], sin[64:]], 0)
    s = np.float32(DH ** -0.5)

    def pattern(kind):
        m = np.zeros((128, 8 * 512), f32)
        for blk in range(8):
            ch, r = blk // 4, blk % 4
            col = blk * 512
            kk = np.arange(128)[:, None] + r * 128
            qq = np.arange(512)[None, :]
            if kind == 0:
                if ch == 0:
                    m[:, col:col + 512] = (qq >= kk)
            else:
                m[:, col:col + 512] = 1.0 if ch == 0 else (qq >= kk)
        return m

    sel = np.zeros((128, 2 * 8 * 512), f32)
    for jj in range(2):
        gq = chunks[jj]
        diag_at = gq - EXT[jj] + 2
        sel[:, jj * 4096:(jj + 1) * 4096] = pattern(1 if diag_at == 1 else 0)
    return {"x": xs, "cosq": np.ascontiguousarray(cos * s), "sinq": np.ascontiguousarray(sinf * s),
            "cosk": np.ascontiguousarray(cos), "sink": np.ascontiguousarray(sinf),
            "cmask": np.ascontiguousarray(sel.astype(ml_dtypes.bfloat16))}


class Sem:
    def __init__(self, nc, name, stack, is_dma=False):
        self.h = stack.enter_context(nc.semaphore(name))
        self.n = 0
        self.name = name
        self.is_dma = is_dma


class Buf:
    __slots__ = ("name", "w", "readers")

    def __init__(self, name):
        self.name = name
        self.w = None
        self.readers = []


class Ctx:

    def __init__(self, nc, stack):
        self.nc = nc
        self.eng_sems = {}
        self.waited = {}
        self.te_id = id(nc.tensor)
        for eng, nm in [(nc.tensor, "te"), (nc.scalar, "sc"), (nc.vector, "ve"),
                        (nc.gpsimd, "gp")]:
            self.eng_sems[id(eng)] = Sem(nc, "e_" + nm, stack)
        self.dma_sems = {nm: Sem(nc, "d_" + nm, stack, is_dma=True) for nm in
                         ["a", "b", "c", "d", "e", "f", "g", "h"]}
        self.cc_sem = Sem(nc, "cc", stack, is_dma=True)

    def _wait(self, eng, semobj, cnt):
        if cnt <= 0:
            return
        if id(eng) == self.te_id and self.eng_sems.get(self.te_id) is semobj:
            return
        if semobj.is_dma:
            cnt = semobj.n
        key = (id(eng), id(semobj))
        if self.waited.get(key, 0) >= cnt:
            return
        eng.wait_ge(semobj.h, cnt)
        self.waited[key] = cnt

    def _pre(self, eng, reads, writes, is_dma):
        need = {}
        for b in reads:
            if b.w:
                s, c = b.w
                need[id(s)] = (s, max(c, need.get(id(s), (s, 0))[1]))
        for b in writes:
            if b.w:
                s, c = b.w
                need[id(s)] = (s, max(c, need.get(id(s), (s, 0))[1]))
            for s, c in b.readers:
                need[id(s)] = (s, max(c, need.get(id(s), (s, 0))[1]))
        for s, c in need.values():
            self._wait(eng, s, c)

    def _post(self, inst, sem, reads, writes, k):
        inst.then_inc(sem.h, k)
        sem.n += k
        for b in writes:
            b.w = (sem, sem.n)
            b.readers = []
        for b in reads:
            b.readers.append((sem, sem.n))
        return sem.n

    def op(self, eng, fn, reads=(), writes=()):
        self._pre(eng, reads, writes, False)
        return self._post(fn(), self.eng_sems[id(eng)], reads, writes, 1)

    def group(self, eng, fns, reads=(), writes=()):
        self._pre(eng, reads, writes, False)
        inst = None
        for f in fns:
            inst = f()
        return self._post(inst, self.eng_sems[id(eng)], reads, writes, 1)

    def dma(self, q, fn, reads=(), writes=(), ch="a"):
        sem = self.dma_sems[ch]
        self._wait(q, sem, sem.n)
        self._pre(q, reads, writes, True)
        return self._post(fn(), sem, reads, writes, 16)

    def cc(self, q, fn, reads=(), writes=()):
        self._pre(q, reads, writes, True)
        return self._post(fn(), self.cc_sem, reads, writes, 1)

    def join(self, engines):
        for e in engines:
            for s in list(self.eng_sems.values()) + list(self.dma_sems.values()) + [self.cc_sem]:
                self._wait(e, s, s.n)


class B:
    def __init__(self, dev="full"):
        self.dev = dev
        nc = bass.Bass("TRN2", target_bir_lowering=False, num_devices=8)
        self.nc = nc
        self.stk = ExitStack()
        self.sy, self.sc, self.ve, self.te, self.gp = nc.sync, nc.scalar, nc.vector, nc.tensor, nc.gpsimd
        self.cx = Ctx(nc, self.stk)

        dt = nc.dram_tensor
        A = lambda name, shape, d=F32, kind="ExternalInput": dt(name, shape, d, kind=kind).ap()
        self.i = {
            "x": A("x", [TOWN, D]),
            "wq": A("wq", [D, D]), "wk": A("wk", [D, D]), "wv": A("wv", [D, D]),
            "wo": A("wo", [D, D]),
            "g": A("g", [NEX * NF * 128, D]), "w1": A("w1", [NEX * NF * 128, D]),
            "w2": A("w2", [D, NEX * NF * 128]),
            "wa": A("wa", [D, NE]),
            "gb": A("gb", [128, NEX * NF]), "b1": A("b1", [128, NEX * NF]),
            "b2": A("b2", [NEX, D]),
            "bq": A("bq", [128, NCH]), "bk": A("bk", [128, NCH]),
            "bv": A("bv", [128, NCH]), "bo": A("bo", [128, NCH]),
            "cosq": A("cosq", [128, TOWN]), "sinq": A("sinq", [128, TOWN]),
            "cosk": A("cosk", [128, TOWN]), "sink": A("sink", [128, TOWN]),
            "cmask": A("cmask", [128, 2 * 8 * 512], BF16),
        }
        self.o_out = A("out", [TOWN, D], F32, "ExternalOutput")
        self.d_xsp = [dt("xsp0", [TOWN, D], F32).ap(), dt("xsp1", [TOWN, D], F32).ap()]
        self.d_ln = dt("lnsp", [TOWN, D], BF16).ap()
        self.d_kvin = dt("kvin", [2 * D, TOWN], BF16)
        self.d_kvout = dt("kvout", [4 * D, TOWN], BF16)

        self.dbg = {}
        if self.dev != "full":
            for nm, shape, d in [("ln1", [TOWN, D], BF16), ("gates", [NEX, TOWN], BF16),
                                 ("x1", [TOWN, D], F32), ("q", [D, TOWN], BF16),
                                 ("aout", [D, TOWN], BF16), ("x2", [TOWN, D], F32),
                                 ("ln2", [TOWN, D], BF16)]:
                self.dbg[nm] = A("dbg_" + nm, shape, d, "ExternalOutput")

        sb = self.stk.enter_context
        U8 = mybir.dt.uint8
        self.A1 = sb(nc.sbuf_tensor("A1", [128, 65536], U8))
        self.A2 = sb(nc.sbuf_tensor("A2", [128, 65536], U8))
        self.A3 = sb(nc.sbuf_tensor("A3", [128, 65536], U8))
        self.SM = sb(nc.sbuf_tensor("SM", [128, 8704], U8))
        self.bufs = {}

        self.bias = self.v(self.SM, 0, 128, 352, F32)
        self.b2e = self.v(self.SM, 1408, NEX, 2048, BF16)
        self.gate_b = self.v(self.SM, 5504, NEX, 1024, BF16)
        self.ones_bf = self.v(self.SM, 7552, 128, 128, BF16)
        self.ident_bf = self.v(self.SM, 7808, 128, 128, BF16)
        self.col = [self.v(self.SM, 8064 + 4 * i, 128, 1, F32) for i in range(4)]
        self.wa_bf = self.v(self.SM, 8192, 128, 256, BF16)

    def bf(self, name):
        if name not in self.bufs:
            self.bufs[name] = Buf(name)
        return self.bufs[name]

    def v(self, arena, off, parts, cols, dtype):
        isz = {F32: 4, BF16: 2}[dtype]
        return arena[0:parts, off:off + cols * isz].bitcast(dtype)

    def bias_col(self, which, idx):
        offs = {"bq": 0, "bk": NCH, "bv": 2 * NCH, "bo": 3 * NCH,
                "gb": 4 * NCH, "b1": 4 * NCH + NEX * NF}
        o = offs[which] + idx
        return self.bias[:, o:o + 1]

    RG = {
        "gwf0": 0, "gwf1": 8192,
        "w2f0": 16384, "w2f1": 20480,
        "gwb0": 24576, "gwb1": 28672,
        "w2b0": 32768, "w2b1": 34816,
        "gbc0": 36864, "gbc1": 38912,
        "ffa0": 40960, "ffa1": 41984,
        "rt": 43008,
        "ct_x0": 45056, "ct_x1": 53248,
        "ct_xb": 61440,
        "gwf2": 45056, "gwf3": 53248,
        "w2f2": 0, "w2f3": 8192,
    }
    RG_ALIAS = {"gwf2": "ct_x0", "gwf3": "ct_x1", "w2f2": "gwf0", "w2f3": "gwf1"}

    def reg(self, name, parts, cols, dtype):
        bufname = "A2_" + self.RG_ALIAS.get(name, name)
        return self.v(self.A2, self.RG[name], parts, cols, dtype), self.bf(bufname)

    def setup(self):
        cx, sy, gp, ve = self.cx, self.sy, self.gp, self.ve
        bS = self.bf("smalls")
        cx.op(gp, lambda: gp.memset(self.ident_bf[:], 1.0), writes=[bS])
        cx.op(gp, lambda: gp.affine_select(self.ident_bf[:], self.ident_bf[:],
                                           pattern=[[1, 128]], compare_op=ALU.is_equal,
                                           fill=0.0, base=0, channel_multiplier=-1),
              reads=[bS], writes=[bS])
        cx.op(ve, lambda: ve.memset(self.ones_bf[:], 1.0), writes=[bS])
        o = 0
        for nm in ["bq", "bk", "bv", "bo", "gb", "b1"]:
            src = self.i[nm]
            w = NCH if nm in ("bq", "bk", "bv", "bo") else NEX * NF
            cx.dma(sy, lambda src=src, o=o, w=w: sy.dma_start(out=self.bias[:, o:o + w], in_=src),
                   writes=[bS], ch="a")
            o += w
        b2f = self.v(self.A3, 0, NEX, 2048, F32)
        bT = self.bf("A3_setup")
        cx.dma(sy, lambda: sy.dma_start(out=b2f, in_=self.i["b2"]), writes=[bT], ch="a")
        cx.op(ve, lambda: ve.tensor_scalar(self.b2e[:, :], b2f, 0.5, None, op0=ALU.mult),
              reads=[bT], writes=[bS])
        rt = self.RG["rt"]
        self.r_abf = self.A2[0:1, rt + 512: rt + 1536].bitcast(BF16)
        wa_f = self.v(self.A3, 16384, 128, 256, F32)
        cx.dma(sy, lambda: sy.dma_start(
            out=wa_f.rearrange("p (c e) -> p c e", c=NCH),
            in_=self.i["wa"].rearrange("(c p) e -> p c e", p=128)), writes=[bT], ch="a")
        cx.op(ve, lambda: ve.tensor_copy(self.wa_bf, wa_f), reads=[bT], writes=[self.bf("wa_bf")])

    def tile_unit(self, tt, x_src, bxsrc, x_dst, bxdst, inc_bf, binc, psums,
                  do_ln, dbg_x=None, dbg_ln=None):
        return self.tile_unit_span(tt, x_src, bxsrc, x_dst, bxdst, inc_bf, binc,
                                   HALF, psums, do_ln, dbg_x, dbg_ln)

    def tile_unit_span(self, tt, x_src, bxsrc, x_dst, bxdst, inc_bf, binc, span,
                       psums, do_ln, dbg_x=None, dbg_ln=None):
        cx = self.cx
        sy, sc, ve, te = self.sy, self.sc, self.ve, self.te
        xt, bxt = self.reg("ct_x%d" % (tt % 2), 128, 2048, F32)
        xb, bxb = self.reg("ct_xb", 128, 2048, BF16)
        sq0, bsq0 = self.reg("ffa0", 128, 512, BF16)
        sq1, bsq1 = self.reg("ffa1", 128, 512, BF16)
        sqs = [(sq0, bsq0), (sq1, bsq1)]
        tloc = tt % (span // 128)
        cx.dma(sy, lambda: sy.dma_start(out=xt, in_=x_src[tt * 128:(tt + 1) * 128, :]),
               reads=[bxsrc] if bxsrc else [], writes=[bxt], ch=["g", "h"][tt % 2])
        if inc_bf is not None:
            for b in range(4):
                fns = []
                for j in range(4):
                    ds = 4 * b + j
                    fns.append(lambda ds=ds, b=b, j=j: te.matmul(
                        psums[b][0][:, j * 128:(j + 1) * 128],
                        inc_bf[:, ds * span + tloc * 128: ds * span + tloc * 128 + 128],
                        self.ident_bf[:, :], start=True, stop=True))
                cx.group(te, fns, reads=[binc, self.bf("smalls")], writes=[psums[b][1]])
            for b in range(4):
                cx.op(ve, lambda b=b: ve.tensor_tensor(
                    xt[:, b * 512:(b + 1) * 512], xt[:, b * 512:(b + 1) * 512],
                    psums[b][0][:, 0:512], op=ALU.add),
                    reads=[psums[b][1], bxt], writes=[bxt])
        if x_dst is not None:
            cx.dma(sy, lambda: sy.dma_start(out=x_dst[tt * 128:(tt + 1) * 128, :], in_=xt),
                   reads=[bxt], writes=[bxdst] if bxdst else [], ch="c")
        if dbg_x is not None:
            cx.dma(sy, lambda: sy.dma_start(out=dbg_x[tt * 128:(tt + 1) * 128, :], in_=xt),
                   reads=[bxt], ch="d")
        if not do_ln:
            return
        cx.op(ve, lambda: ve.tensor_copy(xb, xt), reads=[bxt], writes=[bxb])
        c0, c1, c2, c3 = self.col
        bR = self.bf("cols")
        for ch in range(4):
            sq, bsq = sqs[ch % 2]
            cx.op(sc, lambda ch=ch, sq=sq: sc.activation(
                sq, xb[:, ch * 512:(ch + 1) * 512], AF.Square), reads=[bxb], writes=[bsq])
            if ch == 0:
                cx.op(ve, lambda: ve.reduce_sum(c0, xb[:, 0:512], axis=AXX),
                      reads=[bxb], writes=[bR])
                cx.op(ve, lambda sq=sq: ve.reduce_sum(c1, sq, axis=AXX),
                      reads=[bsq], writes=[bR])
            else:
                cx.op(ve, lambda ch=ch: ve.reduce_sum(c2, xb[:, ch * 512:(ch + 1) * 512],
                                                      axis=AXX), reads=[bxb], writes=[bR])
                cx.op(ve, lambda: ve.tensor_add(c0, c0, c2), reads=[bR], writes=[bR])
                cx.op(ve, lambda sq=sq: ve.reduce_sum(c3, sq, axis=AXX),
                      reads=[bsq], writes=[bR])
                cx.op(ve, lambda: ve.tensor_add(c1, c1, c3), reads=[bR], writes=[bR])
        cx.op(ve, lambda: ve.tensor_scalar(c0, c0, 1.0 / D, None, op0=ALU.mult),
              reads=[bR], writes=[bR])
        cx.op(ve, lambda: ve.tensor_scalar(c1, c1, 1.0 / D, None, op0=ALU.mult),
              reads=[bR], writes=[bR])
        cx.op(ve, lambda: ve.tensor_mul(c2, c0, c0), reads=[bR], writes=[bR])
        cx.op(ve, lambda: ve.tensor_sub(c1, c1, c2), reads=[bR], writes=[bR])
        cx.op(ve, lambda: ve.tensor_scalar(c1, c1, 1e-5, None, op0=ALU.add),
              reads=[bR], writes=[bR])
        cx.op(sc, lambda: sc.activation(c1, c1, AF.Sqrt), reads=[bR], writes=[bR])
        cx.op(ve, lambda: ve.reciprocal(c1, c1), reads=[bR], writes=[bR])
        cx.op(ve, lambda: ve.tensor_mul(c0, c0, c1), reads=[bR], writes=[bR])
        cx.op(ve, lambda: ve.tensor_scalar(c0, c0, -1.0, None, op0=ALU.mult),
              reads=[bR], writes=[bR])
        cx.op(ve, lambda: ve.tensor_scalar(xb, xb, c1, c0, op0=ALU.mult, op1=ALU.add),
              reads=[bR, bxb], writes=[bxb])
        cx.dma(sy, lambda: sy.dma_start(out=self.d_ln[tt * 128:(tt + 1) * 128, :], in_=xb),
               reads=[bxb], writes=[self.bf("dram_ln")], ch="f")
        if dbg_ln is not None:
            cx.dma(sy, lambda: sy.dma_start(out=dbg_ln[tt * 128:(tt + 1) * 128, :], in_=xb),
                   reads=[bxb], ch="d")

    def pre_phase(self, psums):
        for tt in range(16):
            self.tile_unit(tt, self.i["x"], None, None, None, None, None, psums,
                           do_ln=True, dbg_ln=self.dbg.get("ln1"))

    def lnA_ap(self):
        return self.v(self.A3, 0, 128, NCH * HALF, BF16), self.bf("A3_lnA")

    def load_lnA(self, half, psums):
        cx, sy, te, ve = self.cx, self.sy, self.te, self.ve
        lnA, blnA = self.lnA_ap()
        bln = self.bf("dram_ln")
        for i in range(8):
            tt = half * 8 + i
            st, bst = self.reg("ct_x%d" % (i % 2), 128, 4096, BF16)
            cx.dma(sy, lambda tt=tt, st=st: sy.dma_start(
                out=st[:, 0:2048], in_=self.d_ln[tt * 128:(tt + 1) * 128, :]),
                reads=[bln], writes=[bst], ch=["g", "h"][i % 2])
            for b in range(4):
                fns = []
                for j in range(4):
                    ds = 4 * b + j
                    fns.append(lambda ds=ds, b=b, j=j, st=st: te.matmul(
                        psums[b][0][:, j * 128:(j + 1) * 128],
                        st[:, ds * 128:(ds + 1) * 128],
                        self.ident_bf[:, :], start=True, stop=True))
                cx.group(te, fns, reads=[bst, self.bf("smalls")], writes=[psums[b][1]])
            for b in range(4):
                for j in range(4):
                    ds = 4 * b + j
                    cx.op(ve, lambda ds=ds, b=b, j=j, i=i: ve.tensor_copy(
                        lnA[:, ds * HALF + i * 128: ds * HALF + i * 128 + 128],
                        psums[b][0][:, j * 128:(j + 1) * 128]),
                        reads=[psums[b][1]], writes=[blnA])

    def router(self, half, ps_y0, ps_y1):
        nc, cx = self.nc, self.cx
        sy, sc, ve, te = self.sy, self.sc, self.ve, self.te
        lnA, blnA = self.lnA_ap()
        rt = self.RG["rt"]
        aff = self.v(self.A2, rt, 128, 18, F32)
        eq1 = self.v(self.A2, rt + 80, 128, 18, F32)
        eq2 = self.v(self.A2, rt + 160, 128, 18, F32)
        G = self.v(self.A2, rt + 240, 128, 32, BF16)
        Gf = self.v(self.A2, rt + 368, 128, 18, F32)
        m1 = self.v(self.A2, rt + 304, 128, 1, F32)
        m2 = self.v(self.A2, rt + 312, 128, 1, F32)
        p1 = self.v(self.A2, rt + 320, 128, 1, F32)
        p2 = self.v(self.A2, rt + 328, 128, 1, F32)
        ss = self.v(self.A2, rt + 336, 128, 1, F32)
        brt = self.bf("A2_rt")
        bgb = self.bf("gate_b")
        ps_r = ps_y0[:, 0:16]
        ps_g = ps_y1[0:NEX, 0:128]
        bpr, bpg = self.bf("ps_y0"), self.bf("ps_y1")
        for tb in range(8):
            cx.group(te, [(lambda c=c, tb=tb: te.matmul(
                ps_r, lnA[:, c * HALF + tb * 128: c * HALF + tb * 128 + 128],
                self.wa_bf[:, c * 16:(c + 1) * 16],
                start=(c == 0), stop=(c == NCH - 1))) for c in range(NCH)],
                reads=[blnA, self.bf("wa_bf")], writes=[bpr])
            cx.op(sc, lambda: sc.activation(aff[:, 0:16], ps_r, AF.Sigmoid),
                  reads=[bpr], writes=[brt])
            for f in [
                lambda: ve.reduce_max(m1, aff[:, 0:16], axis=AXX),
                lambda: ve.tensor_scalar(eq1[:, 0:16], aff[:, 0:16], m1, None, op0=ALU.is_equal),
                lambda: ve.scalar_tensor_tensor(eq2[:, 0:16], eq1[:, 0:16], -1e9, aff[:, 0:16],
                                                op0=ALU.mult, op1=ALU.add),
                lambda: ve.reduce_max(m2, eq2[:, 0:16], axis=AXX),
                lambda: ve.tensor_add(ss, m1, m2),
                lambda: ve.reciprocal(ss, ss),
                lambda: ve.tensor_mul(p1, m1, ss),
                lambda: ve.tensor_mul(p2, m2, ss),
                lambda: ve.tensor_scalar(eq2[:, 0:16], eq2[:, 0:16], m2, None, op0=ALU.is_equal),
                lambda: ve.tensor_scalar(Gf[:, 0:16], eq1[:, 0:16], p1, None, op0=ALU.mult),
                lambda: ve.scalar_tensor_tensor(G[:, 0:16], eq2[:, 0:16], p2, Gf[:, 0:16],
                                                op0=ALU.mult, op1=ALU.add),
                lambda: ve.memset(G[:, 16:18], 1.0),
            ]:
                cx.op(ve, f, reads=[brt], writes=[brt])
            cx.op(te, lambda: te.matmul(ps_g, G[:, 0:18], self.ident_bf[:, :],
                                        start=True, stop=True),
                  reads=[brt, self.bf("smalls")], writes=[bpg])
            cx.op(ve, lambda tb=tb: ve.tensor_copy(
                self.gate_b[:, tb * 128:(tb + 1) * 128], ps_g),
                reads=[bpg], writes=[bgb])
            if "gates" in self.dbg:
                cx.dma(sy, lambda tb=tb, half=half: sy.dma_start(
                    out=self.dbg["gates"][0:NEX, half * HALF + tb * 128: half * HALF + tb * 128 + 128],
                    in_=self.gate_b[:, tb * 128:(tb + 1) * 128]), reads=[bgb], ch="d")

    def acc_ap(self):
        return self.v(self.A1, 0, 128, NCH * HALF, F32), self.bf("A1_acc")

    def acc_init(self, half, ps0, b0, ps1, b1):
        cx, ve, te = self.cx, self.ve, self.te
        lnA, blnA = self.lnA_ap()
        acc, bacc = self.acc_ap()
        bp = [b0, b1]
        for ds in range(NCH):
            for tcc in range(2):
                ip = (ds * 2 + tcc) % 2
                ps = ps0 if ip == 0 else ps1
                cx.op(te, lambda ds=ds, tcc=tcc, ps=ps: te.matmul(
                    ps[:], self.b2e[:, ds * 128:(ds + 1) * 128],
                    self.gate_b[:, tcc * 512:(tcc + 1) * 512], start=True, stop=True),
                    reads=[self.bf("smalls"), self.bf("gate_b")], writes=[bp[ip]])
                cx.op(ve, lambda ds=ds, tcc=tcc, ps=ps: ve.scalar_tensor_tensor(
                    acc[:, ds * HALF + tcc * 512: ds * HALF + tcc * 512 + 512],
                    lnA[:, ds * HALF + tcc * 512: ds * HALF + tcc * 512 + 512],
                    0.5, ps[:], op0=ALU.mult, op1=ALU.add),
                    reads=[bp[ip], blnA], writes=[bacc])

    def expert_group(self, gi, psums):
        (ps_G0, ps_G1, ps_W0, ps_W1, ps_y0, ps_y1) = psums
        cx = self.cx
        sy, sc, ve, te, gp = self.sy, self.sc, self.ve, self.te, self.gp
        e0, e1 = GROUPS[gi]
        routed = e0 < NE
        lnA, blnA = self.lnA_ap()
        acc, bacc = self.acc_ap()
        yh = self.v(self.A3, 32768, 128, 16 * HALF, BF16)
        byh = self.bf("A3_yh")
        gwf, bgwf, gwb, bgwb, w2f, bw2f, w2b, bw2b, gbc, bgbc, ffa, bffa = \
            [], [], [], [], [], [], [], [], [], [], [], []
        for u in range(4):
            a, b_ = self.reg(f"gwf{u}", 128, 2048, F32); gwf.append(a); bgwf.append(b_)
            a, b_ = self.reg(f"w2f{u}", 128, 1024, F32); w2f.append(a); bw2f.append(b_)
        for u in range(2):
            a, b_ = self.reg(f"gwb{u}", 128, 2048, BF16); gwb.append(a); bgwb.append(b_)
            a, b_ = self.reg(f"w2b{u}", 128, 1024, BF16); w2b.append(a); bw2b.append(b_)
            a, b_ = self.reg(f"gbc{u}", 128, 1024, BF16); gbc.append(a); bgbc.append(b_)
            a, b_ = self.reg(f"ffa{u}", 128, 512, BF16); ffa.append(a); bffa.append(b_)
        pool = [((ps_G0, self.bf("ps_G0")), (ps_G1, self.bf("ps_G1"))),
                ((ps_W0, self.bf("ps_W0")), (ps_W1, self.bf("ps_W1"))),
                ((ps_y0, self.bf("ps_y0")), (ps_y1, self.bf("ps_y1")))]
        if routed:
            for k, e in enumerate((e0, e1)):
                for tcc in range(2):
                    row = self.r_abf
                    pb = pool[(k * 2 + tcc) % 3][0]
                    cx.dma(sy, lambda e=e, tcc=tcc: sy.dma_start(
                        out=row[:, 0:512],
                        in_=self.gate_b[e:e + 1, tcc * 512:(tcc + 1) * 512]),
                        reads=[self.bf("gate_b")], writes=[self.bf("rows")], ch="b")
                    cx.op(te, lambda pb=pb: te.matmul(pb[0][:], self.ones_bf[0:1, :],
                                                      row[:, 0:512], start=True, stop=True),
                          reads=[self.bf("rows"), self.bf("smalls")], writes=[pb[1]])
                    cx.op(ve, lambda k=k, tcc=tcc, pb=pb: ve.tensor_copy(
                        gbc[k][:, tcc * 512:(tcc + 1) * 512], pb[0][:]),
                        reads=[pb[1]], writes=[bgbc[k]])
        for si in range(16):
            ei = (e0, e1)[si // 8]
            fs = si % 8
            row0 = (ei * NF + fs) * 128
            ctx = [pool[(si * 2 + tcc) % 3] for tcc in range(2)]
            for dh2 in range(2):
                u = (si * 2 + dh2) % 4
                u2 = (si * 2 + dh2) % 2
                cx.dma(sy, lambda row0=row0, dh2=dh2, u=u: sy.dma_start(
                    out=gwf[u][:, 0:1024],
                    in_=self.i["g"][row0:row0 + 128, dh2 * 1024:(dh2 + 1) * 1024]),
                    writes=[bgwf[u]], ch=["c", "f"][u % 2])
                cx.dma(sy, lambda row0=row0, dh2=dh2, u=u: sy.dma_start(
                    out=gwf[u][:, 1024:2048],
                    in_=self.i["w1"][row0:row0 + 128, dh2 * 1024:(dh2 + 1) * 1024]),
                    writes=[bgwf[u]], ch=["d", "h"][u % 2])
                cx.op(sc, lambda u=u, u2=u2: sc.activation(gwb[u2][:, :], gwf[u][:, :],
                                                           AF.Identity),
                      reads=[bgwf[u]], writes=[bgwb[u2]])
                for tcc in range(2):
                    (psG, bG_), (psW, bW_) = ctx[tcc]
                    fns = []
                    for cc in range(8):
                        c = dh2 * 8 + cc
                        fns.append(lambda cc=cc, c=c, u2=u2, tcc=tcc, psG=psG: te.matmul(
                            psG[:], gwb[u2][:, cc * 128:(cc + 1) * 128],
                            lnA[:, c * HALF + tcc * 512: c * HALF + tcc * 512 + 512],
                            start=(c == 0), stop=(c == NCH - 1)))
                        fns.append(lambda cc=cc, c=c, u2=u2, tcc=tcc, psW=psW: te.matmul(
                            psW[:], gwb[u2][:, 1024 + cc * 128: 1024 + (cc + 1) * 128],
                            lnA[:, c * HALF + tcc * 512: c * HALF + tcc * 512 + 512],
                            start=(c == 0), stop=(c == NCH - 1)))
                    cx.group(te, fns, reads=[bgwb[u2], blnA], writes=[bG_, bW_])
            for tcc in range(2):
                (psG, bG_), (psW, bW_) = ctx[tcc]
                ysl = yh[:, si * HALF + tcc * 512: si * HALF + tcc * 512 + 512]
                cx.op(sc, lambda psG=psG, ysl=ysl, ei=ei, fs=fs: sc.activation(
                    ysl, psG[:], AF.Gelu, bias=self.bias_col("gb", ei * NF + fs), scale=1.0),
                    reads=[bG_, self.bf("smalls")], writes=[byh])
                cx.op(ve, lambda psW=psW, tcc=tcc, ei=ei, fs=fs: ve.tensor_scalar(
                    ffa[tcc][:, :], psW[:], self.bias_col("b1", ei * NF + fs), None, op0=ALU.add),
                    reads=[bW_, self.bf("smalls")], writes=[bffa[tcc]])
                if routed:
                    k = si // 8
                    cx.op(ve, lambda tcc=tcc, k=k: ve.tensor_tensor(
                        ffa[tcc][:, :], ffa[tcc][:, :],
                        gbc[k][:, tcc * 512:(tcc + 1) * 512], op=ALU.mult),
                        reads=[bffa[tcc], bgbc[k]], writes=[bffa[tcc]])
                cx.op(ve, lambda tcc=tcc, ysl=ysl: ve.tensor_tensor(
                    ysl, ysl, ffa[tcc][:, :], op=ALU.mult),
                    reads=[bffa[tcc], byh], writes=[byh])
        col0 = e0 * NF * 128
        for ds in range(NCH):
            ctx = [pool[(ds * 2 + tcc) % 3] for tcc in range(2)]
            for w2h in range(2):
                u = (ds * 2 + w2h) % 4
                u2 = (ds * 2 + w2h) % 2
                cx.dma(sy, lambda ds=ds, w2h=w2h, u=u, col0=col0: sy.dma_start(
                    out=w2f[u][:, :],
                    in_=self.i["w2"][ds * 128:(ds + 1) * 128,
                                     col0 + w2h * 1024: col0 + (w2h + 1) * 1024]),
                    writes=[bw2f[u]], ch=["e", "a", "b", "g"][u])
                cx.op(sc, lambda u=u, u2=u2: sc.activation(w2b[u2][:, :], w2f[u][:, :],
                                                           AF.Identity, scale=0.5),
                      reads=[bw2f[u]], writes=[bw2b[u2]])
                for tcc in range(2):
                    psY, bY_ = ctx[tcc][0]
                    fns = []
                    for blk in range(8):
                        gblk = w2h * 8 + blk
                        fns.append(lambda blk=blk, gblk=gblk, u2=u2, tcc=tcc, psY=psY: te.matmul(
                            psY[:], w2b[u2][:, blk * 128:(blk + 1) * 128],
                            yh[:, gblk * HALF + tcc * 512: gblk * HALF + tcc * 512 + 512],
                            start=(gblk == 0), stop=(gblk == 15)))
                    cx.group(te, fns, reads=[bw2b[u2], byh], writes=[bY_])
            for tcc in range(2):
                psY, bY_ = ctx[tcc][0]
                asl = acc[:, ds * HALF + tcc * 512: ds * HALF + tcc * 512 + 512]
                cx.op(ve, lambda psY=psY, asl=asl: ve.tensor_tensor(asl, asl, psY[:], op=ALU.add),
                      reads=[bY_], writes=[bacc])

    def combine(self, half, call, xsp_in, bspin, xsp_out, bspout, psums,
                last=False, dbg_key=None):
        cx, ve = self.cx, self.ve
        acc, bacc = self.acc_ap()
        inc_bf = self.v(self.A3, 32768, 128, NCH * HALF, BF16)
        binc = self.bf("A3_yh")
        for q in range(4):
            cx.op(ve, lambda q=q: ve.tensor_copy(
                inc_bf[:, q * 4 * HALF:(q + 1) * 4 * HALF],
                acc[:, q * 4 * HALF:(q + 1) * 4 * HALF]), reads=[bacc], writes=[binc])
        dbg_x = self.dbg.get(dbg_key) if dbg_key else None
        dbg_ln = self.dbg.get("ln2") if (call == 0 and not last) else None
        for i in range(8):
            tt = half * 8 + i
            self.tile_unit(tt, xsp_in, bspin, xsp_out, bspout, inc_bf, binc,
                           psums, do_ln=(not last), dbg_x=dbg_x, dbg_ln=dbg_ln)

    def ffn_call(self, call, xsp_in, bspin, xsp_out, bspout, last=False):
        nc = self.nc
        with (
            nc.psum_tensor("f%d_G0" % call, [128, 512], F32) as ps_G0,
            nc.psum_tensor("f%d_G1" % call, [128, 512], F32) as ps_G1,
            nc.psum_tensor("f%d_W0" % call, [128, 512], F32) as ps_W0,
            nc.psum_tensor("f%d_W1" % call, [128, 512], F32) as ps_W1,
            nc.psum_tensor("f%d_y0" % call, [128, 512], F32) as ps_y0,
            nc.psum_tensor("f%d_y1" % call, [128, 512], F32) as ps_y1,
        ):
            psums = (ps_G0, ps_G1, ps_W0, ps_W1, ps_y0, ps_y1)
            ps4 = [(ps_G0, self.bf("ps_G0")), (ps_G1, self.bf("ps_G1")),
                   (ps_W0, self.bf("ps_W0")), (ps_W1, self.bf("ps_W1"))]
            for half in range(2):
                self.load_lnA(half, ps4)
                self.router(half, ps_y0, ps_y1)
                self.acc_init(half, ps_y0, self.bf("ps_y0"), ps_y1, self.bf("ps_y1"))
                for gi in range(9):
                    self.expert_group(gi, psums)
                self.combine(half, call, xsp_in, bspin, xsp_out, bspout, ps4,
                             last=last, dbg_key=("x1" if call == 0 else None))


    def qrot_ap(self):
        return self.v(self.A1, 0, 128, NH * TOWN, BF16), self.bf("A1_acc")

    def attn_proj(self, half, ps4, ps_q):
        nc, cx = self.nc, self.cx
        sy, sc, ve, te, gp = self.sy, self.sc, self.ve, self.te, self.gp
        lnA, blnA = self.lnA_ap()
        qrot, bqr = self.qrot_ap()
        wf = [self.v(self.A2, i * 8192, 128, 2048, F32) for i in range(2)]
        bwf = [self.bf("A2_gwf%d" % i) for i in range(2)]
        wb = [self.v(self.A2, 16384 + i * 4096, 128, 2048, BF16) for i in range(2)]
        bwb = [self.bf("at_wb%d" % i) for i in range(2)]
        tabs = {}
        for ti, nm in enumerate(["cosq", "sinq", "cosk", "sink"]):
            tabs[nm] = (self.v(self.A2, 24576 + ti * 4096, 128, 1024, F32),
                        self.bf("at_" + nm))
        qraw = self.v(self.A2, 40960, 128, 1024, F32)
        bqraw = self.bf("at_qraw")
        qsw = self.v(self.A2, 45056, 128, 1024, F32)
        bqsw = self.bf("at_qsw")
        u1 = self.v(self.A2, 49152, 128, 1024, F32)
        bu1 = self.bf("at_u1")
        kst = self.v(self.A2, 53248, 128, 1024, BF16)
        bkst = self.bf("at_kst")
        vst = self.v(self.A2, 55296, 128, 1024, BF16)
        bvst = self.bf("at_vst")
        vt_sb = self.v(self.A3, 32768, 128, 8 * 2048, BF16)
        bvt = self.bf("A3_yh")
        bkvin = self.bf("dram_kvin")
        kv = self.d_kvin.ap()
        for nm in ["cosq", "sinq", "cosk", "sink"]:
            cx.dma(sy, lambda nm=nm: sy.dma_start(
                out=tabs[nm][0], in_=self.i[nm][:, half * HALF:(half + 1) * HALF]),
                writes=[tabs[nm][1]], ch="b")
        for j in range(NH):
            for ti, (wnm, bias_nm) in enumerate([("wq", "bq"), ("wk", "bk"), ("wv", "bv")]):
                u = (j * 3 + ti) % 2
                cx.dma(sy, lambda wnm=wnm, j=j, u=u: sy.dma_start(
                    out=wf[u], in_=self.i[wnm][j * 128:(j + 1) * 128, :]),
                    writes=[bwf[u]], ch="c" if ti != 1 else "d")
                cx.op(sc, lambda u=u: sc.activation(wb[u][:, :], wf[u][:, :], AF.Identity),
                      reads=[bwf[u]], writes=[bwb[u]])
                for tcc in range(2):
                    ip = tcc % 2
                    ps = ps_q[ip]
                    fns = [(lambda c=c, u=u, tcc=tcc, ps=ps: te.matmul(
                        ps[0][:], wb[u][:, c * 128:(c + 1) * 128],
                        lnA[:, c * HALF + tcc * 512: c * HALF + tcc * 512 + 512],
                        start=(c == 0), stop=(c == NCH - 1))) for c in range(NCH)]
                    cx.group(te, fns, reads=[bwb[u], blnA], writes=[ps[1]])
                    if ti == 2:
                        cx.op(sc, lambda ps=ps, tcc=tcc, j=j: sc.activation(
                            vst[:, tcc * 512:(tcc + 1) * 512], ps[0][:], AF.Identity,
                            bias=self.bias_col("bv", j), scale=1.0),
                            reads=[ps[1], self.bf("smalls")], writes=[bvst])
                    else:
                        cx.op(sc, lambda ps=ps, tcc=tcc, j=j, bias_nm=bias_nm: sc.activation(
                            qraw[:, tcc * 512:(tcc + 1) * 512], ps[0][:], AF.Identity,
                            bias=self.bias_col(bias_nm, j), scale=1.0),
                            reads=[ps[1], self.bf("smalls")], writes=[bqraw])
                if ti == 2:
                    for tl in range(8):
                        ps = ps_q[tl % 2]
                        cx.op(te, lambda tl=tl, ps=ps: te.matmul(
                            ps[0][:, 0:128], vst[:, tl * 128:(tl + 1) * 128],
                            self.ident_bf[:, :], start=True, stop=True),
                            reads=[bvst, self.bf("smalls")], writes=[ps[1]])
                        cx.op(ve, lambda tl=tl, j=j, ps=ps: ve.tensor_copy(
                            vt_sb[:, tl * 2048 + j * 128: tl * 2048 + j * 128 + 128],
                            ps[0][:, 0:128]), reads=[ps[1]], writes=[bvt])
                else:
                    cx.dma(sy, lambda: sy.dma_start(out=qsw[0:64, :], in_=qraw[64:128, :]),
                           reads=[bqraw], writes=[bqsw], ch="e")
                    cx.dma(sy, lambda: sy.dma_start(out=qsw[64:128, :], in_=qraw[0:64, :]),
                           reads=[bqraw], writes=[bqsw], ch="e")
                    cos = tabs["cosq" if ti == 0 else "cosk"]
                    sin = tabs["sinq" if ti == 0 else "sink"]
                    cx.op(ve, lambda cos=cos: ve.tensor_mul(u1, qraw, cos[0]),
                          reads=[bqraw, cos[1]], writes=[bu1])
                    cx.op(ve, lambda sin=sin: ve.tensor_mul(qsw, qsw, sin[0]),
                          reads=[bqsw, sin[1]], writes=[bqsw])
                    if ti == 0:
                        cx.op(ve, lambda j=j, half=half: ve.tensor_add(
                            qrot[:, j * TOWN + half * HALF: j * TOWN + half * HALF + HALF],
                            u1, qsw), reads=[bu1, bqsw], writes=[bqr])
                    else:
                        cx.op(ve, lambda: ve.tensor_add(kst, u1, qsw),
                              reads=[bu1, bqsw], writes=[bkst])
                        cx.dma(sy, lambda j=j, half=half: sy.dma_start(
                            out=kv[j * 128:(j + 1) * 128, half * HALF:(half + 1) * HALF],
                            in_=kst), reads=[bkst], writes=[bkvin], ch="f")
        for tl in range(8):
            cx.dma(sy, lambda tl=tl, half=half: sy.dma_start(
                out=kv[D + half * HALF + tl * 128: D + half * HALF + (tl + 1) * 128, :],
                in_=vt_sb[:, tl * 2048:(tl + 1) * 2048]),
                reads=[bvt], writes=[bkvin], ch="f")

    def attn_inner(self):
        nc, cx = self.nc, self.cx
        sy, sc, ve, te, gp = self.sy, self.sc, self.ve, self.te, self.gp
        qrot, bqr = self.qrot_ap()
        ao = self.v(self.A2, 0, 128, NH * TOWN, BF16)
        bao = self.bf("at_ao")
        kk = [self.v(self.A3, i * 8192, 128, 4096, BF16) for i in range(2)]
        bkk = [self.bf("at_k%d" % i) for i in range(2)]
        vv = [self.v(self.A3, 16384 + i * 8192, 128, 4096, BF16) for i in range(2)]
        bvv = [self.bf("at_v%d" % i) for i in range(2)]
        cm = self.v(self.A3, 32768, 128, 2 * 8 * 512, BF16)
        bcm = self.bf("at_cm")
        pp = [self.v(self.A3, 49152 + i * 2048, 128, 1024, BF16) for i in range(2)]
        bpp = [self.bf("at_p%d" % i) for i in range(2)]
        bc = [self.v(self.A3, 53248 + i * 2048, 128, 512, F32) for i in range(2)]
        bbc = [self.bf("at_bc%d" % i) for i in range(2)]
        dro = self.A3[0:1, 57344:59392].bitcast(F32)
        drb = self.A3[0:1, 59392:60416].bitcast(BF16)
        bdr = self.bf("at_dr")
        bkvo = self.bf("dram_kvout")
        kvo = self.d_kvout.ap()
        cx.dma(sy, lambda: sy.dma_start(out=cm, in_=self.i["cmask"]), writes=[bcm], ch="b")
        with (
            nc.psum_tensor("a_s0", [128, 1024], F32) as ps_s0,
            nc.psum_tensor("a_s1", [128, 1024], F32) as ps_s1,
            nc.psum_tensor("a_o", [128, 512], F32) as ps_o,
            nc.psum_tensor("a_d", [128, 512], F32) as ps_d,
            nc.psum_tensor("a_b", [128, 512], F32) as ps_b,
        ):
            bs = [self.bf("a_s0"), self.bf("a_s1")]
            pss = [ps_s0, ps_s1]
            bo, bd, bb = self.bf("a_o"), self.bf("a_d"), self.bf("a_b")
            def load_head(h):
                u = h % 2
                for g in range(8):
                    r, lc = CHUNK_OWNER[g]
                    cx.dma(sy, lambda h=h, u=u, g=g, r=r, lc=lc: sy.dma_start(
                        out=kk[u][:, g * 512:(g + 1) * 512],
                        in_=kvo[r * 2 * D + h * 128: r * 2 * D + (h + 1) * 128,
                                lc * 512:(lc + 1) * 512]),
                        reads=[bkvo], writes=[bkk[u]], ch="g")
                    cx.dma(sy, lambda h=h, u=u, g=g, r=r, lc=lc: sy.dma_start(
                        out=vv[u][:, g * 512:(g + 1) * 512].rearrange(
                            "p (kb f) -> p kb f", kb=4),
                        in_=kvo[r * 2 * D + D + lc * 512: r * 2 * D + D + (lc + 1) * 512,
                                h * 128:(h + 1) * 128].rearrange(
                            "(kb p) f -> p kb f", p=128)),
                        reads=[bkvo], writes=[bvv[u]], ch="h")
            load_head(0)
            for h in range(NH):
                u = h % 2
                if h + 1 < NH:
                    load_head(h + 1)
                for j in range(NSLOT):
                    npair = EXT[j] * 2
                    qsl = qrot[:, h * TOWN + j * 512: h * TOWN + (j + 1) * 512]
                    for pi in range(npair):
                        sp = pss[pi % 2]
                        bsp = bs[pi % 2]
                        fns = []
                        for half2 in range(2):
                            kb = pi * 2 + half2
                            fns.append(lambda kb=kb, half2=half2, u=u, sp=sp, qsl=qsl:
                                       te.matmul(sp[:, half2 * 512:(half2 + 1) * 512],
                                                 kk[u][:, kb * 128:(kb + 1) * 128],
                                                 qsl, start=True, stop=True))
                        cx.group(te, fns, reads=[bkk[u], bqr], writes=[bsp])
                        pb = pp[pi % 2]
                        bpb = bpp[pi % 2]
                        cx.op(sc, lambda sp=sp, pb=pb: sc.activation(pb, sp[:], AF.Exp),
                              reads=[bsp], writes=[bpb])
                        trail = pi - (npair - 4)
                        if trail >= 0:
                            mo = (j % 2) * 4096 + trail * 1024
                            cx.op(ve, lambda pb=pb, mo=mo: ve.tensor_mul(
                                pb, pb, cm[:, mo:mo + 1024]),
                                reads=[bpb, bcm], writes=[bpb])
                        fns = []
                        for half2 in range(2):
                            kb = pi * 2 + half2
                            fns.append(lambda kb=kb, half2=half2, pb=pb, pi=pi, j=j:
                                       te.matmul(ps_d[:], self.ones_bf[:, :],
                                                 pb[:, half2 * 512:(half2 + 1) * 512],
                                                 start=(pi == 0 and half2 == 0),
                                                 stop=(pi == npair - 1 and half2 == 1)))
                            fns.append(lambda kb=kb, half2=half2, pb=pb, pi=pi, u=u:
                                       te.matmul(ps_o[:], vv[u][:, kb * 128:(kb + 1) * 128],
                                                 pb[:, half2 * 512:(half2 + 1) * 512],
                                                 start=(pi == 0 and half2 == 0),
                                                 stop=(pi == npair - 1 and half2 == 1)))
                        cx.group(te, fns, reads=[bpb, bvv[u], self.bf("smalls")],
                                 writes=[bd, bo])
                    cx.op(ve, lambda: ve.reciprocal(dro[:, 0:512], ps_d[0:1, 0:512]),
                          reads=[bd], writes=[bdr])
                    cx.op(ve, lambda: ve.tensor_copy(drb[:, 0:512], dro[:, 0:512]),
                          reads=[bdr], writes=[bdr])
                    cx.op(te, lambda: te.matmul(ps_b[:], self.ones_bf[0:1, :],
                                                drb[:, 0:512], start=True, stop=True),
                          reads=[bdr, self.bf("smalls")], writes=[bb])
                    ib = j % 2
                    cx.op(ve, lambda ib=ib: ve.tensor_copy(bc[ib], ps_b[:]),
                          reads=[bb], writes=[bbc[ib]])
                    cx.op(ve, lambda ib=ib, h=h, j=j: ve.tensor_tensor(
                        ao[:, h * TOWN + j * 512: h * TOWN + (j + 1) * 512],
                        ps_o[:], bc[ib], op=ALU.mult),
                        reads=[bo, bbc[ib]], writes=[bao])
                    if "aout" in self.dbg:
                        cx.dma(sy, lambda h=h, j=j: sy.dma_start(
                            out=self.dbg["aout"][h * 128:(h + 1) * 128,
                                                 j * 512:(j + 1) * 512],
                            in_=ao[:, h * TOWN + j * 512: h * TOWN + (j + 1) * 512]),
                            reads=[bao], ch="d")

    def attn_wo_combine(self, ps4):
        nc, cx = self.nc, self.cx
        sy, sc, ve, te, gp = self.sy, self.sc, self.ve, self.te, self.gp
        ao = self.v(self.A2, 0, 128, NH * TOWN, BF16)
        bao = self.bf("at_ao")
        inc_bf = self.v(self.A1, 0, 128, NCH * TOWN, BF16)
        binc = self.bf("A1_acc")
        wf = [self.v(self.A3, i * 8192, 128, 2048, F32) for i in range(2)]
        bwf = [self.bf("at_wof%d" % i) for i in range(2)]
        wb = [self.v(self.A3, 16384 + i * 4096, 128, 2048, BF16) for i in range(2)]
        bwb = [self.bf("at_wob%d" % i) for i in range(2)]
        with (
            nc.psum_tensor("w_p0", [128, 512], F32) as pw0,
            nc.psum_tensor("w_p1", [128, 512], F32) as pw1,
        ):
            bpw = [self.bf("w_p0"), self.bf("w_p1")]
            pws = [pw0, pw1]
            for ds in range(NCH):
                u = ds % 2
                cx.dma(sy, lambda ds=ds, u=u: sy.dma_start(
                    out=wf[u], in_=self.i["wo"][ds * 128:(ds + 1) * 128, :]),
                    writes=[bwf[u]], ch="c")
                cx.op(sc, lambda u=u: sc.activation(wb[u][:, :], wf[u][:, :], AF.Identity),
                      reads=[bwf[u]], writes=[bwb[u]])
                for tcc in range(4):
                    ip = tcc % 2
                    fns = [(lambda hh=hh, u=u, tcc=tcc, ip=ip: te.matmul(
                        pws[ip][:], wb[u][:, hh * 128:(hh + 1) * 128],
                        ao[:, hh * TOWN + tcc * 512: hh * TOWN + tcc * 512 + 512],
                        start=(hh == 0), stop=(hh == NH - 1))) for hh in range(NH)]
                    cx.group(te, fns, reads=[bwb[u], bao], writes=[bpw[ip]])
                    cx.op(sc, lambda ds=ds, tcc=tcc, ip=ip: sc.activation(
                        inc_bf[:, ds * TOWN + tcc * 512: ds * TOWN + tcc * 512 + 512],
                        pws[ip][:], AF.Identity, bias=self.bias_col("bo", ds), scale=1.0),
                        reads=[bpw[ip], self.bf("smalls")], writes=[binc])
        self.cx.join([self.sy, self.sc, self.ve, self.te, self.gp])
        dbg_x = self.dbg.get("x2")
        for tt in range(16):
            self.tile_unit_span(tt, self.d_xsp[0], self.bf("dram_xsp0"),
                                self.d_xsp[1], self.bf("dram_xsp1"), inc_bf, binc,
                                TOWN, ps4, do_ln=True, dbg_x=dbg_x, dbg_ln=None)

    def attention(self):
        nc = self.nc
        with (
            nc.psum_tensor("ap_a", [128, 512], F32) as pa,
            nc.psum_tensor("ap_b", [128, 512], F32) as pb,
            nc.psum_tensor("ap_c", [128, 512], F32) as pc,
            nc.psum_tensor("ap_d", [128, 512], F32) as pd,
        ):
            ps4 = [(pa, self.bf("ap_a")), (pb, self.bf("ap_b")),
                   (pc, self.bf("ap_c")), (pd, self.bf("ap_d"))]
            for half in range(2):
                self.load_lnA(half, ps4)
                self.attn_proj(half, ps4, ps_q=[ps4[0], ps4[1]])
            self.cx.cc(self.gp, lambda: self.gp.collective_compute(
                "AllGather", ALU.bypass,
                replica_groups=[[0, 1], [2, 3], [4, 5], [6, 7]],
                ins=[self.d_kvin.ap()], outs=[self.d_kvout.ap()]),
                reads=[self.bf("dram_kvin")], writes=[self.bf("dram_kvout")])
        self.epoch()
        self.attn_inner()
        self.epoch()
        with (
            nc.psum_tensor("aq_a", [128, 512], F32) as pa,
            nc.psum_tensor("aq_b", [128, 512], F32) as pb,
            nc.psum_tensor("aq_c", [128, 512], F32) as pc,
            nc.psum_tensor("aq_d", [128, 512], F32) as pd,
        ):
            ps4 = [(pa, self.bf("aq_a")), (pb, self.bf("aq_b")),
                   (pc, self.bf("aq_c")), (pd, self.bf("aq_d"))]
            self.attn_wo_combine(ps4)

    def epoch(self):
        self.cx.join([self.sy, self.sc, self.ve, self.te, self.gp])

    def build(self):
        nc = self.nc
        self.setup()
        self.epoch()
        with (
            nc.psum_tensor("p_a", [128, 512], F32) as pa,
            nc.psum_tensor("p_b", [128, 512], F32) as pb,
            nc.psum_tensor("p_c", [128, 512], F32) as pc,
            nc.psum_tensor("p_d", [128, 512], F32) as pd,
        ):
            ps4 = [(pa, self.bf("pre_a")), (pb, self.bf("pre_b")),
                   (pc, self.bf("pre_c")), (pd, self.bf("pre_d"))]
            self.pre_phase(ps4)
        if self.dev.startswith("pre"):
            self.finish()
            return self.nc
        self.epoch()
        self.ffn_call(0, self.i["x"], None, self.d_xsp[0], self.bf("dram_xsp0"))
        if self.dev == "ffn1":
            self.finish()
            return self.nc
        self.epoch()
        self.attention()
        if self.dev == "attn":
            self.finish()
            return self.nc
        if self.dev == "fulld":
            pass
        self.epoch()
        self.ffn_call(1, self.d_xsp[1], self.bf("dram_xsp1"), self.o_out, None,
                      last=True)
        self.finish()
        return self.nc

    def finish(self):
        self.cx.join([self.sy])
        self.stk.close()


def kernel(**inputs):
    shared = _host_layouts(inputs)
    in_maps = [{**shared, **_percore(inputs, c)} for c in range(8)]
    b = B(DEV_UPTO)
    nc = b.build()
    res = run_bass_kernel_spmd(nc, in_maps, core_ids=list(range(8)))
    kernel.last_results = res
    out = np.empty((4, SEQ, D), dtype=np.float32)
    for c in range(8):
        bidx, par = c // 2, c % 2
        chunks = CHUNKS_EVEN if par == 0 else CHUNKS_ODD
        shard = res.results[c]["out"]
        for i, g in enumerate(chunks):
            out[bidx, g * 512:(g + 1) * 512, :] = shard[i * 512:(i + 1) * 512, :]
    return out
